# version 05
# baseline (speedup 1.0000x reference)
import math
import numpy as np
import ml_dtypes
import concourse.bass as bass
import concourse.mybir as mybir
from concourse.bass_utils import run_bass_kernel_spmd

F32 = mybir.dt.float32
BF16 = mybir.dt.bfloat16
AF = mybir.ActivationFunctionType
ALU = mybir.AluOpType
AX = mybir.AxisListType

R = 2
T = 2048
D = 1024
NT = T // 128
EPS = 1e-5
LAMBDA_INIT = 0.8 - 0.6 * math.exp(-0.3)
NEG = -1.0e30
GA = 0.044715
SGA = math.sqrt(GA)
TC = math.sqrt(2.0 / math.pi)
LNL = math.log(1.0 - LAMBDA_INIT)


class Sem:
    __slots__ = ("h", "n")

    def __init__(self, h):
        self.h = h
        self.n = 0

    def tok(self):
        return (self.h, self.n)


class Ctx:
    ENGINES = ("tensor", "vector", "scalar", "gpsimd", "sync")

    def __init__(self, nc):
        self.nc = nc
        self.streams = {k: [] for k in self.ENGINES}
        self._nsem = 0
        self.all_sems = []
        self.esem = {k: self.new_sem() for k in self.ENGINES}

    def new_sem(self):
        self._nsem += 1
        s = Sem(self.nc.alloc_semaphore(name=f"s{self._nsem}"))
        self.all_sems.append(s)
        return s

    serial = False

    def op(self, eng, fn, waits=(), sem="dflt", amt=1):
        if Ctx.serial and fn is not None:
            waits = list(waits) + [s.tok() for s in self.all_sems if s.n > 0]
        if sem == "dflt":
            s, amt = self.esem[eng], 1
        else:
            s = sem
        tok = None
        if s is not None:
            s.n += amt
            tok = (s.h, s.n)
        self.streams[eng].append((tuple(waits), fn, s.h if s else None, amt))
        return tok

    def dma(self, eng, out, in_, waits=(), sem=None, **kw):
        assert sem is not None
        return self.op(eng, lambda e: e.dma_start(out, in_, **kw), waits=waits,
                       sem=sem, amt=16)

    def barrier(self):
        toks = [s.tok() for s in self.all_sems if s.n > 0]
        for e in self.ENGINES:
            self.op(e, None, waits=toks, sem=None)
        return toks

    def finalize(self, block):
        def runner(name):
            stream = self.streams[name]

            def body(eng):
                waited = {}
                for waits, fn, semh, amt in stream:
                    for (h, v) in waits:
                        if waited.get(h, -1) < v:
                            eng.wait_ge(h, v)
                            waited[h] = v
                    if fn is not None:
                        inst = fn(eng)
                        if semh is not None:
                            inst.then_inc(semh, amt)
                            waited[semh] = max(waited.get(semh, -1), 0)
            return body

        block.tensor(runner("tensor"))
        block.vector(runner("vector"))
        block.scalar(runner("scalar"))
        block.gpsimd(runner("gpsimd"))
        block.sync(runner("sync"))


def _host_aux():
    idbf = np.eye(128, dtype=ml_dtypes.bfloat16)
    idf32 = np.eye(128, dtype=np.float32)
    ones_bf = np.ones((128, 128), dtype=ml_dtypes.bfloat16)
    ones_f32 = np.ones((128, 128), dtype=np.float32)
    mask = np.where(np.arange(128)[None, :] <= np.arange(128)[:, None],
                    np.float32(0.0), np.float32(NEG)).astype(np.float32)
    return {"aux_idbf": idbf, "aux_idf32": idf32, "aux_onesbf": ones_bf,
            "aux_onesf32": ones_f32, "aux_mask": mask}


CONSTS = [1e-5, 1.0 / 1024, -1.0 / 1024, -0.5, SGA, TC, 0.5, 0.125, 0.0625,
          1.0 / 128, -1.0 / 128, 1.0 - LAMBDA_INIT, -1.0, 1.0, 2.0,
          1.0 / 512, -1.0 / 512, LNL]


class Aux:

    def __init__(self, nc, ctx, stack):
        self.nc = nc
        self.ctx = ctx
        ap = lambda n, s, d: nc.dram_tensor(n, s, d, kind="ExternalInput").ap()
        self.idbf_h = ap("aux_idbf", [128, 128], BF16)
        self.idf32_h = ap("aux_idf32", [128, 128], F32)
        self.onesbf_h = ap("aux_onesbf", [128, 128], BF16)
        self.onesf32_h = ap("aux_onesf32", [128, 128], F32)
        self.mask_h = ap("aux_mask", [128, 128], F32)
        sb = lambda n, s, d: stack.enter_context(nc.sbuf_tensor(n, s, d))
        self.idbf = sb("idbf", [128, 128], BF16)
        self.idf32 = sb("idf32", [128, 128], F32)
        self.onesbf = sb("onesbf", [128, 128], BF16)
        self.onesf32 = sb("onesf32", [128, 128], F32)
        self.mask = sb("mask", [128, 128], F32)
        self.ld = ctx.new_sem()
        for dst, src in ((self.idbf, self.idbf_h), (self.idf32, self.idf32_h),
                         (self.onesbf, self.onesbf_h), (self.onesf32, self.onesf32_h),
                         (self.mask, self.mask_h)):
            ctx.dma("sync", dst[:], src, sem=self.ld)
        self.ld_tok = self.ld.tok()
        self.cpool = sb("cpool", [128, len(CONSTS)], F32)
        for j, v in enumerate(CONSTS):
            ctx.op("vector", lambda e, j=j, v=v:
                   e.memset(self.cpool[:, j:j + 1], float(v)))
            nc.const_aps.register(F32, float(v), self.cpool[:, j:j + 1])


def ln_to_xnT(ctx, aux, bufs, x_hbm, row, tiles, xnT, tcols, evac_waits=()):
    (xt, stats, sq_scr, xn_bf, ps_tr, xld, prev) = bufs
    xv = xnT[:].rearrange("p (c j) -> p c j", c=8)
    t0 = tiles[0]
    last = None
    for t in tiles:
        s = t % 2
        w = [prev[("xt", s)]] if ("xt", s) in prev else []
        lx = ctx.dma("sync", xt[:, s * 1024:(s + 1) * 1024],
                     x_hbm[row * T + t * 128: row * T + (t + 1) * 128, :],
                     waits=w, sem=xld[s])
        c0, c1, c2, c3, c4, c5 = [stats[:, s * 8 + i:s * 8 + i + 1] for i in range(6)]
        w_st = [prev[("st", s)]] if ("st", s) in prev else []
        tk = ctx.op("vector", lambda e, a=xt[:, s * 1024:(s + 1) * 1024], o=c0:
                    e.reduce_sum(o, a, axis=AX.X), waits=[lx] + w_st)
        tk = ctx.op("vector", lambda e, i=c0, o=c1:
                    e.tensor_scalar_mul(o, i, -1.0 / D), waits=[tk])
        tk = ctx.op("scalar", lambda e, a=xt[:, s * 1024:(s + 1) * 1024], o=sq_scr,
                    b=c1, acc=c2:
                    e.activation(o[:], a, AF.Square, bias=b, accum_out=acc),
                    waits=[tk])
        tk = ctx.op("scalar", lambda e, i=c2, o=c3:
                    e.activation(o, i, AF.Ln, scale=1.0 / D, bias=EPS), waits=[tk])
        tk = ctx.op("scalar", lambda e, i=c3, o=c4:
                    e.activation(o, i, AF.Exp, scale=-0.5), waits=[tk])
        tk = ctx.op("vector", lambda e, a=c1, b=c4, o=c5:
                    e.tensor_mul(o, a, b), waits=[tk])
        tkc = ctx.op("scalar", lambda e, a=xt[:, s * 1024:(s + 1) * 1024],
                     o=xn_bf[:, s * 1024:(s + 1) * 1024], sc=c4, b=c5:
                     e.activation(o, a, AF.Identity, scale=sc, bias=b), waits=[tk])
        prev[("xt", s)] = tkc
        prev[("st", s)] = tkc
        for g in range(2):
            ttr = None
            for cc in range(4):
                c = g * 4 + cc
                w2 = [tkc, aux.ld_tok]
                if ("tr", g) in prev:
                    w2.append(prev[("tr", g)])
                ttr = ctx.op("tensor", lambda e,
                             i=xn_bf[:, s * 1024 + c * 128:s * 1024 + (c + 1) * 128],
                             o=ps_tr[g][:, cc * 128:(cc + 1) * 128]:
                             e.transpose(o, i, aux.idbf[:]),
                             waits=(w2 if cc == 0 else ()),
                             sem=None if cc < 3 else "dflt")
            dst = xv[:, g * 4:(g + 1) * 4, (t - t0) * 128:(t - t0 + 1) * 128]
            tke = ctx.op("vector", lambda e, i=ps_tr[g][:, 0:512],
                         o=dst: e.tensor_copy(o, i),
                         waits=[ttr] + list(evac_waits))
            prev[("tr", g)] = tke
            last = tke
    return last


def sandwich_finish(ctx, aux, bufs, ps_ap, ps_tok, x_hbm_in, x_hbm_out, row, t,
                    prev):
    (xt2, stats2, sq_scr2, lnout, xld2) = bufs
    s = t % 2
    w = [prev[("xt2", s)]] if ("xt2", s) in prev else []
    lx = ctx.dma("sync", xt2[:, s * 1024:(s + 1) * 1024],
                 x_hbm_in[row * T + t * 128: row * T + (t + 1) * 128, :],
                 waits=w, sem=xld2[s])
    c0, c1, c2, c3, c4, c5 = [stats2[:, s * 8 + i:s * 8 + i + 1] for i in range(6)]
    w_st = [prev[("st2", s)]] if ("st2", s) in prev else []
    tk = ctx.op("vector", lambda e, a=ps_ap, o=c0:
                e.reduce_sum(o, a, axis=AX.X), waits=[ps_tok] + w_st)
    tk = ctx.op("vector", lambda e, i=c0, o=c1:
                e.tensor_scalar_mul(o, i, -1.0 / D), waits=[tk])
    tk = ctx.op("scalar", lambda e, a=ps_ap, o=sq_scr2[:],
                b=c1, acc=c2:
                e.activation(o, a, AF.Square, bias=b, accum_out=acc), waits=[tk])
    tk = ctx.op("scalar", lambda e, i=c2, o=c3:
                e.activation(o, i, AF.Ln, scale=1.0 / D, bias=EPS), waits=[tk])
    tk = ctx.op("scalar", lambda e, i=c3, o=c4:
                e.activation(o, i, AF.Exp, scale=-0.5), waits=[tk])
    tk = ctx.op("vector", lambda e, a=c1, b=c4, o=c5:
                e.tensor_mul(o, a, b), waits=[tk])
    wlno = [prev[("lno",)]] if ("lno",) in prev else []
    tk_ps = ctx.op("scalar", lambda e, a=ps_ap, o=lnout[:],
                   sc=c4, b=c5:
                   e.activation(o, a, AF.Identity, scale=sc, bias=b),
                   waits=[tk] + wlno)
    tk2 = ctx.op("vector", lambda e, a=xt2[:, s * 1024:(s + 1) * 1024],
                 b=lnout[:],
                 o=xt2[:, s * 1024:(s + 1) * 1024]:
                 e.tensor_add(o, a, b), waits=[lx, tk_ps])
    prev[("lno",)] = tk2
    st = ctx.dma("sync", x_hbm_out[row * T + t * 128: row * T + (t + 1) * 128, :],
                 xt2[:, s * 1024:(s + 1) * 1024], waits=[tk2], sem=xld2[s])
    prev[("xt2", s)] = st
    prev[("st2", s)] = tk_ps
    return st, tk_ps


def mk_ln_bufs(nc, ctx, stack, tag, ps_tr):
    xt = stack.enter_context(nc.sbuf_tensor(f"xt_{tag}", [128, 2048], F32))
    stats = stack.enter_context(nc.sbuf_tensor(f"st_{tag}", [128, 16], F32))
    sq = stack.enter_context(nc.sbuf_tensor(f"sq_{tag}", [128, 1024], F32))
    xn = stack.enter_context(nc.sbuf_tensor(f"xn_{tag}", [128, 2048], BF16))
    xld = [ctx.new_sem(), ctx.new_sem()]
    return (xt, stats, sq, xn, ps_tr, xld, {})


def mk_sw_bufs(nc, ctx, stack, tag):
    xt2 = stack.enter_context(nc.sbuf_tensor(f"xt2_{tag}", [128, 2048], F32))
    stats2 = stack.enter_context(nc.sbuf_tensor(f"st2_{tag}", [128, 16], F32))
    sq2 = stack.enter_context(nc.sbuf_tensor(f"sq2_{tag}", [128, 1024], F32))
    lnout = stack.enter_context(nc.sbuf_tensor(f"lno_{tag}", [128, 1024], F32))
    xld2 = [ctx.new_sem(), ctx.new_sem()]
    return (xt2, stats2, sq2, lnout, xld2)


def build_l0ffn(nc, ctx, aux, stack, W, x_in, x_out, rows=range(R)):
    u1 = stack.enter_context(nc.sbuf_tensor("u1", [128, 8 * 256], BF16))
    v1 = stack.enter_context(nc.sbuf_tensor("v1", [128, 2 * 4096], BF16))
    u2 = stack.enter_context(nc.sbuf_tensor("u2", [128, 32 * 256], BF16))
    v2 = stack.enter_context(nc.sbuf_tensor("v2", [128, 2 * 1024], BF16))
    b1r = stack.enter_context(nc.sbuf_tensor("b1r", [32, 128], F32))
    b1c = stack.enter_context(nc.sbuf_tensor("b1c", [128, 32], F32))
    b2row = stack.enter_context(nc.sbuf_tensor("b2row", [1, 1024], BF16))
    wld = ctx.new_sem()
    ctx.dma("gpsimd", u1[:].rearrange("p (c m) -> p c m", c=8),
            W["L0_ffn_U1"].rearrange("(c p) m -> p c m", p=128), sem=wld)
    ctx.dma("gpsimd", v1[:].rearrange("p (c m) -> p c m", c=2),
            W["L0_ffn_V1"].rearrange("(c p) m -> p c m", p=128), sem=wld)
    ctx.dma("gpsimd", u2[:].rearrange("p (c m) -> p c m", c=32),
            W["L0_ffn_U2"].rearrange("(c p) m -> p c m", p=128), sem=wld)
    ctx.dma("gpsimd", v2[:].rearrange("p (c m) -> p c m", c=2),
            W["L0_ffn_V2"].rearrange("(c p) m -> p c m", p=128), sem=wld)
    ctx.dma("gpsimd", b1r[:], W["L0_ffn_b1"].rearrange("(c m) -> c m", m=128),
            sem=wld)
    ctx.dma("gpsimd", b2row[:], W["L0_ffn_b2"].rearrange("(c m) -> c m", c=1),
            sem=wld)
    wtok = wld.tok()

    ps_tr = [stack.enter_context(nc.psum_tensor(f"pstr_f0_{i}", [128, 512], BF16))
             for i in range(2)]
    ps = [stack.enter_context(nc.psum_tensor(f"psf0_{i}", [128, 512], F32))
          for i in range(2)]
    ps_o = [stack.enter_context(nc.psum_tensor(f"psof0_{i}", [128, 1024], F32))
            for i in range(2)]
    t1 = ctx.op("tensor", lambda e: e.transpose(ps[0][:, 0:32], b1r[:],
                                                aux.idf32[0:32, 0:32]),
                waits=[wtok, aux.ld_tok])
    t1 = ctx.op("vector", lambda e: e.tensor_copy(b1c[:], ps[0][:, 0:32]),
                waits=[t1])
    ps_cons = {0: t1}
    pso_cons = {}
    cons = {}
    prev = {}

    lnb = mk_ln_bufs(nc, ctx, stack, "f0", ps_tr)
    swb = mk_sw_bufs(nc, ctx, stack, "f0")
    xnT = stack.enter_context(nc.sbuf_tensor("xnT_f0", [128, 8 * 512], BF16))
    pT = stack.enter_context(nc.sbuf_tensor("pT_f0", [128, 2 * 512], BF16))
    hT = stack.enter_context(nc.sbuf_tensor("hT_f0", [128, 32 * 512], BF16))
    qT = stack.enter_context(nc.sbuf_tensor("qT_f0", [128, 2 * 512], BF16))
    zb = stack.enter_context(nc.sbuf_tensor("zb_f0", [128, 4 * 512], F32))
    sb_ = stack.enter_context(nc.sbuf_tensor("sb_f0", [128, 4 * 512], F32))
    ib = stack.enter_context(nc.sbuf_tensor("ib_f0", [128, 4 * 512], F32))
    tb = stack.enter_context(nc.sbuf_tensor("tb_f0", [128, 4 * 512], F32))

    for row in rows:
        for g in range(4):
            tiles = range(g * 4, (g + 1) * 4)
            ew = [cons["xnT"]] if "xnT" in cons else []
            xn_tok = ln_to_xnT(ctx, aux, lnb, x_in, row, tiles, xnT, 512,
                               evac_waits=ew)
            ptoks = []
            for rc in range(2):
                pslot = rc
                wts = [xn_tok, wtok] + ([ps_cons[pslot]] if pslot in ps_cons else [])
                if "pT" in cons:
                    wts.append(cons["pT"])
                mm = None
                for c in range(8):
                    mm = ctx.op("tensor", lambda e, c=c, rc=rc, o=ps[pslot]:
                                e.matmul(o[:], u1[:, c * 256 + rc * 128:
                                                 c * 256 + rc * 128 + 128],
                                         xnT[:, c * 512:(c + 1) * 512],
                                         start=(c == 0), stop=(c == 7)),
                                waits=wts if c == 0 else (),
                                sem=None if c < 7 else "dflt")
                ev = ctx.op("scalar", lambda e, rc=rc, i=ps[pslot]:
                            e.activation(pT[:, rc * 512:(rc + 1) * 512], i[:],
                                         AF.Copy), waits=[mm])
                ps_cons[pslot] = ev
                ptoks.append(ev)
            h_mm_last = None
            htoks = []
            for f in range(32):
                pslot = f % 2
                fs = f % 4
                wts = ptoks + ([ps_cons[pslot]] if pslot in ps_cons else [])
                if f < 2 and "hT" in cons:
                    wts.append(cons["hT"])
                mm = None
                for rc in range(2):
                    mm = ctx.op("tensor", lambda e, f=f, rc=rc, o=ps[pslot]:
                                e.matmul(o[:], v1[:, rc * 4096 + f * 128:
                                                 rc * 4096 + (f + 1) * 128],
                                         pT[:, rc * 512:(rc + 1) * 512],
                                         start=(rc == 0), stop=(rc == 1)),
                                waits=wts if rc == 0 else (),
                                sem=None if rc < 1 else "dflt")
                h_mm_last = mm
                wz = [mm] + ([prev[("z", fs)]] if ("z", fs) in prev else [])
                tz = ctx.op("scalar", lambda e, f=f, fs=fs, i=ps[pslot]:
                            e.activation(zb[:, fs * 512:(fs + 1) * 512], i[:],
                                         AF.Identity, bias=b1c[:, f:f + 1]), waits=wz)
                ps_cons[pslot] = tz
                ts = ctx.op("scalar", lambda e, fs=fs:
                            e.activation(sb_[:, fs * 512:(fs + 1) * 512],
                                         zb[:, fs * 512:(fs + 1) * 512],
                                         AF.Square, scale=SGA), waits=[tz])
                ti = ctx.op("vector", lambda e, fs=fs:
                            e.scalar_tensor_tensor(ib[:, fs * 512:(fs + 1) * 512],
                                                   sb_[:, fs * 512:(fs + 1) * 512],
                                                   1.0,
                                                   zb[:, fs * 512:(fs + 1) * 512],
                                                   ALU.add, ALU.mult), waits=[ts])
                tt = ctx.op("scalar", lambda e, fs=fs:
                            e.activation(tb[:, fs * 512:(fs + 1) * 512],
                                         ib[:, fs * 512:(fs + 1) * 512],
                                         AF.Tanh, scale=TC), waits=[ti])
                tp = ctx.op("vector", lambda e, fs=fs:
                            e.tensor_scalar(ib[:, fs * 512:(fs + 1) * 512],
                                            tb[:, fs * 512:(fs + 1) * 512],
                                            0.5, 0.5, ALU.mult, ALU.add), waits=[tt])
                th = ctx.op("vector", lambda e, f=f, fs=fs:
                            e.tensor_mul(hT[:, f * 512:(f + 1) * 512],
                                         ib[:, fs * 512:(fs + 1) * 512],
                                         zb[:, fs * 512:(fs + 1) * 512]), waits=[tp])
                prev[("z", fs)] = th
                htoks.append(th)
            cons["pT"] = h_mm_last
            qtoks = []
            q_mm_last = None
            for rc in range(2):
                pslot = rc
                wts = htoks + ([ps_cons[pslot]] if pslot in ps_cons else [])
                if "qT" in cons:
                    wts.append(cons["qT"])
                mm = None
                for f in range(32):
                    mm = ctx.op("tensor", lambda e, f=f, rc=rc, o=ps[pslot]:
                                e.matmul(o[:], u2[:, f * 256 + rc * 128:
                                                 f * 256 + rc * 128 + 128],
                                         hT[:, f * 512:(f + 1) * 512],
                                         start=(f == 0), stop=(f == 31)),
                                waits=wts if f == 0 else (),
                                sem=None if f < 31 else "dflt")
                q_mm_last = mm
                ev = ctx.op("vector", lambda e, rc=rc, i=ps[pslot]:
                            e.tensor_copy(qT[:, rc * 512:(rc + 1) * 512], i[:]),
                            waits=[mm])
                ps_cons[pslot] = ev
                qtoks.append(ev)
            cons["hT"] = q_mm_last
            out_mm_last = None
            for tt_ in range(4):
                t = g * 4 + tt_
                oslot = tt_ % 2
                wts = qtoks + [wtok] + ([pso_cons[oslot]] if oslot in pso_cons else [])
                mm = None
                for db in range(2):
                    for rc in range(2):
                        mm = ctx.op("tensor", lambda e, tt_=tt_, db=db, rc=rc,
                                    o=ps_o[oslot]:
                                    e.matmul(o[:, db * 512:(db + 1) * 512],
                                             qT[:, rc * 512 + tt_ * 128:
                                                rc * 512 + tt_ * 128 + 128],
                                             v2[:, rc * 1024 + db * 512:
                                                rc * 1024 + (db + 1) * 512],
                                             start=(rc == 0), stop=False),
                                    waits=wts if (db == 0 and rc == 0) else (),
                                    sem=None)
                    mm = ctx.op("tensor", lambda e, db=db, o=ps_o[oslot]:
                                e.matmul(o[:, db * 512:(db + 1) * 512],
                                         aux.onesbf[0:1, :],
                                         b2row[0:1, db * 512:(db + 1) * 512],
                                         start=False, stop=True),
                                waits=(), sem=None if db < 1 else "dflt")
                out_mm_last = mm
                st, tk_ps = sandwich_finish(ctx, aux, swb, ps_o[oslot][:],
                                            mm, x_in, x_out, row, t, prev)
                pso_cons[oslot] = tk_ps
            cons["qT"] = out_mm_last
            cons["xnT"] = h_mm_last


def _winputs(nc, names):
    shp = dict(WSHAPES)
    return {n: nc.dram_tensor(n, list(shp[n]), F32, kind="ExternalInput").ap()
            for n in names}


WSHAPES = {
    "L0_attn_Wq1": (1024, 1024), "L0_attn_Wq2": (1024, 1024),
    "L0_attn_Wk1": (1024, 1024), "L0_attn_Wk2": (1024, 1024),
    "L0_attn_Wv": (1024, 2048), "L0_attn_Wc": (2048, 1024),
    "L0_attn_lq1": (16, 64), "L0_attn_lk1": (16, 64),
    "L0_attn_lq2": (16, 64), "L0_attn_lk2": (16, 64),
    "L0_ffn_U1": (1024, 256), "L0_ffn_V1": (256, 4096), "L0_ffn_b1": (4096,),
    "L0_ffn_U2": (4096, 256), "L0_ffn_V2": (256, 1024), "L0_ffn_b2": (1024,),
    "L1_attn_Wq": (1024, 256), "L1_attn_Wk": (1024, 256),
    "L1_attn_Wvdown": (1024, 256), "L1_attn_Wvup": (256, 1024),
    "L1_ffn_w1_w": (4096, 1024), "L1_ffn_w1_s": (4096,),
    "L1_ffn_w2_w": (4096, 1024), "L1_ffn_w2_s": (4096,),
    "L1_ffn_h1_w": (4096, 4096), "L1_ffn_h1_s": (4096,),
    "L1_ffn_h2_w": (4096, 4096), "L1_ffn_h2_s": (4096,),
    "L1_ffn_w3_w": (1024, 4096), "L1_ffn_w3_s": (1024,),
}

L0FFN_W = ["L0_ffn_U1", "L0_ffn_V1", "L0_ffn_b1", "L0_ffn_U2", "L0_ffn_V2",
           "L0_ffn_b2"]


def build_prog(stages):
    import contextlib
    nc = bass.Bass("TRN2", target_bir_lowering=False)
    x_in = nc.dram_tensor("x_in", [R * T, D], F32, kind="ExternalInput").ap()
    x_out = nc.dram_tensor("x_out", [R * T, D], F32, kind="ExternalOutput").ap()
    x_mid = [nc.dram_tensor(f"x_mid{i}", [R * T, D], F32, kind="Internal").ap()
             for i in range(max(0, len(stages) - 1))]
    wnames = sorted(set(n for _, ws in stages for n in ws))
    W = _winputs(nc, wnames)
    with contextlib.ExitStack() as stack:
        ctx = Ctx(nc)
        aux = Aux(nc, ctx, stack)
        for i, (builder, _) in enumerate(stages):
            xi = x_in if i == 0 else x_mid[i - 1]
            xo = x_out if i == len(stages) - 1 else x_mid[i]
            with contextlib.ExitStack() as sstack:
                builder(nc, ctx, aux, sstack, W, xi, xo)
                ctx.barrier()
        with nc.Block() as block:
            ctx.finalize(block)
    return nc, wnames


def _run(progs, inputs, x):
    aux_np = _host_aux()
    cur = [np.ascontiguousarray(x[2 * c:2 * c + 2].reshape(R * T, D))
           for c in range(8)]
    for nc, wnames in progs:
        in_maps = []
        for c in range(8):
            m = dict(aux_np)
            m["x_in"] = cur[c]
            for n in wnames:
                m[n] = np.ascontiguousarray(np.asarray(inputs[n], dtype=np.float32))
            in_maps.append(m)
        res = run_bass_kernel_spmd(nc, in_maps, core_ids=list(range(8)))
        cur = [res.results[c]["x_out"] for c in range(8)]
    return np.stack([cur[c].reshape(R, T, D) for c in range(8)])


def kernel(**inputs):
    x = np.ascontiguousarray(np.asarray(inputs["L0_attn_x"], dtype=np.float32))
    B = x.shape[0]
    progs = [build_prog([(build_l0attn, L0ATTN_W), (build_l0ffn, L0FFN_W),
                         (build_l1attn, L1ATTN_W), (build_l1ffn, L1FFN_W)])]
    out = _run(progs, inputs, x)
    return out.reshape(B, T, D)


L1ATTN_W = ["L1_attn_Wq", "L1_attn_Wk", "L1_attn_Wvdown", "L1_attn_Wvup"]


def build_l1attn(nc, ctx, aux, stack, W, x_in, x_out, rows=range(R)):
    wq = stack.enter_context(nc.sbuf_tensor("wq_a1", [128, 8 * 256], BF16))
    wk = stack.enter_context(nc.sbuf_tensor("wk_a1", [128, 8 * 256], BF16))
    wvd = stack.enter_context(nc.sbuf_tensor("wvd_a1", [128, 8 * 256], BF16))
    wvdT = stack.enter_context(nc.sbuf_tensor("wvdT_a1", [128, 2 * 1024], BF16))
    wvu = stack.enter_context(nc.sbuf_tensor("wvu_a1", [128, 2 * 1024], BF16))
    wveff = stack.enter_context(nc.sbuf_tensor("wveff_a1", [128, 8 * 1024], BF16))
    wld = ctx.new_sem()
    ctx.dma("gpsimd", wq[:].rearrange("p (c m) -> p c m", c=8),
            W["L1_attn_Wq"].rearrange("(c p) m -> p c m", p=128), sem=wld)
    ctx.dma("gpsimd", wk[:].rearrange("p (c m) -> p c m", c=8),
            W["L1_attn_Wk"].rearrange("(c p) m -> p c m", p=128), sem=wld)
    ctx.dma("gpsimd", wvd[:].rearrange("p (c m) -> p c m", c=8),
            W["L1_attn_Wvdown"].rearrange("(c p) m -> p c m", p=128), sem=wld)
    ctx.dma("gpsimd", wvu[:].rearrange("p (c m) -> p c m", c=2),
            W["L1_attn_Wvup"].rearrange("(c p) m -> p c m", p=128), sem=wld)
    wtok = wld.tok()

    ps_tr = [stack.enter_context(nc.psum_tensor(f"pstr_a1_{i}", [128, 512], BF16))
             for i in range(2)]
    ps = [stack.enter_context(nc.psum_tensor(f"ps_a1_{i}", [128, 512], F32))
          for i in range(2)]
    ps_o = [stack.enter_context(nc.psum_tensor(f"pso_a1_{i}", [128, 1024], F32))
            for i in range(2)]

    tr_toks = []
    for c in range(8):
        for rr in range(2):
            tt = ctx.op("tensor", lambda e, c=c, rr=rr:
                        e.transpose(ps_tr[rr][:, (c % 4) * 128:(c % 4 + 1) * 128],
                                    wvd[:, c * 256 + rr * 128:c * 256 + (rr + 1) * 128],
                                    aux.idbf[:]),
                        waits=[wtok, aux.ld_tok] + tr_toks[-2:],
                        sem=None if False else "dflt")
        if c % 4 == 3:
            for rr in range(2):
                tk = ctx.op("vector", lambda e, c=c, rr=rr:
                            e.tensor_copy(wvdT[:, rr * 1024 + (c - 3) * 128:
                                               rr * 1024 + (c + 1) * 128],
                                          ps_tr[rr][:, 0:512]),
                            waits=[ctx.esem["tensor"].tok()])
                tr_toks.append(tk)
    wvdT_tok = tr_toks[-1]
    ve_toks = []
    for ec in range(8):
        for dvb in range(2):
            pslot = dvb
            wts = [wvdT_tok, wtok] + ve_toks[-2:]
            mm = None
            for rr in range(2):
                mm = ctx.op("tensor", lambda e, ec=ec, dvb=dvb, rr=rr, o=ps[pslot]:
                            e.matmul(o[:], wvdT[:, rr * 1024 + ec * 128:
                                             rr * 1024 + (ec + 1) * 128],
                                     wvu[:, rr * 1024 + dvb * 512:
                                         rr * 1024 + (dvb + 1) * 512],
                                     start=(rr == 0), stop=(rr == 1)),
                            waits=wts if rr == 0 else (),
                            sem=None if rr < 1 else "dflt")
            ev = ctx.op("scalar", lambda e, ec=ec, dvb=dvb, i=ps[pslot]:
                        e.activation(wveff[:, ec * 1024 + dvb * 512:
                                        ec * 1024 + (dvb + 1) * 512], i[:],
                                     AF.Copy), waits=[mm])
            ve_toks.append(ev)
    ps_cons = {0: ve_toks[-2], 1: ve_toks[-1]}
    pso_cons = {}
    cons = {}
    prev = {}

    lnb = mk_ln_bufs(nc, ctx, stack, "a1", ps_tr)
    swb = mk_sw_bufs(nc, ctx, stack, "a1")
    xnT = stack.enter_context(nc.sbuf_tensor("xnT_a1", [128, 8 * 2048], BF16))
    qT = stack.enter_context(nc.sbuf_tensor("qT_a1", [128, 2 * 2048], BF16))
    kT = stack.enter_context(nc.sbuf_tensor("kT_a1", [128, 2 * 2048], BF16))
    v_sb = stack.enter_context(nc.sbuf_tensor("v_a1", [128, 16 * 1024], BF16))
    e_bf = stack.enter_context(nc.sbuf_tensor("e_a1", [128, 2048], BF16))
    att_bf = stack.enter_context(nc.sbuf_tensor("att_a1", [128, 2048], BF16))
    attT = stack.enter_context(nc.sbuf_tensor("attT_a1", [128, 2048], BF16))
    s_sb = stack.enter_context(nc.sbuf_tensor("s_a1", [128, 512], F32))
    zp = stack.enter_context(nc.sbuf_tensor("zp_a1", [128, 8], F32))

    for row in rows:
        ew = [cons["xnT"]] if "xnT" in cons else []
        if row == 0:
            lnb[6][("tr", 0)] = tr_toks[-2]
            lnb[6][("tr", 1)] = tr_toks[-1]
        xn_tok = ln_to_xnT(ctx, aux, lnb, x_in, row, range(NT), xnT, 2048,
                           evac_waits=ew)
        qk_toks = []
        for w_sb, oT in ((wq, qT), (wk, kT)):
            for c2 in range(2):
                for nb in range(4):
                    pslot = nb % 2
                    wts = [xn_tok, wtok] + ([ps_cons[pslot]] if pslot in ps_cons else [])
                    if "qT" in cons:
                        wts.append(cons["qT"])
                    mm = None
                    for c in range(8):
                        mm = ctx.op("tensor", lambda e, c=c, c2=c2, nb=nb,
                                    w_sb=w_sb, o=ps[pslot]:
                                    e.matmul(o[:], w_sb[:, c * 256 + c2 * 128:
                                                     c * 256 + c2 * 128 + 128],
                                             xnT[:, c * 2048 + nb * 512:
                                                 c * 2048 + (nb + 1) * 512],
                                             start=(c == 0), stop=(c == 7)),
                                    waits=wts if c == 0 else (),
                                    sem=None if c < 7 else "dflt")
                    ev = ctx.op("scalar", lambda e, c2=c2, nb=nb, oT=oT, i=ps[pslot]:
                                e.activation(oT[:, c2 * 2048 + nb * 512:
                                              c2 * 2048 + (nb + 1) * 512], i[:],
                                             AF.Copy), waits=[mm])
                    ps_cons[pslot] = ev
                    qk_toks.append(ev)
        v_toks = []
        for t in range(NT):
            for dvb in range(2):
                pslot = dvb
                wts = [xn_tok, ve_toks[-1]] + ([ps_cons[pslot]] if pslot in ps_cons else [])
                if "v_sb" in cons:
                    wts.append(cons["v_sb"])
                mm = None
                for c in range(8):
                    mm = ctx.op("tensor", lambda e, c=c, t=t, dvb=dvb, o=ps[pslot]:
                                e.matmul(o[:], xnT[:, c * 2048 + t * 128:
                                                 c * 2048 + (t + 1) * 128],
                                         wveff[:, c * 1024 + dvb * 512:
                                             c * 1024 + (dvb + 1) * 512],
                                         start=(c == 0), stop=(c == 7)),
                                waits=wts if c == 0 else (),
                                sem=None if c < 7 else "dflt")
                ev = ctx.op("vector", lambda e, t=t, dvb=dvb, i=ps[pslot]:
                            e.tensor_copy(v_sb[:, t * 1024 + dvb * 512:
                                             t * 1024 + (dvb + 1) * 512], i[:]),
                            waits=[mm])
                ps_cons[pslot] = ev
                v_toks.append(ev)
        cons["xnT"] = v_toks[-1]
        for i in range(NT):
            L = (i + 1) * 128
            nkb = (L + 511) // 512
            ztoks = []
            etoks = []
            for kb in range(nkb):
                klen = min(512, L - kb * 512)
                pslot = kb % 2
                wts = qk_toks[-1:] + ([ps_cons[pslot]] if pslot in ps_cons else [])
                ewait = [prev[("e",)]] if (("e",) in prev and kb == 0) else []
                mm = None
                for c2 in range(2):
                    mm = ctx.op("tensor", lambda e, i=i, kb=kb, c2=c2, klen=klen,
                                o=ps[pslot]:
                                e.matmul(o[:, 0:klen],
                                         qT[:, c2 * 2048 + i * 128:
                                            c2 * 2048 + (i + 1) * 128],
                                         kT[:, c2 * 2048 + kb * 512:
                                            c2 * 2048 + kb * 512 + klen],
                                         start=(c2 == 0), stop=(c2 == 1)),
                                waits=wts if c2 == 0 else (),
                                sem=None if c2 < 1 else "dflt")
                if kb == nkb - 1:
                    t1 = ctx.op("scalar", lambda e, klen=klen, i_=ps[pslot]:
                                e.activation(s_sb[:, 0:klen], i_[:, 0:klen],
                                             AF.Identity, scale=0.0625),
                                waits=[mm, prev.get("s_sb", mm)] + ewait)
                    ps_cons[pslot] = t1
                    t2 = ctx.op("vector", lambda e, klen=klen:
                                e.tensor_add(s_sb[:, klen - 128:klen],
                                             s_sb[:, klen - 128:klen],
                                             aux.mask[:]), waits=[t1, aux.ld_tok])
                    t3 = ctx.op("scalar", lambda e, kb=kb, klen=klen:
                                e.activation(e_bf[:, kb * 512:kb * 512 + klen],
                                             s_sb[:, 0:klen], AF.Exp,
                                             accum_out=zp[:, kb:kb + 1]),
                                waits=[t2])
                    prev["s_sb"] = t3
                    ztoks.append(t3)
                    etoks.append(t3)
                else:
                    t3 = ctx.op("scalar", lambda e, kb=kb, i_=ps[pslot]:
                                e.activation(e_bf[:, kb * 512:(kb + 1) * 512],
                                             i_[:], AF.Exp, scale=0.0625,
                                             accum_out=zp[:, kb:kb + 1]),
                                waits=[mm] + ewait)
                    ps_cons[pslot] = t3
                    ztoks.append(t3)
                    etoks.append(t3)
            tz = ctx.op("vector", lambda e, nkb=nkb:
                        e.reduce_sum(zp[:, 6:7], zp[:, 0:nkb], axis=AX.X),
                        waits=ztoks)
            tz = ctx.op("vector", lambda e: e.reciprocal(zp[:, 7:8], zp[:, 6:7]),
                        waits=[tz])
            ta = ctx.op("vector", lambda e, L=L:
                        e.tensor_scalar(att_bf[:, 0:L], e_bf[:, 0:L],
                                        zp[:, 7:8], None, ALU.mult),
                        waits=[tz] + etoks + ([prev[("att", 0)]] if ("att", 0) in prev else []))
            prev[("e",)] = ta
            ev_toks = []
            for jg in range((i + 1 + 3) // 4):
                ntile = min(4, i + 1 - jg * 4)
                tslot = jg % 2
                wts = [ta] + ([prev[("atr", tslot)]] if ("atr", tslot) in prev else [])
                if ("tr", tslot) in lnb[6]:
                    wts.append(lnb[6][("tr", tslot)])
                ttr = None
                for jj in range(ntile):
                    j = jg * 4 + jj
                    ttr = ctx.op("tensor", lambda e, j=j, jj=jj, tslot=tslot:
                                 e.transpose(ps_tr[tslot][:, jj * 128:(jj + 1) * 128],
                                             att_bf[:, j * 128:(j + 1) * 128],
                                             aux.idbf[:]),
                                 waits=wts if jj == 0 else (),
                                 sem=None if jj < ntile - 1 else "dflt")
                tke = ctx.op("vector", lambda e, jg=jg, ntile=ntile, tslot=tslot:
                             e.tensor_copy(attT[:, jg * 512:jg * 512 + ntile * 128],
                                           ps_tr[tslot][:, 0:ntile * 128]),
                             waits=[ttr])
                lnb[6][("tr", tslot)] = tke
                prev[("atr", tslot)] = tke
                ev_toks.append(tke)
            oslot = i % 2
            wts = ev_toks + v_toks[-1:] + ([pso_cons[oslot]] if oslot in pso_cons else [])
            mm = None
            for dvb in range(2):
                for j in range(i + 1):
                    mm = ctx.op("tensor", lambda e, j=j, dvb=dvb, o=ps_o[oslot]:
                                e.matmul(o[:, dvb * 512:(dvb + 1) * 512],
                                         attT[:, j * 128:(j + 1) * 128],
                                         v_sb[:, j * 1024 + dvb * 512:
                                             j * 1024 + (dvb + 1) * 512],
                                         start=(j == 0), stop=(j == i)),
                                waits=wts if (dvb == 0 and j == 0) else (),
                                sem=None if not (dvb == 1 and j == i) else "dflt")
            prev[("att", 0)] = mm
            st, tk_ps = sandwich_finish(ctx, aux, swb, ps_o[oslot][:], mm,
                                        x_in, x_out, row, i, prev)
            pso_cons[oslot] = tk_ps
        cons["qT"] = mm
        cons["v_sb"] = mm


L1FFN_W = ["L1_ffn_w1_w", "L1_ffn_w1_s", "L1_ffn_w2_w", "L1_ffn_w2_s",
           "L1_ffn_h1_w", "L1_ffn_h1_s", "L1_ffn_h2_w", "L1_ffn_h2_s",
           "L1_ffn_w3_w", "L1_ffn_w3_s"]


def l1ffn_prep(nc, ctx, aux, stack, W, scr, wch, ssb, ps_tr):
    if True:
        wsem = [ctx.new_sem(), ctx.new_sem()]
        stsem = [ctx.new_sem(), ctx.new_sem()]
        prev = {}
        for mat, ns, nc_ in (("L1_ffn_w1_w", 32, 8), ("L1_ffn_w2_w", 32, 8),
                             ("L1_ffn_h1_w", 32, 32), ("L1_ffn_h2_w", 32, 32),
                             ("L1_ffn_w3_w", 8, 32)):
            ni = nc_ * 128
            for s in range(ns):
                sl = (len(prev) + s) % 2 if False else s % 2
                w = [prev[("wch", sl)]] if ("wch", sl) in prev else []
                ld = ctx.dma("gpsimd", wch[sl][:, 0:ni],
                             W[mat][s * 128:(s + 1) * 128, :], waits=w,
                             sem=wsem[sl])
                ev_last = None
                for cg in range((nc_ + 3) // 4):
                    ntile = min(4, nc_ - cg * 4)
                    tslot = cg % 2
                    wts = [ld, aux.ld_tok]
                    if ("ptr", tslot) in prev:
                        wts.append(prev[("ptr", tslot)])
                    ttr = None
                    for cc in range(ntile):
                        c = cg * 4 + cc
                        ttr = ctx.op("tensor", lambda e, c=c, cc=cc, sl=sl, tslot=tslot:
                                     e.transpose(ps_tr[tslot][:, cc * 128:(cc + 1) * 128],
                                                 wch[sl][:, c * 128:(c + 1) * 128],
                                                 aux.idbf[:]),
                                     waits=wts if cc == 0 else (),
                                     sem=None if cc < ntile - 1 else "dflt")
                    wse = [prev[("ssb", sl)]] if (("ssb", sl) in prev and cg == 0) else []
                    ev_last = ctx.op("vector", lambda e, cg=cg, ntile=ntile, sl=sl,
                                     tslot=tslot:
                                     e.tensor_copy(ssb[sl][:, cg * 512:cg * 512 + ntile * 128],
                                                   ps_tr[tslot][:, 0:ntile * 128]),
                                     waits=[ttr] + wse)
                    prev[("ptr", tslot)] = ev_last
                prev[("wch", sl)] = ev_last
                st = ctx.dma("sync", scr[mat][s * 128:(s + 1) * 128, :],
                             ssb[sl][:, 0:ni], waits=[ev_last], sem=stsem[sl])
                prev[("ssb", sl)] = st
        ctx.barrier()


def build_l1ffn(nc, ctx, aux, stack, W, x_in, x_out, rows=range(R), scr=None):
    ps = [stack.enter_context(nc.psum_tensor(f"ps_f1_{i}", [128, 512], F32))
          for i in range(2)]
    ps_tr = [stack.enter_context(nc.psum_tensor(f"pstr_f1_{i}", [128, 512], BF16))
             for i in range(2)]
    pso_bf = [stack.enter_context(nc.psum_tensor(f"psob_f1_{i}", [128, 1024], BF16))
              for i in range(2)]
    a_sb = stack.enter_context(nc.sbuf_tensor("a_f1", [128, 32 * 1024], BF16))
    h0T = stack.enter_context(nc.sbuf_tensor("h0_f1", [128, 32 * 1024], BF16))
    if scr is None:
        scr = {}
        for mat, ns, nc_ in (("L1_ffn_w1_w", 32, 8), ("L1_ffn_w2_w", 32, 8),
                             ("L1_ffn_h1_w", 32, 32), ("L1_ffn_h2_w", 32, 32),
                             ("L1_ffn_w3_w", 8, 32)):
            scr[mat] = nc.dram_tensor("scr_" + mat, [ns * 128, nc_ * 128], BF16,
                                      kind="Internal").ap()
        l1ffn_prep(nc, ctx, aux, stack, W, scr,
                   [a_sb[:, 0:4096], a_sb[:, 4096:8192]],
                   [h0T[:, 0:4096], h0T[:, 4096:8192]], ps_tr)
    scol = stack.enter_context(nc.sbuf_tensor("scol_f1", [128, 4 * 32 + 8], F32))
    sld = ctx.new_sem()
    ps_cons = {}
    stoks = []
    srow = stack.enter_context(nc.sbuf_tensor("srow_f1", [32, 128], F32))
    for j, (sn, ncol) in enumerate((("L1_ffn_w1_s", 32), ("L1_ffn_w2_s", 32),
                                    ("L1_ffn_h1_s", 32), ("L1_ffn_h2_s", 32),
                                    ("L1_ffn_w3_s", 8))):
        w = [stoks[-1]] if stoks else []
        ld = ctx.dma("sync", srow[0:ncol, :],
                     W[sn].rearrange("(c m) -> c m", m=128), waits=w, sem=sld)
        tt = ctx.op("tensor", lambda e, ncol=ncol:
                    e.transpose(ps[0][:, 0:ncol], srow[0:ncol, :],
                                aux.idf32[0:ncol, 0:ncol]),
                    waits=[ld, aux.ld_tok] + ([stoks[-1]] if stoks else []))
        tk = ctx.op("vector", lambda e, j=j, ncol=ncol:
                    e.tensor_copy(scol[:, j * 32:j * 32 + ncol], ps[0][:, 0:ncol]),
                    waits=[tt])
        stoks.append(tk)
    scol_tok = stoks[-1]
    ps_cons[0] = scol_tok
    w1s = scol[:, 0:32]; w2s = scol[:, 32:64]
    h1s = scol[:, 64:96]; h2s = scol[:, 96:128]; w3s = scol[:, 128:136]

    lnb = mk_ln_bufs(nc, ctx, stack, "f1", ps_tr)
    swb = mk_sw_bufs(nc, ctx, stack, "f1")
    xnT = stack.enter_context(nc.sbuf_tensor("xnT_f1", [128, 8 * 1024], BF16))
    outT = xnT
    wsb_s = [stack.enter_context(nc.sbuf_tensor(f"wss_f1_{i}", [128, 1024], BF16))
             for i in range(2)]
    wsb_b = [stack.enter_context(nc.sbuf_tensor(f"wsb_f1_{i}", [128, 4096], BF16))
             for i in range(2)]
    wsem_s = [ctx.new_sem(), ctx.new_sem()]
    wsem_b = [ctx.new_sem(), ctx.new_sem()]
    cons = {}
    prev = {}

    def tlin_pass(src_mat, nc_, rhs_buf, evac_fn, war_tok, xn_tok, ns=32,
                  wsb=None, wsem=None):
        toks = []
        mm = None
        for s in range(ns):
            sl = s % 2
            w = [prev[(src_mat, "w", sl)]] if (src_mat, "w", sl) in prev else []
            if ("wuse", wsb[sl].name) in prev:
                w.append(prev[("wuse", wsb[sl].name)])
            ld = ctx.dma("sync", wsb[sl][:, 0:nc_ * 128],
                         scr[src_mat][s * 128:(s + 1) * 128, :], waits=w,
                         sem=wsem[sl])
            for ng in range(2):
                pslot = ng
                wts = [ld, xn_tok] + ([ps_cons[pslot]] if pslot in ps_cons else [])
                if war_tok is not None and s == 0 and ng == 0:
                    wts.append(war_tok)
                for c in range(nc_):
                    mm = ctx.op("tensor", lambda e, s=s, sl=sl, c=c, ng=ng,
                                o=ps[pslot], rhs_buf=rhs_buf:
                                e.matmul(o[:], wsb[sl][:, c * 128:(c + 1) * 128],
                                         rhs_buf[:, c * 1024 + ng * 512:
                                                 c * 1024 + ng * 512 + 512],
                                         start=(c == 0), stop=(c == nc_ - 1)),
                                waits=wts if c == 0 else (),
                                sem=None if c < nc_ - 1 else "dflt")
                ev = evac_fn(s, ng, pslot, mm)
                ps_cons[pslot] = ev
                toks.append(ev)
            prev[("wuse", wsb[sl].name)] = mm
        return toks, mm

    for row in rows:
        for hh in range(2):
            tiles = range(hh * 8, (hh + 1) * 8)
            ew = [cons["xnT"]] if "xnT" in cons else []
            xn_tok = ln_to_xnT(ctx, aux, lnb, x_in, row, tiles, xnT, 1024,
                               evac_waits=ew)
            def ev_t1(s, ng, pslot, mm):
                w = [mm, scol_tok]
                if ("a", s, ng) in prev:
                    w.append(prev[("a", s, ng)])
                if cons.get("a_war") is not None and s == 0 and ng == 0:
                    w.append(cons["a_war"])
                return ctx.op("scalar", lambda e, s=s, ng=ng, i=ps[pslot]:
                              e.activation(a_sb[:, s * 1024 + ng * 512:
                                              s * 1024 + ng * 512 + 512],
                                           i[:], AF.Silu, scale=w1s[:, s:s + 1]),
                              waits=w)
            t1toks, mm1 = tlin_pass("L1_ffn_w1_w", 8, xnT, ev_t1,
                                    cons.get("xn_war"), xn_tok, wsb=wsb_s,
                                    wsem=wsem_s)
            def ev_t2(s, ng, pslot, mm):
                w = [mm, scol_tok, t1toks[2 * s + ng]]
                if ("h0", s, ng) in prev:
                    w.append(prev[("h0", s, ng)])
                tk = ctx.op("vector", lambda e, s=s, ng=ng, i=ps[pslot]:
                            e.scalar_tensor_tensor(
                                h0T[:, s * 1024 + ng * 512:s * 1024 + ng * 512 + 512],
                                i[:], w2s[:, s:s + 1],
                                a_sb[:, s * 1024 + ng * 512:s * 1024 + ng * 512 + 512],
                                ALU.mult, ALU.mult), waits=w)
                prev[("h0", s, ng)] = tk
                return tk
            t2toks, mm2 = tlin_pass("L1_ffn_w2_w", 8, xnT, ev_t2, None, xn_tok,
                                    wsb=wsb_s, wsem=wsem_s)
            cons["xnT_dead"] = mm2
            def ev_h1(s, ng, pslot, mm):
                w = [mm, scol_tok, t2toks[-1]]
                return ctx.op("scalar", lambda e, s=s, ng=ng, i=ps[pslot]:
                              e.activation(a_sb[:, s * 1024 + ng * 512:
                                              s * 1024 + ng * 512 + 512],
                                           i[:], AF.Gelu, scale=h1s[:, s:s + 1]),
                              waits=w)
            h1toks, mm3 = tlin_pass("L1_ffn_h1_w", 32, h0T, ev_h1, t2toks[-1],
                                    t2toks[-1], wsb=wsb_b, wsem=wsem_b)
            def ev_h2(s, ng, pslot, mm):
                w = [mm, scol_tok, h1toks[-1]]
                return ctx.op("scalar", lambda e, s=s, ng=ng, i=ps[pslot]:
                              e.activation(h0T[:, s * 1024 + ng * 512:
                                              s * 1024 + ng * 512 + 512],
                                           i[:], AF.Gelu, scale=h2s[:, s:s + 1]),
                              waits=w)
            h2toks, mm4 = tlin_pass("L1_ffn_h2_w", 32, a_sb, ev_h2, mm3,
                                    h1toks[-1], wsb=wsb_b, wsem=wsem_b)
            cons["a_war"] = mm4
            def ev_w3(s, ng, pslot, mm):
                w = [mm, scol_tok, h2toks[-1], cons["xnT_dead"]]
                if "oT_war" in cons:
                    w.append(cons["oT_war"])
                if ("oT", s, ng) in prev:
                    w.append(prev[("oT", s, ng)])
                tk = ctx.op("scalar", lambda e, s=s, ng=ng, i=ps[pslot]:
                            e.activation(outT[:, s * 1024 + ng * 512:
                                            s * 1024 + ng * 512 + 512],
                                         i[:], AF.Identity, scale=w3s[:, s:s + 1]),
                            waits=w)
                prev[("oT", s, ng)] = tk
                return tk
            w3toks, mm5 = tlin_pass("L1_ffn_w3_w", 32, h0T, ev_w3, mm4,
                                    h2toks[-1], ns=8, wsb=wsb_b, wsem=wsem_b)
            cons["xn_war"] = mm5
            for tt_ in range(8):
                t = hh * 8 + tt_
                oslot = tt_ % 2
                wts = [w3toks[-1]]
                if ("pso", oslot) in prev:
                    wts.append(prev[("pso", oslot)])
                ttr = None
                for oc in range(8):
                    ttr = ctx.op("tensor", lambda e, oc=oc, tt_=tt_, oslot=oslot:
                                 e.transpose(pso_bf[oslot][:, oc * 128:(oc + 1) * 128],
                                             outT[:, oc * 1024 + tt_ * 128:
                                                 oc * 1024 + (tt_ + 1) * 128],
                                             aux.idbf[:]),
                                 waits=wts if oc == 0 else (),
                                 sem=None if oc < 7 else "dflt")
                st, tk_ps = sandwich_finish(ctx, aux, swb, pso_bf[oslot][:], ttr,
                                            x_in, x_out, row, t, prev)
                prev[("pso", oslot)] = tk_ps
                cons["oT_war"] = ttr
                cons["xnT"] = ttr


L0ATTN_W = ["L0_attn_Wq1", "L0_attn_Wq2", "L0_attn_Wk1", "L0_attn_Wk2",
            "L0_attn_Wv", "L0_attn_Wc", "L0_attn_lq1", "L0_attn_lk1",
            "L0_attn_lq2", "L0_attn_lk2"]

def build_l0attn(nc, ctx, aux, stack, W, x_in, x_out, rows=range(R)):
    import contextlib
    ps_big = [stack.enter_context(nc.psum_tensor(f"psb_a0_{i}", [128, 1024], F32))
              for i in range(2)]
    ps_tr = [stack.enter_context(nc.psum_tensor(f"pstr_a0_{i}", [128, 512], BF16))
             for i in range(2)]
    ps_y = [stack.enter_context(nc.psum_tensor(f"psy_a0_{i}", [128, 128], F32))
            for i in range(2)]
    lt = stack.enter_context(nc.sbuf_tensor("lt_a0", [64, 4 * 16], F32))
    lm = stack.enter_context(nc.sbuf_tensor("lm_a0", [64, 2 * 16], F32))
    lam_bc = stack.enter_context(nc.sbuf_tensor("lam_a0", [128, 32], F32))
    lld = ctx.new_sem()
    for j, n in enumerate(("L0_attn_lq1", "L0_attn_lk1", "L0_attn_lq2",
                           "L0_attn_lk2")):
        ctx.dma("sync", lt[:, j * 16:(j + 1) * 16],
                W[n].rearrange("h d -> d h"), sem=lld,
                allow_slow_non_contiguous=True)
    ltok = lld.tok()
    tm1 = ctx.op("vector", lambda e: e.tensor_mul(lm[:, 0:16], lt[:, 0:16],
                                                  lt[:, 16:32]), waits=[ltok])
    tm2 = ctx.op("vector", lambda e: e.tensor_mul(lm[:, 16:32], lt[:, 32:48],
                                                  lt[:, 48:64]), waits=[tm1])
    tl1 = ctx.op("tensor", lambda e: e.matmul(ps_big[0][:, 0:16],
                                              aux.onesf32[0:64, :],
                                              lm[:, 0:16], start=True, stop=True),
                 waits=[tm1, aux.ld_tok])
    tl2 = ctx.op("tensor", lambda e: e.matmul(ps_big[0][:, 16:32],
                                              aux.onesf32[0:64, :],
                                              lm[:, 16:32], start=True, stop=True),
                 waits=[tm2])
    te1 = ctx.op("scalar", lambda e: e.activation(lam_bc[:, 16:32],
                                                  ps_big[0][:, 0:16],
                                                  AF.Exp), waits=[tl1, tl2])
    te2 = ctx.op("scalar", lambda e: e.activation(lam_bc[:, 0:16],
                                                  ps_big[0][:, 16:32],
                                                  AF.Exp), waits=[te1])
    tv1 = ctx.op("vector", lambda e:
                 e.scalar_tensor_tensor(lam_bc[:, 0:16], lam_bc[:, 0:16], -1.0,
                                        lam_bc[:, 16:32], ALU.mult, ALU.add),
                 waits=[te2])
    tv2 = ctx.op("vector", lambda e:
                 e.tensor_scalar_add(lam_bc[:, 0:16], lam_bc[:, 0:16],
                                     LAMBDA_INIT), waits=[tv1])
    lam_tok = tv2
    ps_cons = {("b", 0): te2}

    xnT = stack.enter_context(nc.sbuf_tensor("xnT_a0", [128, 8 * 2048], BF16))
    ylnT = stack.enter_context(nc.sbuf_tensor("ylnT_a0", [128, 16 * 2048], BF16))
    cons = {}
    prev = {}
    wsem = [ctx.new_sem(), ctx.new_sem()]
    wcld = ctx.new_sem()

    lnb = mk_ln_bufs(nc, ctx, stack, "a0", ps_tr)
    swb = mk_sw_bufs(nc, ctx, stack, "a0")
    trd = lnb[6]
    qkT = [stack.enter_context(nc.sbuf_tensor(f"qkT_a0_{i}", [128, 2048], BF16))
           for i in range(4)]
    v_p = [stack.enter_context(nc.sbuf_tensor("vp_a0_0", [128, 16 * 256], BF16))]
    wqk = [stack.enter_context(nc.sbuf_tensor("wqk_a0_0",
                                              [128, 4 * 1024 + 2048], BF16))]
    e1b = [stack.enter_context(nc.sbuf_tensor(f"e1_a0_{i}", [128, 2048], BF16))
           for i in range(2)]
    e2b = [stack.enter_context(nc.sbuf_tensor(f"e2_a0_{i}", [128, 2048], BF16))
           for i in range(2)]
    attb = stack.enter_context(nc.sbuf_tensor("att_a0", [128, 2048], BF16))
    attc = [stack.enter_context(nc.sbuf_tensor(f"attc_a0_{i}", [128, 2048], BF16))
            for i in range(2)]
    attT = [stack.enter_context(nc.sbuf_tensor(f"attT_a0_{i}", [128, 2048], BF16))
            for i in range(2)]
    ssb1 = [stack.enter_context(nc.sbuf_tensor("ss1_a0", [128, 512], F32))] * 2
    ssb2 = [stack.enter_context(nc.sbuf_tensor("ss2_a0", [128, 512], F32))] * 2
    gn = stack.enter_context(nc.sbuf_tensor("gn_a0", [128, 128], F32))
    ylnbf = [stack.enter_context(nc.sbuf_tensor(f"ylnbf_a0_{i}", [128, 128], BF16))
             for i in range(2)]
    zp = stack.enter_context(nc.sbuf_tensor("zp_a0", [128, 32], F32))
    wc_sb = xnT

    for row in rows:
        ew = [cons["xnT"]] if "xnT" in cons else []
        xn_tok = ln_to_xnT(ctx, aux, lnb, x_in, row, range(NT), xnT, 2048,
                           evac_waits=ew)
        pair_mm_last = None
        for p in range(8):
            sl = 0
            wv = [cons[("w", sl)]] if ("w", sl) in cons else []
            wts_ld = []
            for j, n in enumerate(("L0_attn_Wq1", "L0_attn_Wq2", "L0_attn_Wk1",
                                   "L0_attn_Wk2")):
                wts_ld.append(ctx.dma(
                    "gpsimd",
                    wqk[sl][:, j * 1024:(j + 1) * 1024].rearrange(
                        "p (c m) -> p c m", c=8),
                    W[n][:, p * 128:(p + 1) * 128].rearrange(
                        "(c p) m -> p c m", p=128), waits=wv, sem=wsem[sl]))
            wts_ld.append(ctx.dma(
                "gpsimd",
                wqk[sl][:, 4096:4096 + 2048].rearrange("p (c m) -> p c m", c=8),
                W["L0_attn_Wv"][:, p * 256:(p + 1) * 256].rearrange(
                    "(c p) m -> p c m", p=128), waits=wv, sem=wsem[sl]))
            wtok = wts_ld[-1]
            pj_toks = []
            for j in range(4):
                for nb in range(4):
                    pslot = nb % 2
                    wts = [xn_tok, wtok] + ([ps_cons[("b", pslot)]]
                                            if ("b", pslot) in ps_cons else [])
                    if ("qkT", j) in cons:
                        wts.append(cons[("qkT", j)])
                    mm = None
                    for c in range(8):
                        mm = ctx.op("tensor", lambda e, j=j, nb=nb, c=c, sl=sl,
                                    o=ps_big[pslot]:
                                    e.matmul(o[:, 0:512],
                                             wqk[sl][:, j * 1024 + c * 128:
                                                     j * 1024 + (c + 1) * 128],
                                             xnT[:, c * 2048 + nb * 512:
                                                 c * 2048 + (nb + 1) * 512],
                                             start=(c == 0), stop=(c == 7)),
                                    waits=wts if c == 0 else (),
                                    sem=None if c < 7 else "dflt")
                    ev = ctx.op("scalar", lambda e, j=j, nb=nb, i=ps_big[pslot]:
                                e.activation(qkT[j][:, nb * 512:(nb + 1) * 512],
                                             i[:, 0:512], AF.Copy), waits=[mm])
                    ps_cons[("b", pslot)] = ev
                    pj_toks.append(ev)
            v_toks = []
            for t in range(NT):
                pslot = t % 2
                wts = [xn_tok, wtok] + ([ps_cons[("b", pslot)]]
                                        if ("b", pslot) in ps_cons else [])
                if ("vp", 0) in cons:
                    wts.append(cons[("vp", 0)])
                mm = None
                for c in range(8):
                    mm = ctx.op("tensor", lambda e, t=t, c=c, sl=sl,
                                o=ps_big[pslot]:
                                e.matmul(o[:, 512:768],
                                         xnT[:, c * 2048 + t * 128:
                                             c * 2048 + (t + 1) * 128],
                                         wqk[sl][:, 4096 + c * 256:
                                                 4096 + (c + 1) * 256],
                                         start=(c == 0), stop=(c == 7)),
                                waits=wts if c == 0 else (),
                                sem=None if c < 7 else "dflt")
                ev = ctx.op("vector", lambda e, t=t, sl=sl, i=ps_big[pslot]:
                            e.tensor_copy(v_p[0][:, t * 256:(t + 1) * 256],
                                          i[:, 512:768]), waits=[mm])
                ps_cons[("b", pslot)] = ev
                v_toks.append(ev)
            for h01 in range(2):
                h = 2 * p + h01
                hb = 64 * h01
                for i in range(NT):
                    q2 = i % 2
                    zb_ = q2 * 16
                    s2d = i % 2
                    L = (i + 1) * 128
                    nkb = (L + 511) // 512
                    etoks = []
                    for kb in range(nkb):
                        klen = min(512, L - kb * 512)
                        pslot = kb % 2
                        wts = [pj_toks[-1]] + ([ps_cons[("b", pslot)]]
                                               if ("b", pslot) in ps_cons else [])
                        ewait = ([prev[("eL0", q2)]]
                                 if (("eL0", q2) in prev and kb == 0) else [])
                        mm1 = ctx.op("tensor", lambda e, i=i, kb=kb, hb=hb,
                                     klen=klen, o=ps_big[pslot]:
                                     e.matmul(o[:, 0:klen],
                                              qkT[0][hb:hb + 64, i * 128:(i + 1) * 128],
                                              qkT[2][hb:hb + 64,
                                                     kb * 512:kb * 512 + klen],
                                              start=True, stop=True),
                                     waits=wts, sem=None)
                        mm2 = ctx.op("tensor", lambda e, i=i, kb=kb, hb=hb,
                                     klen=klen, o=ps_big[pslot]:
                                     e.matmul(o[:, 512:512 + klen],
                                              qkT[1][hb:hb + 64, i * 128:(i + 1) * 128],
                                              qkT[3][hb:hb + 64,
                                                     kb * 512:kb * 512 + klen],
                                              start=True, stop=True),
                                     waits=(), sem="dflt")
                        if kb == nkb - 1:
                            tA = ctx.op("scalar", lambda e, klen=klen, s2d=s2d,
                                        i_=ps_big[pslot]:
                                        e.activation(ssb1[s2d][:, 0:klen],
                                                     i_[:, 0:klen],
                                                     AF.Identity, scale=0.125),
                                        waits=[mm2, prev.get(("ssb", s2d), mm2)]
                                        + ewait)
                            tB = ctx.op("scalar", lambda e, klen=klen, s2d=s2d,
                                        i_=ps_big[pslot]:
                                        e.activation(ssb2[s2d][:, 0:klen],
                                                     i_[:, 512:512 + klen],
                                                     AF.Identity, scale=0.125),
                                        waits=[tA])
                            ps_cons[("b", pslot)] = tB
                            tC = ctx.op("vector", lambda e, klen=klen, s2d=s2d:
                                        e.tensor_add(ssb1[s2d][:, klen - 128:klen],
                                                     ssb1[s2d][:, klen - 128:klen],
                                                     aux.mask[:]),
                                        waits=[tA, aux.ld_tok])
                            tD = ctx.op("vector", lambda e, klen=klen, s2d=s2d:
                                        e.tensor_add(ssb2[s2d][:, klen - 128:klen],
                                                     ssb2[s2d][:, klen - 128:klen],
                                                     aux.mask[:]), waits=[tB])
                            tE = ctx.op("scalar", lambda e, kb=kb, klen=klen, q2=q2,
                                        s2d=s2d, zb_=zb_:
                                        e.activation(e1b[q2][:, kb * 512:
                                                             kb * 512 + klen],
                                                     ssb1[s2d][:, 0:klen], AF.Exp,
                                                     accum_out=zp[:, zb_ + kb:
                                                                  zb_ + kb + 1]),
                                        waits=[tC, tD])
                            tF = ctx.op("scalar", lambda e, kb=kb, klen=klen, q2=q2,
                                        s2d=s2d, zb_=zb_:
                                        e.activation(e2b[q2][:, kb * 512:
                                                             kb * 512 + klen],
                                                     ssb2[s2d][:, 0:klen], AF.Exp,
                                                     accum_out=zp[:, zb_ + 4 + kb:
                                                                  zb_ + 5 + kb]),
                                        waits=[tD])
                            prev[("ssb", s2d)] = tF
                            etoks += [tE, tF]
                        else:
                            tE = ctx.op("scalar", lambda e, kb=kb, q2=q2, zb_=zb_,
                                        i_=ps_big[pslot]:
                                        e.activation(e1b[q2][:, kb * 512:
                                                             (kb + 1) * 512],
                                                     i_[:, 0:512], AF.Exp,
                                                     scale=0.125,
                                                     accum_out=zp[:, zb_ + kb:
                                                                  zb_ + kb + 1]),
                                        waits=[mm2] + ewait)
                            tF = ctx.op("scalar", lambda e, kb=kb, q2=q2, zb_=zb_,
                                        i_=ps_big[pslot]:
                                        e.activation(e2b[q2][:, kb * 512:
                                                             (kb + 1) * 512],
                                                     i_[:, 512:1024], AF.Exp,
                                                     scale=0.125,
                                                     accum_out=zp[:, zb_ + 4 + kb:
                                                                  zb_ + 5 + kb]),
                                        waits=[mm2])
                            ps_cons[("b", pslot)] = tF
                            etoks += [tE, tF]
                    tz = ctx.op("vector", lambda e, nkb=nkb, zb_=zb_:
                                e.reduce_sum(zp[:, zb_ + 8:zb_ + 9],
                                             zp[:, zb_:zb_ + nkb], axis=AX.X),
                                waits=etoks)
                    tz2 = ctx.op("vector", lambda e, nkb=nkb, zb_=zb_:
                                 e.reduce_sum(zp[:, zb_ + 9:zb_ + 10],
                                              zp[:, zb_ + 4:zb_ + 4 + nkb],
                                              axis=AX.X), waits=etoks)
                    tr1 = ctx.op("vector", lambda e, zb_=zb_:
                                 e.reciprocal(zp[:, zb_ + 10:zb_ + 11],
                                              zp[:, zb_ + 8:zb_ + 9]), waits=[tz2])
                    tr2 = ctx.op("vector", lambda e, zb_=zb_:
                                 e.reciprocal(zp[:, zb_ + 11:zb_ + 12],
                                              zp[:, zb_ + 9:zb_ + 10]), waits=[tz2])
                    tns = ctx.op("vector", lambda e, h=h, zb_=zb_:
                                 e.tensor_scalar(zp[:, zb_ + 12:zb_ + 13],
                                                 zp[:, zb_ + 11:zb_ + 12],
                                                 lam_bc[:, h:h + 1], -1.0,
                                                 ALU.mult, ALU.mult),
                                 waits=[tr2, lam_tok])
                    ta = ctx.op("vector", lambda e, L=L, q2=q2, zb_=zb_:
                                e.tensor_scalar(attb[:, 0:L], e1b[q2][:, 0:L],
                                                zp[:, zb_ + 10:zb_ + 11], None,
                                                ALU.mult),
                                waits=[tns])
                    tc = ctx.op("vector", lambda e, L=L, q2=q2, zb_=zb_:
                                e.scalar_tensor_tensor(attc[q2][:, 0:L],
                                                       e2b[q2][:, 0:L],
                                                       zp[:, zb_ + 12:zb_ + 13],
                                                       attb[:, 0:L],
                                                       ALU.mult, ALU.add),
                                waits=[ta] + ([prev[("attcL0", q2)]]
                                              if ("attcL0", q2) in prev else []))
                    prev[("eL0", q2)] = tc
                    ev_toks = []
                    for jg in range((i + 4) // 4):
                        ntile = min(4, i + 1 - jg * 4)
                        tslot = jg % 2
                        wts = [tc] + ([prev[("atrL0", tslot)]]
                                      if ("atrL0", tslot) in prev else [])
                        if ("tr", tslot) in trd:
                            wts.append(trd[("tr", tslot)])
                        ttr = None
                        for jj in range(ntile):
                            j = jg * 4 + jj
                            ttr = ctx.op("tensor", lambda e, j=j, jj=jj, q2=q2,
                                         tslot=tslot:
                                         e.transpose(ps_tr[tslot][:, jj * 128:
                                                                  (jj + 1) * 128],
                                                     attc[q2][:, j * 128:
                                                              (j + 1) * 128],
                                                     aux.idbf[:]),
                                         waits=wts if jj == 0 else (),
                                         sem=None if jj < ntile - 1 else "dflt")
                        wat = ([prev[("avmm", s2d)]]
                               if (("avmm", s2d) in prev and jg == 0) else [])
                        tke = ctx.op("vector", lambda e, jg=jg, ntile=ntile, s2d=s2d,
                                     tslot=tslot:
                                     e.tensor_copy(attT[s2d][:, jg * 512:
                                                            jg * 512 + ntile * 128],
                                                   ps_tr[tslot][:, 0:ntile * 128]),
                                     waits=[ttr] + wat)
                        trd[("tr", tslot)] = tke
                        prev[("atrL0", tslot)] = tke
                        ev_toks.append(tke)
                    prev[("attcL0", q2)] = ttr
                    yslot = s2d
                    wts = ev_toks + [v_toks[-1]] + ([ps_cons[("y", yslot)]]
                                                    if ("y", yslot) in ps_cons else [])
                    mm = None
                    for j in range(i + 1):
                        mm = ctx.op("tensor", lambda e, j=j, h01=h01, sl=sl, s2d=s2d,
                                    yslot=yslot:
                                    e.matmul(ps_y[yslot][:],
                                             attT[s2d][:, j * 128:(j + 1) * 128],
                                             v_p[0][:, j * 256 + h01 * 128:
                                                     j * 256 + h01 * 128 + 128],
                                             start=(j == 0), stop=(j == i)),
                                    waits=wts if j == 0 else (),
                                    sem=None if j < i else "dflt")
                    prev[("avmm", s2d)] = mm
                    wgn = [prev[("gn", yslot)]] if ("gn", yslot) in prev else []
                    g0 = ctx.op("vector", lambda e, yslot=yslot, zb_=zb_:
                                e.reduce_sum(zp[:, zb_ + 13:zb_ + 14],
                                             ps_y[yslot][:], axis=AX.X),
                                waits=[mm] + wgn)
                    g1 = ctx.op("vector", lambda e, zb_=zb_:
                                e.tensor_scalar_mul(zp[:, zb_ + 14:zb_ + 15],
                                                    zp[:, zb_ + 13:zb_ + 14],
                                                    -1.0 / 128), waits=[g0])
                    g2 = ctx.op("scalar", lambda e, yslot=yslot, zb_=zb_:
                                e.activation(gn[:], ps_y[yslot][:], AF.Square,
                                             bias=zp[:, zb_ + 14:zb_ + 15],
                                             accum_out=zp[:, zb_ + 13:zb_ + 14]),
                                waits=[g1])
                    g3 = ctx.op("scalar", lambda e, zb_=zb_:
                                e.activation(zp[:, zb_ + 15:zb_ + 16],
                                             zp[:, zb_ + 13:zb_ + 14], AF.Ln,
                                             scale=1.0 / 128, bias=EPS), waits=[g2])
                    g4 = ctx.op("scalar", lambda e, zb_=zb_:
                                e.activation(zp[:, zb_ + 13:zb_ + 14],
                                             zp[:, zb_ + 15:zb_ + 16], AF.Exp,
                                             scale=-0.5, bias=LNL), waits=[g3])
                    g6 = ctx.op("vector", lambda e, zb_=zb_:
                                e.tensor_mul(zp[:, zb_ + 14:zb_ + 15],
                                             zp[:, zb_ + 14:zb_ + 15],
                                             zp[:, zb_ + 13:zb_ + 14]), waits=[g4])
                    wyb = [prev[("ylnbf", yslot)]] if ("ylnbf", yslot) in prev else []
                    g7 = ctx.op("scalar", lambda e, yslot=yslot, zb_=zb_:
                                e.activation(ylnbf[yslot][:], ps_y[yslot][:],
                                             AF.Identity,
                                             scale=zp[:, zb_ + 13:zb_ + 14],
                                             bias=zp[:, zb_ + 14:zb_ + 15]),
                                waits=[g6] + wyb)
                    prev[("gn", yslot)] = g7
                    ps_cons[("y", yslot)] = g7
                    tslot2 = s2d
                    wts = [g7] + ([prev[("ytr", tslot2)]]
                                  if ("ytr", tslot2) in prev else [])
                    if ("tr", tslot2) in trd:
                        wts.append(trd[("tr", tslot2)])
                    ty = ctx.op("tensor", lambda e, yslot=yslot, tslot2=tslot2:
                                e.transpose(ps_tr[tslot2][:, 0:128], ylnbf[yslot][:],
                                            aux.idbf[:]), waits=wts)
                    tye = ctx.op("vector", lambda e, h=h, i=i, tslot2=tslot2:
                                 e.tensor_copy(ylnT[:, h * 2048 + i * 128:
                                                   h * 2048 + (i + 1) * 128],
                                               ps_tr[tslot2][:, 0:128]), waits=[ty])
                    trd[("tr", tslot2)] = tye
                    prev[("ytr", tslot2)] = tye
                    prev[("yln",)] = tye
                    pair_mm_last = mm
            cons[("w", sl)] = pair_mm_last
            cons[("vp", 0)] = pair_mm_last
            for j in range(4):
                cons[("qkT", j)] = pair_mm_last
        wc_tok = ctx.dma("gpsimd", wc_sb[:].rearrange("p (c m) -> p c m", c=16),
                         W["L0_attn_Wc"].rearrange("(c p) m -> p c m", p=128),
                         waits=[pair_mm_last, prev[("yln",)]], sem=wcld)
        out_mm_last = None
        for i in range(NT):
            oslot = i % 2
            wts = [prev[("yln",)], wc_tok] + ([ps_cons[("o", oslot)]]
                                              if ("o", oslot) in ps_cons else [])
            mm = None
            for db in range(2):
                for hch in range(16):
                    mm = ctx.op("tensor", lambda e, i=i, db=db, hch=hch,
                                o=ps_big[oslot]:
                                e.matmul(o[:, db * 512:(db + 1) * 512],
                                         ylnT[:, hch * 2048 + i * 128:
                                             hch * 2048 + (i + 1) * 128],
                                         wc_sb[:, hch * 1024 + db * 512:
                                             hch * 1024 + (db + 1) * 512],
                                         start=(hch == 0), stop=(hch == 15)),
                                waits=wts if (db == 0 and hch == 0) else (),
                                sem=None if not (db == 1 and hch == 15) else "dflt")
            out_mm_last = mm
            st, tk_ps = sandwich_finish(ctx, aux, swb, ps_big[oslot][:], mm,
                                        x_in, x_out, row, i, prev)
            ps_cons[("o", oslot)] = tk_ps
            ps_cons[("b", oslot)] = tk_ps
        cons["wc"] = out_mm_last
        cons["xnT"] = out_mm_last
        ctx.barrier()


# version 06
# speedup vs baseline: 1.1728x; 1.1728x over previous
import math
import numpy as np
import ml_dtypes
import concourse.bass as bass
import concourse.mybir as mybir
from concourse.bass_utils import run_bass_kernel_spmd

F32 = mybir.dt.float32
BF16 = mybir.dt.bfloat16
AF = mybir.ActivationFunctionType
ALU = mybir.AluOpType
AX = mybir.AxisListType

R = 2
T = 2048
D = 1024
NT = T // 128
EPS = 1e-5
LAMBDA_INIT = 0.8 - 0.6 * math.exp(-0.3)
NEG = -1.0e30
GA = 0.044715
SGA = math.sqrt(GA)
TC = math.sqrt(2.0 / math.pi)
LNL = math.log(1.0 - LAMBDA_INIT)


class Sem:
    __slots__ = ("h", "n")

    def __init__(self, h):
        self.h = h
        self.n = 0

    def tok(self):
        return (self.h, self.n)


class Ctx:
    ENGINES = ("tensor", "vector", "scalar", "gpsimd", "sync")

    def __init__(self, nc):
        self.nc = nc
        self.streams = {k: [] for k in self.ENGINES}
        self._nsem = 0
        self.all_sems = []
        self.esem = {k: self.new_sem() for k in self.ENGINES}

    def new_sem(self):
        self._nsem += 1
        s = Sem(self.nc.alloc_semaphore(name=f"s{self._nsem}"))
        self.all_sems.append(s)
        return s

    serial = False

    def op(self, eng, fn, waits=(), sem="dflt", amt=1):
        if Ctx.serial and fn is not None:
            waits = list(waits) + [s.tok() for s in self.all_sems if s.n > 0]
        if sem == "dflt":
            s, amt = self.esem[eng], 1
        else:
            s = sem
        tok = None
        if s is not None:
            s.n += amt
            tok = (s.h, s.n)
        self.streams[eng].append((tuple(waits), fn, s.h if s else None, amt))
        return tok

    def dma(self, eng, out, in_, waits=(), sem=None, **kw):
        assert sem is not None
        return self.op(eng, lambda e: e.dma_start(out, in_, **kw), waits=waits,
                       sem=sem, amt=16)

    def barrier(self):
        toks = [s.tok() for s in self.all_sems if s.n > 0]
        for e in self.ENGINES:
            self.op(e, None, waits=toks, sem=None)
        return toks

    def finalize(self, block):
        def runner(name):
            stream = self.streams[name]

            def body(eng):
                waited = {}
                for waits, fn, semh, amt in stream:
                    for (h, v) in waits:
                        if waited.get(h, -1) < v:
                            eng.wait_ge(h, v)
                            waited[h] = v
                    if fn is not None:
                        inst = fn(eng)
                        if semh is not None:
                            inst.then_inc(semh, amt)
                            waited[semh] = max(waited.get(semh, -1), 0)
            return body

        block.tensor(runner("tensor"))
        block.vector(runner("vector"))
        block.scalar(runner("scalar"))
        block.gpsimd(runner("gpsimd"))
        block.sync(runner("sync"))


def _host_aux():
    idbf = np.eye(128, dtype=ml_dtypes.bfloat16)
    idf32 = np.eye(128, dtype=np.float32)
    ones_bf = np.ones((128, 128), dtype=ml_dtypes.bfloat16)
    ones_f32 = np.ones((128, 128), dtype=np.float32)
    mask = np.where(np.arange(128)[None, :] <= np.arange(128)[:, None],
                    np.float32(0.0), np.float32(NEG)).astype(np.float32)
    return {"aux_idbf": idbf, "aux_idf32": idf32, "aux_onesbf": ones_bf,
            "aux_onesf32": ones_f32, "aux_mask": mask}


CONSTS = [1e-5, 1.0 / 1024, -1.0 / 1024, -0.5, SGA, TC, 0.5, 0.125, 0.0625,
          1.0 / 128, -1.0 / 128, 1.0 - LAMBDA_INIT, -1.0, 1.0, 2.0,
          1.0 / 512, -1.0 / 512, LNL]


class Aux:

    def __init__(self, nc, ctx, stack):
        self.nc = nc
        self.ctx = ctx
        ap = lambda n, s, d: nc.dram_tensor(n, s, d, kind="ExternalInput").ap()
        self.idbf_h = ap("aux_idbf", [128, 128], BF16)
        self.idf32_h = ap("aux_idf32", [128, 128], F32)
        self.onesbf_h = ap("aux_onesbf", [128, 128], BF16)
        self.onesf32_h = ap("aux_onesf32", [128, 128], F32)
        self.mask_h = ap("aux_mask", [128, 128], F32)
        sb = lambda n, s, d: stack.enter_context(nc.sbuf_tensor(n, s, d))
        self.idbf = sb("idbf", [128, 128], BF16)
        self.idf32 = sb("idf32", [128, 128], F32)
        self.onesbf = sb("onesbf", [128, 128], BF16)
        self.onesf32 = sb("onesf32", [128, 128], F32)
        self.mask = sb("mask", [128, 128], F32)
        self.ld = ctx.new_sem()
        for dst, src in ((self.idbf, self.idbf_h), (self.idf32, self.idf32_h),
                         (self.onesbf, self.onesbf_h), (self.onesf32, self.onesf32_h),
                         (self.mask, self.mask_h)):
            ctx.dma("sync", dst[:], src, sem=self.ld)
        self.ld_tok = self.ld.tok()
        self.cpool = sb("cpool", [128, len(CONSTS)], F32)
        for j, v in enumerate(CONSTS):
            ctx.op("vector", lambda e, j=j, v=v:
                   e.memset(self.cpool[:, j:j + 1], float(v)))
            nc.const_aps.register(F32, float(v), self.cpool[:, j:j + 1])


def ln_to_xnT(ctx, aux, bufs, x_hbm, row, tiles, xnT, tcols, evac_waits=()):
    (xt, stats, sq_scr, xn_bf, ps_tr, xld, prev) = bufs
    xv = xnT[:].rearrange("p (c j) -> p c j", c=8)
    t0 = tiles[0]
    last = None
    for t in tiles:
        s = t % 2
        w = [prev[("xt", s)]] if ("xt", s) in prev else []
        lx = ctx.dma("sync", xt[:, s * 1024:(s + 1) * 1024],
                     x_hbm[row * T + t * 128: row * T + (t + 1) * 128, :],
                     waits=w, sem=xld[s])
        c0, c1, c2, c3, c4, c5 = [stats[:, s * 8 + i:s * 8 + i + 1] for i in range(6)]
        w_st = [prev[("st", s)]] if ("st", s) in prev else []
        tk = ctx.op("vector", lambda e, a=xt[:, s * 1024:(s + 1) * 1024], o=c0:
                    e.reduce_sum(o, a, axis=AX.X), waits=[lx] + w_st)
        tk = ctx.op("vector", lambda e, i=c0, o=c1:
                    e.tensor_scalar_mul(o, i, -1.0 / D), waits=[tk])
        tk = ctx.op("scalar", lambda e, a=xt[:, s * 1024:(s + 1) * 1024], o=sq_scr,
                    b=c1, acc=c2:
                    e.activation(o[:], a, AF.Square, bias=b, accum_out=acc),
                    waits=[tk])
        tk = ctx.op("scalar", lambda e, i=c2, o=c3:
                    e.activation(o, i, AF.Ln, scale=1.0 / D, bias=EPS), waits=[tk])
        tk = ctx.op("scalar", lambda e, i=c3, o=c4:
                    e.activation(o, i, AF.Exp, scale=-0.5), waits=[tk])
        tk = ctx.op("vector", lambda e, a=c1, b=c4, o=c5:
                    e.tensor_mul(o, a, b), waits=[tk])
        tkc = ctx.op("scalar", lambda e, a=xt[:, s * 1024:(s + 1) * 1024],
                     o=xn_bf[:, s * 1024:(s + 1) * 1024], sc=c4, b=c5:
                     e.activation(o, a, AF.Identity, scale=sc, bias=b), waits=[tk])
        prev[("xt", s)] = tkc
        prev[("st", s)] = tkc
        for g in range(2):
            ttr = None
            for cc in range(4):
                c = g * 4 + cc
                w2 = [tkc, aux.ld_tok]
                if ("tr", g) in prev:
                    w2.append(prev[("tr", g)])
                ttr = ctx.op("tensor", lambda e,
                             i=xn_bf[:, s * 1024 + c * 128:s * 1024 + (c + 1) * 128],
                             o=ps_tr[g][:, cc * 128:(cc + 1) * 128]:
                             e.transpose(o, i, aux.idbf[:]),
                             waits=(w2 if cc == 0 else ()),
                             sem=None if cc < 3 else "dflt")
            dst = xv[:, g * 4:(g + 1) * 4, (t - t0) * 128:(t - t0 + 1) * 128]
            tke = ctx.op("vector", lambda e, i=ps_tr[g][:, 0:512],
                         o=dst: e.tensor_copy(o, i),
                         waits=[ttr] + list(evac_waits))
            prev[("tr", g)] = tke
            last = tke
    return last


def sandwich_finish(ctx, aux, bufs, ps_ap, ps_tok, x_hbm_in, x_hbm_out, row, t,
                    prev):
    (xt2, stats2, sq_scr2, lnout, xld2) = bufs
    s = t % 2
    w = [prev[("xt2", s)]] if ("xt2", s) in prev else []
    lx = ctx.dma("sync", xt2[:, s * 1024:(s + 1) * 1024],
                 x_hbm_in[row * T + t * 128: row * T + (t + 1) * 128, :],
                 waits=w, sem=xld2[s])
    c0, c1, c2, c3, c4, c5 = [stats2[:, s * 8 + i:s * 8 + i + 1] for i in range(6)]
    w_st = [prev[("st2", s)]] if ("st2", s) in prev else []
    tk = ctx.op("vector", lambda e, a=ps_ap, o=c0:
                e.reduce_sum(o, a, axis=AX.X), waits=[ps_tok] + w_st)
    tk = ctx.op("vector", lambda e, i=c0, o=c1:
                e.tensor_scalar_mul(o, i, -1.0 / D), waits=[tk])
    tk = ctx.op("scalar", lambda e, a=ps_ap, o=sq_scr2[:],
                b=c1, acc=c2:
                e.activation(o, a, AF.Square, bias=b, accum_out=acc), waits=[tk])
    tk = ctx.op("scalar", lambda e, i=c2, o=c3:
                e.activation(o, i, AF.Ln, scale=1.0 / D, bias=EPS), waits=[tk])
    tk = ctx.op("scalar", lambda e, i=c3, o=c4:
                e.activation(o, i, AF.Exp, scale=-0.5), waits=[tk])
    tk = ctx.op("vector", lambda e, a=c1, b=c4, o=c5:
                e.tensor_mul(o, a, b), waits=[tk])
    wlno = [prev[("lno",)]] if ("lno",) in prev else []
    tk_ps = ctx.op("scalar", lambda e, a=ps_ap, o=lnout[:],
                   sc=c4, b=c5:
                   e.activation(o, a, AF.Identity, scale=sc, bias=b),
                   waits=[tk] + wlno)
    tk2 = ctx.op("vector", lambda e, a=xt2[:, s * 1024:(s + 1) * 1024],
                 b=lnout[:],
                 o=xt2[:, s * 1024:(s + 1) * 1024]:
                 e.tensor_add(o, a, b), waits=[lx, tk_ps])
    prev[("lno",)] = tk2
    st = ctx.dma("sync", x_hbm_out[row * T + t * 128: row * T + (t + 1) * 128, :],
                 xt2[:, s * 1024:(s + 1) * 1024], waits=[tk2], sem=xld2[s])
    prev[("xt2", s)] = st
    prev[("st2", s)] = tk_ps
    return st, tk_ps


def mk_ln_bufs(nc, ctx, stack, tag, ps_tr):
    xt = stack.enter_context(nc.sbuf_tensor(f"xt_{tag}", [128, 2048], F32))
    stats = stack.enter_context(nc.sbuf_tensor(f"st_{tag}", [128, 16], F32))
    sq = stack.enter_context(nc.sbuf_tensor(f"sq_{tag}", [128, 1024], F32))
    xn = stack.enter_context(nc.sbuf_tensor(f"xn_{tag}", [128, 2048], BF16))
    xld = [ctx.new_sem(), ctx.new_sem()]
    return (xt, stats, sq, xn, ps_tr, xld, {})


def mk_sw_bufs(nc, ctx, stack, tag):
    xt2 = stack.enter_context(nc.sbuf_tensor(f"xt2_{tag}", [128, 2048], F32))
    stats2 = stack.enter_context(nc.sbuf_tensor(f"st2_{tag}", [128, 16], F32))
    sq2 = stack.enter_context(nc.sbuf_tensor(f"sq2_{tag}", [128, 1024], F32))
    lnout = stack.enter_context(nc.sbuf_tensor(f"lno_{tag}", [128, 1024], F32))
    xld2 = [ctx.new_sem(), ctx.new_sem()]
    return (xt2, stats2, sq2, lnout, xld2)


def build_l0ffn(nc, ctx, aux, stack, W, x_in, x_out, rows=range(R)):
    u1 = stack.enter_context(nc.sbuf_tensor("u1", [128, 8 * 256], BF16))
    v1 = stack.enter_context(nc.sbuf_tensor("v1", [128, 2 * 4096], BF16))
    u2 = stack.enter_context(nc.sbuf_tensor("u2", [128, 32 * 256], BF16))
    v2 = stack.enter_context(nc.sbuf_tensor("v2", [128, 2 * 1024], BF16))
    b1r = stack.enter_context(nc.sbuf_tensor("b1r", [32, 128], F32))
    b1c = stack.enter_context(nc.sbuf_tensor("b1c", [128, 32], F32))
    b2row = stack.enter_context(nc.sbuf_tensor("b2row", [1, 1024], BF16))
    wld = ctx.new_sem()
    ctx.dma("gpsimd", u1[:].rearrange("p (c m) -> p c m", c=8),
            W["L0_ffn_U1"].rearrange("(c p) m -> p c m", p=128), sem=wld)
    ctx.dma("gpsimd", v1[:].rearrange("p (c m) -> p c m", c=2),
            W["L0_ffn_V1"].rearrange("(c p) m -> p c m", p=128), sem=wld)
    ctx.dma("gpsimd", u2[:].rearrange("p (c m) -> p c m", c=32),
            W["L0_ffn_U2"].rearrange("(c p) m -> p c m", p=128), sem=wld)
    ctx.dma("gpsimd", v2[:].rearrange("p (c m) -> p c m", c=2),
            W["L0_ffn_V2"].rearrange("(c p) m -> p c m", p=128), sem=wld)
    ctx.dma("gpsimd", b1r[:], W["L0_ffn_b1"].rearrange("(c m) -> c m", m=128),
            sem=wld)
    ctx.dma("gpsimd", b2row[:], W["L0_ffn_b2"].rearrange("(c m) -> c m", c=1),
            sem=wld)
    wtok = wld.tok()

    ps_tr = [stack.enter_context(nc.psum_tensor(f"pstr_f0_{i}", [128, 512], BF16))
             for i in range(2)]
    ps = [stack.enter_context(nc.psum_tensor(f"psf0_{i}", [128, 512], F32))
          for i in range(2)]
    ps_o = [stack.enter_context(nc.psum_tensor(f"psof0_{i}", [128, 1024], F32))
            for i in range(2)]
    t1 = ctx.op("tensor", lambda e: e.transpose(ps[0][:, 0:32], b1r[:],
                                                aux.idf32[0:32, 0:32]),
                waits=[wtok, aux.ld_tok])
    t1 = ctx.op("vector", lambda e: e.tensor_copy(b1c[:], ps[0][:, 0:32]),
                waits=[t1])
    ps_cons = {0: t1}
    pso_cons = {}
    cons = {}
    prev = {}

    lnb = mk_ln_bufs(nc, ctx, stack, "f0", ps_tr)
    swb = mk_sw_bufs(nc, ctx, stack, "f0")
    xnT = stack.enter_context(nc.sbuf_tensor("xnT_f0", [128, 8 * 512], BF16))
    pT = stack.enter_context(nc.sbuf_tensor("pT_f0", [128, 2 * 512], BF16))
    hT = stack.enter_context(nc.sbuf_tensor("hT_f0", [128, 32 * 512], BF16))
    qT = stack.enter_context(nc.sbuf_tensor("qT_f0", [128, 2 * 512], BF16))
    zb = stack.enter_context(nc.sbuf_tensor("zb_f0", [128, 4 * 512], F32))
    sb_ = stack.enter_context(nc.sbuf_tensor("sb_f0", [128, 4 * 512], F32))
    ib = stack.enter_context(nc.sbuf_tensor("ib_f0", [128, 4 * 512], F32))
    tb = stack.enter_context(nc.sbuf_tensor("tb_f0", [128, 4 * 512], F32))

    for row in rows:
        for g in range(4):
            tiles = range(g * 4, (g + 1) * 4)
            ew = [cons["xnT"]] if "xnT" in cons else []
            xn_tok = ln_to_xnT(ctx, aux, lnb, x_in, row, tiles, xnT, 512,
                               evac_waits=ew)
            ptoks = []
            for rc in range(2):
                pslot = rc
                wts = [xn_tok, wtok] + ([ps_cons[pslot]] if pslot in ps_cons else [])
                if "pT" in cons:
                    wts.append(cons["pT"])
                mm = None
                for c in range(8):
                    mm = ctx.op("tensor", lambda e, c=c, rc=rc, o=ps[pslot]:
                                e.matmul(o[:], u1[:, c * 256 + rc * 128:
                                                 c * 256 + rc * 128 + 128],
                                         xnT[:, c * 512:(c + 1) * 512],
                                         start=(c == 0), stop=(c == 7)),
                                waits=wts if c == 0 else (),
                                sem=None if c < 7 else "dflt")
                ev = ctx.op("scalar", lambda e, rc=rc, i=ps[pslot]:
                            e.activation(pT[:, rc * 512:(rc + 1) * 512], i[:],
                                         AF.Copy), waits=[mm])
                ps_cons[pslot] = ev
                ptoks.append(ev)
            h_mm_last = None
            htoks = []
            for f in range(32):
                pslot = f % 2
                fs = f % 4
                wts = ptoks + ([ps_cons[pslot]] if pslot in ps_cons else [])
                if f < 2 and "hT" in cons:
                    wts.append(cons["hT"])
                mm = None
                for rc in range(2):
                    mm = ctx.op("tensor", lambda e, f=f, rc=rc, o=ps[pslot]:
                                e.matmul(o[:], v1[:, rc * 4096 + f * 128:
                                                 rc * 4096 + (f + 1) * 128],
                                         pT[:, rc * 512:(rc + 1) * 512],
                                         start=(rc == 0), stop=(rc == 1)),
                                waits=wts if rc == 0 else (),
                                sem=None if rc < 1 else "dflt")
                h_mm_last = mm
                wz = [mm] + ([prev[("z", fs)]] if ("z", fs) in prev else [])
                tz = ctx.op("scalar", lambda e, f=f, fs=fs, i=ps[pslot]:
                            e.activation(zb[:, fs * 512:(fs + 1) * 512], i[:],
                                         AF.Identity, bias=b1c[:, f:f + 1]), waits=wz)
                ps_cons[pslot] = tz
                ts = ctx.op("scalar", lambda e, fs=fs:
                            e.activation(sb_[:, fs * 512:(fs + 1) * 512],
                                         zb[:, fs * 512:(fs + 1) * 512],
                                         AF.Square, scale=SGA), waits=[tz])
                ti = ctx.op("vector", lambda e, fs=fs:
                            e.scalar_tensor_tensor(ib[:, fs * 512:(fs + 1) * 512],
                                                   sb_[:, fs * 512:(fs + 1) * 512],
                                                   1.0,
                                                   zb[:, fs * 512:(fs + 1) * 512],
                                                   ALU.add, ALU.mult), waits=[ts])
                tt = ctx.op("scalar", lambda e, fs=fs:
                            e.activation(tb[:, fs * 512:(fs + 1) * 512],
                                         ib[:, fs * 512:(fs + 1) * 512],
                                         AF.Tanh, scale=TC), waits=[ti])
                tp = ctx.op("vector", lambda e, fs=fs:
                            e.tensor_scalar(ib[:, fs * 512:(fs + 1) * 512],
                                            tb[:, fs * 512:(fs + 1) * 512],
                                            0.5, 0.5, ALU.mult, ALU.add), waits=[tt])
                th = ctx.op("vector", lambda e, f=f, fs=fs:
                            e.tensor_mul(hT[:, f * 512:(f + 1) * 512],
                                         ib[:, fs * 512:(fs + 1) * 512],
                                         zb[:, fs * 512:(fs + 1) * 512]), waits=[tp])
                prev[("z", fs)] = th
                htoks.append(th)
            cons["pT"] = h_mm_last
            qtoks = []
            q_mm_last = None
            for rc in range(2):
                pslot = rc
                wts = htoks + ([ps_cons[pslot]] if pslot in ps_cons else [])
                if "qT" in cons:
                    wts.append(cons["qT"])
                mm = None
                for f in range(32):
                    mm = ctx.op("tensor", lambda e, f=f, rc=rc, o=ps[pslot]:
                                e.matmul(o[:], u2[:, f * 256 + rc * 128:
                                                 f * 256 + rc * 128 + 128],
                                         hT[:, f * 512:(f + 1) * 512],
                                         start=(f == 0), stop=(f == 31)),
                                waits=wts if f == 0 else (),
                                sem=None if f < 31 else "dflt")
                q_mm_last = mm
                ev = ctx.op("vector", lambda e, rc=rc, i=ps[pslot]:
                            e.tensor_copy(qT[:, rc * 512:(rc + 1) * 512], i[:]),
                            waits=[mm])
                ps_cons[pslot] = ev
                qtoks.append(ev)
            cons["hT"] = q_mm_last
            out_mm_last = None
            for tt_ in range(4):
                t = g * 4 + tt_
                oslot = tt_ % 2
                wts = qtoks + [wtok] + ([pso_cons[oslot]] if oslot in pso_cons else [])
                mm = None
                for db in range(2):
                    for rc in range(2):
                        mm = ctx.op("tensor", lambda e, tt_=tt_, db=db, rc=rc,
                                    o=ps_o[oslot]:
                                    e.matmul(o[:, db * 512:(db + 1) * 512],
                                             qT[:, rc * 512 + tt_ * 128:
                                                rc * 512 + tt_ * 128 + 128],
                                             v2[:, rc * 1024 + db * 512:
                                                rc * 1024 + (db + 1) * 512],
                                             start=(rc == 0), stop=False),
                                    waits=wts if (db == 0 and rc == 0) else (),
                                    sem=None)
                    mm = ctx.op("tensor", lambda e, db=db, o=ps_o[oslot]:
                                e.matmul(o[:, db * 512:(db + 1) * 512],
                                         aux.onesbf[0:1, :],
                                         b2row[0:1, db * 512:(db + 1) * 512],
                                         start=False, stop=True),
                                waits=(), sem=None if db < 1 else "dflt")
                out_mm_last = mm
                st, tk_ps = sandwich_finish(ctx, aux, swb, ps_o[oslot][:],
                                            mm, x_in, x_out, row, t, prev)
                pso_cons[oslot] = tk_ps
            cons["qT"] = out_mm_last
            cons["xnT"] = h_mm_last


def _winputs(nc, names):
    shp = dict(WSHAPES)
    return {n: nc.dram_tensor(n, list(shp[n]), F32, kind="ExternalInput").ap()
            for n in names}


WSHAPES = {
    "L0_attn_Wq1": (1024, 1024), "L0_attn_Wq2": (1024, 1024),
    "L0_attn_Wk1": (1024, 1024), "L0_attn_Wk2": (1024, 1024),
    "L0_attn_Wv": (1024, 2048), "L0_attn_Wc": (2048, 1024),
    "L0_attn_lq1": (16, 64), "L0_attn_lk1": (16, 64),
    "L0_attn_lq2": (16, 64), "L0_attn_lk2": (16, 64),
    "L0_ffn_U1": (1024, 256), "L0_ffn_V1": (256, 4096), "L0_ffn_b1": (4096,),
    "L0_ffn_U2": (4096, 256), "L0_ffn_V2": (256, 1024), "L0_ffn_b2": (1024,),
    "L1_attn_Wq": (1024, 256), "L1_attn_Wk": (1024, 256),
    "L1_attn_Wvdown": (1024, 256), "L1_attn_Wvup": (256, 1024),
    "L1_ffn_w1_w": (4096, 1024), "L1_ffn_w1_s": (4096,),
    "L1_ffn_w2_w": (4096, 1024), "L1_ffn_w2_s": (4096,),
    "L1_ffn_h1_w": (4096, 4096), "L1_ffn_h1_s": (4096,),
    "L1_ffn_h2_w": (4096, 4096), "L1_ffn_h2_s": (4096,),
    "L1_ffn_w3_w": (1024, 4096), "L1_ffn_w3_s": (1024,),
}

L0FFN_W = ["L0_ffn_U1", "L0_ffn_V1", "L0_ffn_b1", "L0_ffn_U2", "L0_ffn_V2",
           "L0_ffn_b2"]


def build_prog(stages):
    import contextlib
    nc = bass.Bass("TRN2", target_bir_lowering=False)
    x_in = nc.dram_tensor("x_in", [R * T, D], F32, kind="ExternalInput").ap()
    x_out = nc.dram_tensor("x_out", [R * T, D], F32, kind="ExternalOutput").ap()
    x_mid = [nc.dram_tensor(f"x_mid{i}", [R * T, D], F32, kind="Internal").ap()
             for i in range(max(0, len(stages) - 1))]
    wnames = sorted(set(n for _, ws in stages for n in ws))
    W = _winputs(nc, wnames)
    with contextlib.ExitStack() as stack:
        ctx = Ctx(nc)
        aux = Aux(nc, ctx, stack)
        for i, (builder, _) in enumerate(stages):
            xi = x_in if i == 0 else x_mid[i - 1]
            xo = x_out if i == len(stages) - 1 else x_mid[i]
            with contextlib.ExitStack() as sstack:
                builder(nc, ctx, aux, sstack, W, xi, xo)
                ctx.barrier()
        with nc.Block() as block:
            ctx.finalize(block)
    return nc, wnames


def _run(progs, inputs, x):
    aux_np = _host_aux()
    cur = [np.ascontiguousarray(x[2 * c:2 * c + 2].reshape(R * T, D))
           for c in range(8)]
    for nc, wnames in progs:
        in_maps = []
        for c in range(8):
            m = dict(aux_np)
            m["x_in"] = cur[c]
            for n in wnames:
                m[n] = np.ascontiguousarray(np.asarray(inputs[n], dtype=np.float32))
            in_maps.append(m)
        res = run_bass_kernel_spmd(nc, in_maps, core_ids=list(range(8)))
        cur = [res.results[c]["x_out"] for c in range(8)]
    return np.stack([cur[c].reshape(R, T, D) for c in range(8)])


def kernel(**inputs):
    x = np.ascontiguousarray(np.asarray(inputs["L0_attn_x"], dtype=np.float32))
    B = x.shape[0]
    progs = [build_prog([(build_l0attn, L0ATTN_W), (build_l0ffn, L0FFN_W),
                         (build_l1attn, L1ATTN_W), (build_l1ffn, L1FFN_W)])]
    out = _run(progs, inputs, x)
    return out.reshape(B, T, D)


L1ATTN_W = ["L1_attn_Wq", "L1_attn_Wk", "L1_attn_Wvdown", "L1_attn_Wvup"]


def build_l1attn(nc, ctx, aux, stack, W, x_in, x_out, rows=range(R)):
    wq = stack.enter_context(nc.sbuf_tensor("wq_a1", [128, 8 * 256], BF16))
    wk = stack.enter_context(nc.sbuf_tensor("wk_a1", [128, 8 * 256], BF16))
    wvd = stack.enter_context(nc.sbuf_tensor("wvd_a1", [128, 8 * 256], BF16))
    wvdT = stack.enter_context(nc.sbuf_tensor("wvdT_a1", [128, 2 * 1024], BF16))
    wvu = stack.enter_context(nc.sbuf_tensor("wvu_a1", [128, 2 * 1024], BF16))
    wveff = stack.enter_context(nc.sbuf_tensor("wveff_a1", [128, 8 * 1024], BF16))
    wld = ctx.new_sem()
    ctx.dma("gpsimd", wq[:].rearrange("p (c m) -> p c m", c=8),
            W["L1_attn_Wq"].rearrange("(c p) m -> p c m", p=128), sem=wld)
    ctx.dma("gpsimd", wk[:].rearrange("p (c m) -> p c m", c=8),
            W["L1_attn_Wk"].rearrange("(c p) m -> p c m", p=128), sem=wld)
    ctx.dma("gpsimd", wvd[:].rearrange("p (c m) -> p c m", c=8),
            W["L1_attn_Wvdown"].rearrange("(c p) m -> p c m", p=128), sem=wld)
    ctx.dma("gpsimd", wvu[:].rearrange("p (c m) -> p c m", c=2),
            W["L1_attn_Wvup"].rearrange("(c p) m -> p c m", p=128), sem=wld)
    wtok = wld.tok()

    ps_tr = [stack.enter_context(nc.psum_tensor(f"pstr_a1_{i}", [128, 512], BF16))
             for i in range(2)]
    ps = [stack.enter_context(nc.psum_tensor(f"ps_a1_{i}", [128, 512], F32))
          for i in range(2)]
    ps_o = [stack.enter_context(nc.psum_tensor(f"pso_a1_{i}", [128, 1024], F32))
            for i in range(2)]

    tr_toks = []
    for c in range(8):
        for rr in range(2):
            tt = ctx.op("tensor", lambda e, c=c, rr=rr:
                        e.transpose(ps_tr[rr][:, (c % 4) * 128:(c % 4 + 1) * 128],
                                    wvd[:, c * 256 + rr * 128:c * 256 + (rr + 1) * 128],
                                    aux.idbf[:]),
                        waits=[wtok, aux.ld_tok] + tr_toks[-2:],
                        sem=None if False else "dflt")
        if c % 4 == 3:
            for rr in range(2):
                tk = ctx.op("vector", lambda e, c=c, rr=rr:
                            e.tensor_copy(wvdT[:, rr * 1024 + (c - 3) * 128:
                                               rr * 1024 + (c + 1) * 128],
                                          ps_tr[rr][:, 0:512]),
                            waits=[ctx.esem["tensor"].tok()])
                tr_toks.append(tk)
    wvdT_tok = tr_toks[-1]
    ve_toks = []
    for ec in range(8):
        for dvb in range(2):
            pslot = dvb
            wts = [wvdT_tok, wtok] + ve_toks[-2:]
            mm = None
            for rr in range(2):
                mm = ctx.op("tensor", lambda e, ec=ec, dvb=dvb, rr=rr, o=ps[pslot]:
                            e.matmul(o[:], wvdT[:, rr * 1024 + ec * 128:
                                             rr * 1024 + (ec + 1) * 128],
                                     wvu[:, rr * 1024 + dvb * 512:
                                         rr * 1024 + (dvb + 1) * 512],
                                     start=(rr == 0), stop=(rr == 1)),
                            waits=wts if rr == 0 else (),
                            sem=None if rr < 1 else "dflt")
            ev = ctx.op("scalar", lambda e, ec=ec, dvb=dvb, i=ps[pslot]:
                        e.activation(wveff[:, ec * 1024 + dvb * 512:
                                        ec * 1024 + (dvb + 1) * 512], i[:],
                                     AF.Copy), waits=[mm])
            ve_toks.append(ev)
    ps_cons = {0: ve_toks[-2], 1: ve_toks[-1]}
    pso_cons = {}
    cons = {}
    prev = {}

    lnb = mk_ln_bufs(nc, ctx, stack, "a1", ps_tr)
    swb = mk_sw_bufs(nc, ctx, stack, "a1")
    xnT = stack.enter_context(nc.sbuf_tensor("xnT_a1", [128, 8 * 2048], BF16))
    qT = stack.enter_context(nc.sbuf_tensor("qT_a1", [128, 2 * 2048], BF16))
    kT = stack.enter_context(nc.sbuf_tensor("kT_a1", [128, 2 * 2048], BF16))
    v_sb = stack.enter_context(nc.sbuf_tensor("v_a1", [128, 16 * 1024], BF16))
    e_bf = stack.enter_context(nc.sbuf_tensor("e_a1", [128, 2048], BF16))
    att_bf = stack.enter_context(nc.sbuf_tensor("att_a1", [128, 2048], BF16))
    attT = stack.enter_context(nc.sbuf_tensor("attT_a1", [128, 2048], BF16))
    s_sb = stack.enter_context(nc.sbuf_tensor("s_a1", [128, 512], F32))
    zp = stack.enter_context(nc.sbuf_tensor("zp_a1", [128, 8], F32))

    for row in rows:
        ew = [cons["xnT"]] if "xnT" in cons else []
        if row == 0:
            lnb[6][("tr", 0)] = tr_toks[-2]
            lnb[6][("tr", 1)] = tr_toks[-1]
        xn_tok = ln_to_xnT(ctx, aux, lnb, x_in, row, range(NT), xnT, 2048,
                           evac_waits=ew)
        qk_toks = []
        for w_sb, oT in ((wq, qT), (wk, kT)):
            for c2 in range(2):
                for nb in range(4):
                    pslot = nb % 2
                    wts = [xn_tok, wtok] + ([ps_cons[pslot]] if pslot in ps_cons else [])
                    if "qT" in cons:
                        wts.append(cons["qT"])
                    mm = None
                    for c in range(8):
                        mm = ctx.op("tensor", lambda e, c=c, c2=c2, nb=nb,
                                    w_sb=w_sb, o=ps[pslot]:
                                    e.matmul(o[:], w_sb[:, c * 256 + c2 * 128:
                                                     c * 256 + c2 * 128 + 128],
                                             xnT[:, c * 2048 + nb * 512:
                                                 c * 2048 + (nb + 1) * 512],
                                             start=(c == 0), stop=(c == 7)),
                                    waits=wts if c == 0 else (),
                                    sem=None if c < 7 else "dflt")
                    ev = ctx.op("scalar", lambda e, c2=c2, nb=nb, oT=oT, i=ps[pslot]:
                                e.activation(oT[:, c2 * 2048 + nb * 512:
                                              c2 * 2048 + (nb + 1) * 512], i[:],
                                             AF.Copy), waits=[mm])
                    ps_cons[pslot] = ev
                    qk_toks.append(ev)
        v_toks = []
        for t in range(NT):
            for dvb in range(2):
                pslot = dvb
                wts = [xn_tok, ve_toks[-1]] + ([ps_cons[pslot]] if pslot in ps_cons else [])
                if "v_sb" in cons:
                    wts.append(cons["v_sb"])
                mm = None
                for c in range(8):
                    mm = ctx.op("tensor", lambda e, c=c, t=t, dvb=dvb, o=ps[pslot]:
                                e.matmul(o[:], xnT[:, c * 2048 + t * 128:
                                                 c * 2048 + (t + 1) * 128],
                                         wveff[:, c * 1024 + dvb * 512:
                                             c * 1024 + (dvb + 1) * 512],
                                         start=(c == 0), stop=(c == 7)),
                                waits=wts if c == 0 else (),
                                sem=None if c < 7 else "dflt")
                ev = ctx.op("vector", lambda e, t=t, dvb=dvb, i=ps[pslot]:
                            e.tensor_copy(v_sb[:, t * 1024 + dvb * 512:
                                             t * 1024 + (dvb + 1) * 512], i[:]),
                            waits=[mm])
                ps_cons[pslot] = ev
                v_toks.append(ev)
        cons["xnT"] = v_toks[-1]
        for i in range(NT):
            L = (i + 1) * 128
            nkb = (L + 511) // 512
            ztoks = []
            etoks = []
            for kb in range(nkb):
                klen = min(512, L - kb * 512)
                pslot = kb % 2
                wts = qk_toks[-1:] + ([ps_cons[pslot]] if pslot in ps_cons else [])
                ewait = [prev[("e",)]] if (("e",) in prev and kb == 0) else []
                mm = None
                for c2 in range(2):
                    mm = ctx.op("tensor", lambda e, i=i, kb=kb, c2=c2, klen=klen,
                                o=ps[pslot]:
                                e.matmul(o[:, 0:klen],
                                         qT[:, c2 * 2048 + i * 128:
                                            c2 * 2048 + (i + 1) * 128],
                                         kT[:, c2 * 2048 + kb * 512:
                                            c2 * 2048 + kb * 512 + klen],
                                         start=(c2 == 0), stop=(c2 == 1)),
                                waits=wts if c2 == 0 else (),
                                sem=None if c2 < 1 else "dflt")
                if kb == nkb - 1:
                    t1 = ctx.op("scalar", lambda e, klen=klen, i_=ps[pslot]:
                                e.activation(s_sb[:, 0:klen], i_[:, 0:klen],
                                             AF.Identity, scale=0.0625),
                                waits=[mm, prev.get("s_sb", mm)] + ewait)
                    ps_cons[pslot] = t1
                    t2 = ctx.op("vector", lambda e, klen=klen:
                                e.tensor_add(s_sb[:, klen - 128:klen],
                                             s_sb[:, klen - 128:klen],
                                             aux.mask[:]), waits=[t1, aux.ld_tok])
                    t3 = ctx.op("scalar", lambda e, kb=kb, klen=klen:
                                e.activation(e_bf[:, kb * 512:kb * 512 + klen],
                                             s_sb[:, 0:klen], AF.Exp,
                                             accum_out=zp[:, kb:kb + 1]),
                                waits=[t2])
                    prev["s_sb"] = t3
                    ztoks.append(t3)
                    etoks.append(t3)
                else:
                    t3 = ctx.op("scalar", lambda e, kb=kb, i_=ps[pslot]:
                                e.activation(e_bf[:, kb * 512:(kb + 1) * 512],
                                             i_[:], AF.Exp, scale=0.0625,
                                             accum_out=zp[:, kb:kb + 1]),
                                waits=[mm] + ewait)
                    ps_cons[pslot] = t3
                    ztoks.append(t3)
                    etoks.append(t3)
            tz = ctx.op("vector", lambda e, nkb=nkb:
                        e.reduce_sum(zp[:, 6:7], zp[:, 0:nkb], axis=AX.X),
                        waits=ztoks)
            tz = ctx.op("vector", lambda e: e.reciprocal(zp[:, 7:8], zp[:, 6:7]),
                        waits=[tz])
            ta = ctx.op("vector", lambda e, L=L:
                        e.tensor_scalar(att_bf[:, 0:L], e_bf[:, 0:L],
                                        zp[:, 7:8], None, ALU.mult),
                        waits=[tz] + etoks + ([prev[("att", 0)]] if ("att", 0) in prev else []))
            prev[("e",)] = ta
            ev_toks = []
            for jg in range((i + 1 + 3) // 4):
                ntile = min(4, i + 1 - jg * 4)
                tslot = jg % 2
                wts = [ta] + ([prev[("atr", tslot)]] if ("atr", tslot) in prev else [])
                if ("tr", tslot) in lnb[6]:
                    wts.append(lnb[6][("tr", tslot)])
                ttr = None
                for jj in range(ntile):
                    j = jg * 4 + jj
                    ttr = ctx.op("tensor", lambda e, j=j, jj=jj, tslot=tslot:
                                 e.transpose(ps_tr[tslot][:, jj * 128:(jj + 1) * 128],
                                             att_bf[:, j * 128:(j + 1) * 128],
                                             aux.idbf[:]),
                                 waits=wts if jj == 0 else (),
                                 sem=None if jj < ntile - 1 else "dflt")
                tke = ctx.op("vector", lambda e, jg=jg, ntile=ntile, tslot=tslot:
                             e.tensor_copy(attT[:, jg * 512:jg * 512 + ntile * 128],
                                           ps_tr[tslot][:, 0:ntile * 128]),
                             waits=[ttr])
                lnb[6][("tr", tslot)] = tke
                prev[("atr", tslot)] = tke
                ev_toks.append(tke)
            oslot = i % 2
            wts = ev_toks + v_toks[-1:] + ([pso_cons[oslot]] if oslot in pso_cons else [])
            mm = None
            for dvb in range(2):
                for j in range(i + 1):
                    mm = ctx.op("tensor", lambda e, j=j, dvb=dvb, o=ps_o[oslot]:
                                e.matmul(o[:, dvb * 512:(dvb + 1) * 512],
                                         attT[:, j * 128:(j + 1) * 128],
                                         v_sb[:, j * 1024 + dvb * 512:
                                             j * 1024 + (dvb + 1) * 512],
                                         start=(j == 0), stop=(j == i)),
                                waits=wts if (dvb == 0 and j == 0) else (),
                                sem=None if not (dvb == 1 and j == i) else "dflt")
            prev[("att", 0)] = mm
            st, tk_ps = sandwich_finish(ctx, aux, swb, ps_o[oslot][:], mm,
                                        x_in, x_out, row, i, prev)
            pso_cons[oslot] = tk_ps
        cons["qT"] = mm
        cons["v_sb"] = mm


L1FFN_W = ["L1_ffn_w1_w", "L1_ffn_w1_s", "L1_ffn_w2_w", "L1_ffn_w2_s",
           "L1_ffn_h1_w", "L1_ffn_h1_s", "L1_ffn_h2_w", "L1_ffn_h2_s",
           "L1_ffn_w3_w", "L1_ffn_w3_s"]


def l1ffn_prep(nc, ctx, aux, stack, W, scr, wch, ssb, ps_tr):
    if True:
        wsem = [ctx.new_sem(), ctx.new_sem()]
        stsem = [ctx.new_sem(), ctx.new_sem()]
        prev = {}
        for mat, ns, nc_ in (("L1_ffn_w1_w", 32, 8), ("L1_ffn_w2_w", 32, 8),
                             ("L1_ffn_h1_w", 32, 32), ("L1_ffn_h2_w", 32, 32),
                             ("L1_ffn_w3_w", 8, 32)):
            ni = nc_ * 128
            for s in range(ns):
                sl = (len(prev) + s) % 2 if False else s % 2
                w = [prev[("wch", sl)]] if ("wch", sl) in prev else []
                ld = ctx.dma("gpsimd", wch[sl][:, 0:ni],
                             W[mat][s * 128:(s + 1) * 128, :], waits=w,
                             sem=wsem[sl])
                ev_last = None
                for cg in range((nc_ + 3) // 4):
                    ntile = min(4, nc_ - cg * 4)
                    tslot = cg % 2
                    wts = [ld, aux.ld_tok]
                    if ("ptr", tslot) in prev:
                        wts.append(prev[("ptr", tslot)])
                    ttr = None
                    for cc in range(ntile):
                        c = cg * 4 + cc
                        ttr = ctx.op("tensor", lambda e, c=c, cc=cc, sl=sl, tslot=tslot:
                                     e.transpose(ps_tr[tslot][:, cc * 128:(cc + 1) * 128],
                                                 wch[sl][:, c * 128:(c + 1) * 128],
                                                 aux.idbf[:]),
                                     waits=wts if cc == 0 else (),
                                     sem=None if cc < ntile - 1 else "dflt")
                    wse = [prev[("ssb", sl)]] if (("ssb", sl) in prev and cg == 0) else []
                    ev_last = ctx.op("vector", lambda e, cg=cg, ntile=ntile, sl=sl,
                                     tslot=tslot:
                                     e.tensor_copy(ssb[sl][:, cg * 512:cg * 512 + ntile * 128],
                                                   ps_tr[tslot][:, 0:ntile * 128]),
                                     waits=[ttr] + wse)
                    prev[("ptr", tslot)] = ev_last
                prev[("wch", sl)] = ev_last
                st = ctx.dma("sync", scr[mat][s * 128:(s + 1) * 128, :],
                             ssb[sl][:, 0:ni], waits=[ev_last], sem=stsem[sl])
                prev[("ssb", sl)] = st
        ctx.barrier()


def build_l1ffn(nc, ctx, aux, stack, W, x_in, x_out, rows=range(R), scr=None):
    ps = [stack.enter_context(nc.psum_tensor(f"ps_f1_{i}", [128, 512], F32))
          for i in range(2)]
    ps_tr = [stack.enter_context(nc.psum_tensor(f"pstr_f1_{i}", [128, 512], BF16))
             for i in range(2)]
    pso_bf = [stack.enter_context(nc.psum_tensor(f"psob_f1_{i}", [128, 1024], BF16))
              for i in range(2)]
    a_sb = stack.enter_context(nc.sbuf_tensor("a_f1", [128, 32 * 1024], BF16))
    h0T = stack.enter_context(nc.sbuf_tensor("h0_f1", [128, 32 * 1024], BF16))
    if scr is None:
        scr = {}
        for mat, ns, nc_ in (("L1_ffn_w1_w", 32, 8), ("L1_ffn_w2_w", 32, 8),
                             ("L1_ffn_h1_w", 32, 32), ("L1_ffn_h2_w", 32, 32),
                             ("L1_ffn_w3_w", 8, 32)):
            scr[mat] = nc.dram_tensor("scr_" + mat, [ns * 128, nc_ * 128], BF16,
                                      kind="Internal").ap()
        l1ffn_prep(nc, ctx, aux, stack, W, scr,
                   [a_sb[:, 0:4096], a_sb[:, 4096:8192]],
                   [h0T[:, 0:4096], h0T[:, 4096:8192]], ps_tr)
    scol = stack.enter_context(nc.sbuf_tensor("scol_f1", [128, 4 * 32 + 8], F32))
    sld = ctx.new_sem()
    ps_cons = {}
    stoks = []
    srow = stack.enter_context(nc.sbuf_tensor("srow_f1", [32, 128], F32))
    for j, (sn, ncol) in enumerate((("L1_ffn_w1_s", 32), ("L1_ffn_w2_s", 32),
                                    ("L1_ffn_h1_s", 32), ("L1_ffn_h2_s", 32),
                                    ("L1_ffn_w3_s", 8))):
        w = [stoks[-1]] if stoks else []
        ld = ctx.dma("sync", srow[0:ncol, :],
                     W[sn].rearrange("(c m) -> c m", m=128), waits=w, sem=sld)
        tt = ctx.op("tensor", lambda e, ncol=ncol:
                    e.transpose(ps[0][:, 0:ncol], srow[0:ncol, :],
                                aux.idf32[0:ncol, 0:ncol]),
                    waits=[ld, aux.ld_tok] + ([stoks[-1]] if stoks else []))
        tk = ctx.op("vector", lambda e, j=j, ncol=ncol:
                    e.tensor_copy(scol[:, j * 32:j * 32 + ncol], ps[0][:, 0:ncol]),
                    waits=[tt])
        stoks.append(tk)
    scol_tok = stoks[-1]
    ps_cons[0] = scol_tok
    w1s = scol[:, 0:32]; w2s = scol[:, 32:64]
    h1s = scol[:, 64:96]; h2s = scol[:, 96:128]; w3s = scol[:, 128:136]

    lnb = mk_ln_bufs(nc, ctx, stack, "f1", ps_tr)
    swb = mk_sw_bufs(nc, ctx, stack, "f1")
    xnT = stack.enter_context(nc.sbuf_tensor("xnT_f1", [128, 8 * 1024], BF16))
    outT = xnT
    wsb_s = [stack.enter_context(nc.sbuf_tensor(f"wss_f1_{i}", [128, 1024], BF16))
             for i in range(2)]
    wsb_b = [stack.enter_context(nc.sbuf_tensor(f"wsb_f1_{i}", [128, 4096], BF16))
             for i in range(2)]
    wsem_s = [ctx.new_sem(), ctx.new_sem()]
    wsem_b = [ctx.new_sem(), ctx.new_sem()]
    cons = {}
    prev = {}

    def tlin_pass(src_mat, nc_, rhs_buf, evac_fn, war_tok, xn_tok, ns=32,
                  wsb=None, wsem=None):
        toks = []
        mm = None
        for s in range(ns):
            sl = s % 2
            w = [prev[(src_mat, "w", sl)]] if (src_mat, "w", sl) in prev else []
            if ("wuse", wsb[sl].name) in prev:
                w.append(prev[("wuse", wsb[sl].name)])
            ld = ctx.dma("sync", wsb[sl][:, 0:nc_ * 128],
                         scr[src_mat][s * 128:(s + 1) * 128, :], waits=w,
                         sem=wsem[sl])
            for ng in range(2):
                pslot = ng
                wts = [ld, xn_tok] + ([ps_cons[pslot]] if pslot in ps_cons else [])
                if war_tok is not None and s == 0 and ng == 0:
                    wts.append(war_tok)
                for c in range(nc_):
                    mm = ctx.op("tensor", lambda e, s=s, sl=sl, c=c, ng=ng,
                                o=ps[pslot], rhs_buf=rhs_buf:
                                e.matmul(o[:], wsb[sl][:, c * 128:(c + 1) * 128],
                                         rhs_buf[:, c * 1024 + ng * 512:
                                                 c * 1024 + ng * 512 + 512],
                                         start=(c == 0), stop=(c == nc_ - 1)),
                                waits=wts if c == 0 else (),
                                sem=None if c < nc_ - 1 else "dflt")
                ev = evac_fn(s, ng, pslot, mm)
                ps_cons[pslot] = ev
                toks.append(ev)
            prev[("wuse", wsb[sl].name)] = mm
        return toks, mm

    for row in rows:
        for hh in range(2):
            tiles = range(hh * 8, (hh + 1) * 8)
            ew = [cons["xnT"]] if "xnT" in cons else []
            xn_tok = ln_to_xnT(ctx, aux, lnb, x_in, row, tiles, xnT, 1024,
                               evac_waits=ew)
            def ev_t1(s, ng, pslot, mm):
                w = [mm, scol_tok]
                if ("a", s, ng) in prev:
                    w.append(prev[("a", s, ng)])
                if cons.get("a_war") is not None and s == 0 and ng == 0:
                    w.append(cons["a_war"])
                return ctx.op("scalar", lambda e, s=s, ng=ng, i=ps[pslot]:
                              e.activation(a_sb[:, s * 1024 + ng * 512:
                                              s * 1024 + ng * 512 + 512],
                                           i[:], AF.Silu, scale=w1s[:, s:s + 1]),
                              waits=w)
            t1toks, mm1 = tlin_pass("L1_ffn_w1_w", 8, xnT, ev_t1,
                                    cons.get("xn_war"), xn_tok, wsb=wsb_s,
                                    wsem=wsem_s)
            def ev_t2(s, ng, pslot, mm):
                w = [mm, scol_tok, t1toks[2 * s + ng]]
                if ("h0", s, ng) in prev:
                    w.append(prev[("h0", s, ng)])
                tk = ctx.op("vector", lambda e, s=s, ng=ng, i=ps[pslot]:
                            e.scalar_tensor_tensor(
                                h0T[:, s * 1024 + ng * 512:s * 1024 + ng * 512 + 512],
                                i[:], w2s[:, s:s + 1],
                                a_sb[:, s * 1024 + ng * 512:s * 1024 + ng * 512 + 512],
                                ALU.mult, ALU.mult), waits=w)
                prev[("h0", s, ng)] = tk
                return tk
            t2toks, mm2 = tlin_pass("L1_ffn_w2_w", 8, xnT, ev_t2, None, xn_tok,
                                    wsb=wsb_s, wsem=wsem_s)
            cons["xnT_dead"] = mm2
            def ev_h1(s, ng, pslot, mm):
                w = [mm, scol_tok, t2toks[-1]]
                return ctx.op("scalar", lambda e, s=s, ng=ng, i=ps[pslot]:
                              e.activation(a_sb[:, s * 1024 + ng * 512:
                                              s * 1024 + ng * 512 + 512],
                                           i[:], AF.Gelu, scale=h1s[:, s:s + 1]),
                              waits=w)
            h1toks, mm3 = tlin_pass("L1_ffn_h1_w", 32, h0T, ev_h1, t2toks[-1],
                                    t2toks[-1], wsb=wsb_b, wsem=wsem_b)
            def ev_h2(s, ng, pslot, mm):
                w = [mm, scol_tok, h1toks[-1]]
                return ctx.op("scalar", lambda e, s=s, ng=ng, i=ps[pslot]:
                              e.activation(h0T[:, s * 1024 + ng * 512:
                                              s * 1024 + ng * 512 + 512],
                                           i[:], AF.Gelu, scale=h2s[:, s:s + 1]),
                              waits=w)
            h2toks, mm4 = tlin_pass("L1_ffn_h2_w", 32, a_sb, ev_h2, mm3,
                                    h1toks[-1], wsb=wsb_b, wsem=wsem_b)
            cons["a_war"] = mm4
            def ev_w3(s, ng, pslot, mm):
                w = [mm, scol_tok, h2toks[-1], cons["xnT_dead"]]
                if "oT_war" in cons:
                    w.append(cons["oT_war"])
                if ("oT", s, ng) in prev:
                    w.append(prev[("oT", s, ng)])
                tk = ctx.op("scalar", lambda e, s=s, ng=ng, i=ps[pslot]:
                            e.activation(outT[:, s * 1024 + ng * 512:
                                            s * 1024 + ng * 512 + 512],
                                         i[:], AF.Identity, scale=w3s[:, s:s + 1]),
                            waits=w)
                prev[("oT", s, ng)] = tk
                return tk
            w3toks, mm5 = tlin_pass("L1_ffn_w3_w", 32, h0T, ev_w3, mm4,
                                    h2toks[-1], ns=8, wsb=wsb_b, wsem=wsem_b)
            cons["xn_war"] = mm5
            for tt_ in range(8):
                t = hh * 8 + tt_
                oslot = tt_ % 2
                wts = [w3toks[-1]]
                if ("pso", oslot) in prev:
                    wts.append(prev[("pso", oslot)])
                ttr = None
                for oc in range(8):
                    ttr = ctx.op("tensor", lambda e, oc=oc, tt_=tt_, oslot=oslot:
                                 e.transpose(pso_bf[oslot][:, oc * 128:(oc + 1) * 128],
                                             outT[:, oc * 1024 + tt_ * 128:
                                                 oc * 1024 + (tt_ + 1) * 128],
                                             aux.idbf[:]),
                                 waits=wts if oc == 0 else (),
                                 sem=None if oc < 7 else "dflt")
                st, tk_ps = sandwich_finish(ctx, aux, swb, pso_bf[oslot][:], ttr,
                                            x_in, x_out, row, t, prev)
                prev[("pso", oslot)] = tk_ps
                cons["oT_war"] = ttr
                cons["xnT"] = ttr


L0ATTN_W = ["L0_attn_Wq1", "L0_attn_Wq2", "L0_attn_Wk1", "L0_attn_Wk2",
            "L0_attn_Wv", "L0_attn_Wc", "L0_attn_lq1", "L0_attn_lk1",
            "L0_attn_lq2", "L0_attn_lk2"]

def build_l0attn(nc, ctx, aux, stack, W, x_in, x_out, rows=range(R)):
    import contextlib
    ps_big = [stack.enter_context(nc.psum_tensor(f"psb_a0_{i}", [128, 1024], F32))
              for i in range(2)]
    ps_tr = [stack.enter_context(nc.psum_tensor(f"pstr_a0_{i}", [128, 512], BF16))
             for i in range(2)]
    ps_y = [stack.enter_context(nc.psum_tensor(f"psy_a0_{i}", [128, 512], F32))
            for i in range(2)]
    lt = stack.enter_context(nc.sbuf_tensor("lt_a0", [64, 4 * 16], F32))
    lm = stack.enter_context(nc.sbuf_tensor("lm_a0", [64, 2 * 16], F32))
    lam_bc = stack.enter_context(nc.sbuf_tensor("lam_a0", [128, 32], F32))
    lld = ctx.new_sem()
    for j, n in enumerate(("L0_attn_lq1", "L0_attn_lk1", "L0_attn_lq2",
                           "L0_attn_lk2")):
        ctx.dma("sync", lt[:, j * 16:(j + 1) * 16],
                W[n].rearrange("h d -> d h"), sem=lld,
                allow_slow_non_contiguous=True)
    ltok = lld.tok()
    tm1 = ctx.op("vector", lambda e: e.tensor_mul(lm[:, 0:16], lt[:, 0:16],
                                                  lt[:, 16:32]), waits=[ltok])
    tm2 = ctx.op("vector", lambda e: e.tensor_mul(lm[:, 16:32], lt[:, 32:48],
                                                  lt[:, 48:64]), waits=[tm1])
    tl1 = ctx.op("tensor", lambda e: e.matmul(ps_big[0][:, 0:16],
                                              aux.onesf32[0:64, :],
                                              lm[:, 0:16], start=True, stop=True),
                 waits=[tm1, aux.ld_tok])
    tl2 = ctx.op("tensor", lambda e: e.matmul(ps_big[0][:, 16:32],
                                              aux.onesf32[0:64, :],
                                              lm[:, 16:32], start=True, stop=True),
                 waits=[tm2])
    te1 = ctx.op("scalar", lambda e: e.activation(lam_bc[:, 16:32],
                                                  ps_big[0][:, 0:16],
                                                  AF.Exp), waits=[tl1, tl2])
    te2 = ctx.op("scalar", lambda e: e.activation(lam_bc[:, 0:16],
                                                  ps_big[0][:, 16:32],
                                                  AF.Exp), waits=[te1])
    tv1 = ctx.op("vector", lambda e:
                 e.scalar_tensor_tensor(lam_bc[:, 0:16], lam_bc[:, 0:16], -1.0,
                                        lam_bc[:, 16:32], ALU.mult, ALU.add),
                 waits=[te2])
    tv2 = ctx.op("vector", lambda e:
                 e.tensor_scalar_add(lam_bc[:, 0:16], lam_bc[:, 0:16],
                                     LAMBDA_INIT), waits=[tv1])
    lam_tok = tv2
    ps_cons = {("b", 0): te2}

    xnT = stack.enter_context(nc.sbuf_tensor("xnT_a0", [128, 8 * 2048], BF16))
    ylnT = stack.enter_context(nc.sbuf_tensor("ylnT_a0", [128, 16 * 2048], BF16))
    cons = {}
    prev = {}
    wsem = [ctx.new_sem(), ctx.new_sem()]
    wcld = ctx.new_sem()

    lnb = mk_ln_bufs(nc, ctx, stack, "a0", ps_tr)
    swb = mk_sw_bufs(nc, ctx, stack, "a0")
    trd = lnb[6]
    qkT = [stack.enter_context(nc.sbuf_tensor(f"qkT_a0_{i}", [128, 2048], BF16))
           for i in range(4)]
    v_p = [stack.enter_context(nc.sbuf_tensor("vp_a0_0", [128, 16 * 256], BF16))]
    wqk = [stack.enter_context(nc.sbuf_tensor("wqk_a0_0",
                                              [128, 4 * 1024 + 2048], BF16))]
    e1b = [stack.enter_context(nc.sbuf_tensor(f"e1_a0_{i}", [128, 2048], BF16))
           for i in range(2)]
    e2b = [stack.enter_context(nc.sbuf_tensor(f"e2_a0_{i}", [128, 2048], BF16))
           for i in range(2)]
    attb = stack.enter_context(nc.sbuf_tensor("att_a0", [128, 2048], BF16))
    attc = [stack.enter_context(nc.sbuf_tensor(f"attc_a0_{i}", [128, 2048], BF16))
            for i in range(2)]
    attT = [stack.enter_context(nc.sbuf_tensor(f"attT_a0_{i}", [128, 2048], BF16))
            for i in range(2)]
    ssb1 = [stack.enter_context(nc.sbuf_tensor("ss1_a0", [128, 512], F32))] * 2
    ssb2 = [stack.enter_context(nc.sbuf_tensor("ss2_a0", [128, 512], F32))] * 2
    gn = stack.enter_context(nc.sbuf_tensor("gn_a0", [128, 128], F32))
    ylnbf = [stack.enter_context(nc.sbuf_tensor(f"ylnbf_a0_{i}", [128, 128], BF16))
             for i in range(2)]
    zp = stack.enter_context(nc.sbuf_tensor("zp_a0", [128, 32], F32))
    wc_sb = xnT

    for row in rows:
        ew = [cons["xnT"]] if "xnT" in cons else []
        xn_tok = ln_to_xnT(ctx, aux, lnb, x_in, row, range(NT), xnT, 2048,
                           evac_waits=ew)
        pair_mm_last = None
        for p in range(8):
            sl = 0
            wv = [cons[("w", sl)]] if ("w", sl) in cons else []
            wts_ld = []
            for j, n in enumerate(("L0_attn_Wq1", "L0_attn_Wq2", "L0_attn_Wk1",
                                   "L0_attn_Wk2")):
                wts_ld.append(ctx.dma(
                    "gpsimd",
                    wqk[sl][:, j * 1024:(j + 1) * 1024].rearrange(
                        "p (c m) -> p c m", c=8),
                    W[n][:, p * 128:(p + 1) * 128].rearrange(
                        "(c p) m -> p c m", p=128), waits=wv, sem=wsem[sl]))
            wts_ld.append(ctx.dma(
                "gpsimd",
                wqk[sl][:, 4096:4096 + 2048].rearrange("p (c m) -> p c m", c=8),
                W["L0_attn_Wv"][:, p * 256:(p + 1) * 256].rearrange(
                    "(c p) m -> p c m", p=128), waits=wv, sem=wsem[sl]))
            wtok = wts_ld[-1]
            pj_toks = []
            for j in range(4):
                for nb in range(4):
                    pslot = nb % 2
                    wts = [xn_tok, wtok] + ([ps_cons[("b", pslot)]]
                                            if ("b", pslot) in ps_cons else [])
                    if ("qkT", j) in cons:
                        wts.append(cons[("qkT", j)])
                    mm = None
                    for c in range(8):
                        mm = ctx.op("tensor", lambda e, j=j, nb=nb, c=c, sl=sl,
                                    o=ps_big[pslot]:
                                    e.matmul(o[:, 0:512],
                                             wqk[sl][:, j * 1024 + c * 128:
                                                     j * 1024 + (c + 1) * 128],
                                             xnT[:, c * 2048 + nb * 512:
                                                 c * 2048 + (nb + 1) * 512],
                                             start=(c == 0), stop=(c == 7)),
                                    waits=wts if c == 0 else (),
                                    sem=None if c < 7 else "dflt")
                    ev = ctx.op("scalar", lambda e, j=j, nb=nb, i=ps_big[pslot]:
                                e.activation(qkT[j][:, nb * 512:(nb + 1) * 512],
                                             i[:, 0:512], AF.Copy), waits=[mm])
                    ps_cons[("b", pslot)] = ev
                    pj_toks.append(ev)
            v_toks = []
            for t in range(NT):
                pslot = t % 2
                wts = [xn_tok, wtok] + ([ps_cons[("b", pslot)]]
                                        if ("b", pslot) in ps_cons else [])
                if ("vp", 0) in cons:
                    wts.append(cons[("vp", 0)])
                mm = None
                for c in range(8):
                    mm = ctx.op("tensor", lambda e, t=t, c=c, sl=sl,
                                o=ps_big[pslot]:
                                e.matmul(o[:, 512:768],
                                         xnT[:, c * 2048 + t * 128:
                                             c * 2048 + (t + 1) * 128],
                                         wqk[sl][:, 4096 + c * 256:
                                                 4096 + (c + 1) * 256],
                                         start=(c == 0), stop=(c == 7)),
                                waits=wts if c == 0 else (),
                                sem=None if c < 7 else "dflt")
                ev = ctx.op("vector", lambda e, t=t, sl=sl, i=ps_big[pslot]:
                            e.tensor_copy(v_p[0][:, t * 256:(t + 1) * 256],
                                          i[:, 512:768]), waits=[mm])
                ps_cons[("b", pslot)] = ev
                v_toks.append(ev)
            for h01 in range(2):
                h = 2 * p + h01
                hb = 64 * h01
                for i in range(NT):
                    q2 = i % 2
                    zb_ = q2 * 16
                    s2d = i % 2
                    L = (i + 1) * 128
                    nkb = (L + 511) // 512
                    etoks = []
                    for kb in range(nkb):
                        klen = min(512, L - kb * 512)
                        pslot = kb % 2
                        wts = [pj_toks[-1]] + ([ps_cons[("b", pslot)]]
                                               if ("b", pslot) in ps_cons else [])
                        ewait = ([prev[("eL0", q2)]]
                                 if (("eL0", q2) in prev and kb == 0) else [])
                        mm1 = ctx.op("tensor", lambda e, i=i, kb=kb, hb=hb,
                                     klen=klen, o=ps_big[pslot]:
                                     e.matmul(o[:, 0:klen],
                                              qkT[0][hb:hb + 64, i * 128:(i + 1) * 128],
                                              qkT[2][hb:hb + 64,
                                                     kb * 512:kb * 512 + klen],
                                              start=True, stop=True),
                                     waits=wts, sem=None)
                        mm2 = ctx.op("tensor", lambda e, i=i, kb=kb, hb=hb,
                                     klen=klen, o=ps_big[pslot]:
                                     e.matmul(o[:, 512:512 + klen],
                                              qkT[1][hb:hb + 64, i * 128:(i + 1) * 128],
                                              qkT[3][hb:hb + 64,
                                                     kb * 512:kb * 512 + klen],
                                              start=True, stop=True),
                                     waits=(), sem="dflt")
                        if kb == nkb - 1:
                            tA = ctx.op("scalar", lambda e, klen=klen, s2d=s2d,
                                        i_=ps_big[pslot]:
                                        e.activation(ssb1[s2d][:, 0:klen],
                                                     i_[:, 0:klen],
                                                     AF.Identity, scale=0.125),
                                        waits=[mm2, prev.get(("ssb", s2d), mm2)]
                                        + ewait)
                            tB = ctx.op("scalar", lambda e, klen=klen, s2d=s2d,
                                        i_=ps_big[pslot]:
                                        e.activation(ssb2[s2d][:, 0:klen],
                                                     i_[:, 512:512 + klen],
                                                     AF.Identity, scale=0.125),
                                        waits=[tA])
                            ps_cons[("b", pslot)] = tB
                            tC = ctx.op("vector", lambda e, klen=klen, s2d=s2d:
                                        e.tensor_add(ssb1[s2d][:, klen - 128:klen],
                                                     ssb1[s2d][:, klen - 128:klen],
                                                     aux.mask[:]),
                                        waits=[tA, aux.ld_tok])
                            tD = ctx.op("vector", lambda e, klen=klen, s2d=s2d:
                                        e.tensor_add(ssb2[s2d][:, klen - 128:klen],
                                                     ssb2[s2d][:, klen - 128:klen],
                                                     aux.mask[:]), waits=[tB])
                            tE = ctx.op("scalar", lambda e, kb=kb, klen=klen, q2=q2,
                                        s2d=s2d, zb_=zb_:
                                        e.activation(e1b[q2][:, kb * 512:
                                                             kb * 512 + klen],
                                                     ssb1[s2d][:, 0:klen], AF.Exp,
                                                     accum_out=zp[:, zb_ + kb:
                                                                  zb_ + kb + 1]),
                                        waits=[tC, tD])
                            tF = ctx.op("scalar", lambda e, kb=kb, klen=klen, q2=q2,
                                        s2d=s2d, zb_=zb_:
                                        e.activation(e2b[q2][:, kb * 512:
                                                             kb * 512 + klen],
                                                     ssb2[s2d][:, 0:klen], AF.Exp,
                                                     accum_out=zp[:, zb_ + 4 + kb:
                                                                  zb_ + 5 + kb]),
                                        waits=[tD])
                            prev[("ssb", s2d)] = tF
                            etoks += [tE, tF]
                        else:
                            tE = ctx.op("scalar", lambda e, kb=kb, q2=q2, zb_=zb_,
                                        i_=ps_big[pslot]:
                                        e.activation(e1b[q2][:, kb * 512:
                                                             (kb + 1) * 512],
                                                     i_[:, 0:512], AF.Exp,
                                                     scale=0.125,
                                                     accum_out=zp[:, zb_ + kb:
                                                                  zb_ + kb + 1]),
                                        waits=[mm2] + ewait)
                            tF = ctx.op("scalar", lambda e, kb=kb, q2=q2, zb_=zb_,
                                        i_=ps_big[pslot]:
                                        e.activation(e2b[q2][:, kb * 512:
                                                             (kb + 1) * 512],
                                                     i_[:, 512:1024], AF.Exp,
                                                     scale=0.125,
                                                     accum_out=zp[:, zb_ + 4 + kb:
                                                                  zb_ + 5 + kb]),
                                        waits=[mm2])
                            ps_cons[("b", pslot)] = tF
                            etoks += [tE, tF]
                    tz = ctx.op("vector", lambda e, nkb=nkb, zb_=zb_:
                                e.reduce_sum(zp[:, zb_ + 8:zb_ + 9],
                                             zp[:, zb_:zb_ + nkb], axis=AX.X),
                                waits=etoks)
                    tz2 = ctx.op("vector", lambda e, nkb=nkb, zb_=zb_:
                                 e.reduce_sum(zp[:, zb_ + 9:zb_ + 10],
                                              zp[:, zb_ + 4:zb_ + 4 + nkb],
                                              axis=AX.X), waits=etoks)
                    tr1 = ctx.op("vector", lambda e, zb_=zb_:
                                 e.reciprocal(zp[:, zb_ + 10:zb_ + 11],
                                              zp[:, zb_ + 8:zb_ + 9]), waits=[tz2])
                    tr2 = ctx.op("vector", lambda e, zb_=zb_:
                                 e.reciprocal(zp[:, zb_ + 11:zb_ + 12],
                                              zp[:, zb_ + 9:zb_ + 10]), waits=[tz2])
                    tns = ctx.op("vector", lambda e, h=h, zb_=zb_:
                                 e.tensor_scalar(zp[:, zb_ + 12:zb_ + 13],
                                                 zp[:, zb_ + 11:zb_ + 12],
                                                 lam_bc[:, h:h + 1], -1.0,
                                                 ALU.mult, ALU.mult),
                                 waits=[tr2, lam_tok])
                    ta = ctx.op("vector", lambda e, L=L, q2=q2, zb_=zb_:
                                e.tensor_scalar(attb[:, 0:L], e1b[q2][:, 0:L],
                                                zp[:, zb_ + 10:zb_ + 11], None,
                                                ALU.mult),
                                waits=[tns])
                    tc = ctx.op("vector", lambda e, L=L, q2=q2, zb_=zb_:
                                e.scalar_tensor_tensor(attc[q2][:, 0:L],
                                                       e2b[q2][:, 0:L],
                                                       zp[:, zb_ + 12:zb_ + 13],
                                                       attb[:, 0:L],
                                                       ALU.mult, ALU.add),
                                waits=[ta] + ([prev[("attcL0", q2)]]
                                              if ("attcL0", q2) in prev else []))
                    prev[("eL0", q2)] = tc
                    ev_toks = []
                    for jg in range((i + 4) // 4):
                        ntile = min(4, i + 1 - jg * 4)
                        tslot = jg % 2
                        wts = [tc] + ([prev[("atrL0", tslot)]]
                                      if ("atrL0", tslot) in prev else [])
                        if ("tr", tslot) in trd:
                            wts.append(trd[("tr", tslot)])
                        ttr = None
                        for jj in range(ntile):
                            j = jg * 4 + jj
                            ttr = ctx.op("tensor", lambda e, j=j, jj=jj, q2=q2,
                                         tslot=tslot:
                                         e.transpose(ps_tr[tslot][:, jj * 128:
                                                                  (jj + 1) * 128],
                                                     attc[q2][:, j * 128:
                                                              (j + 1) * 128],
                                                     aux.idbf[:]),
                                         waits=wts if jj == 0 else (),
                                         sem=None if jj < ntile - 1 else "dflt")
                        wat = ([prev[("avmm", s2d)]]
                               if (("avmm", s2d) in prev and jg == 0) else [])
                        tke = ctx.op("vector", lambda e, jg=jg, ntile=ntile, s2d=s2d,
                                     tslot=tslot:
                                     e.tensor_copy(attT[s2d][:, jg * 512:
                                                            jg * 512 + ntile * 128],
                                                   ps_tr[tslot][:, 0:ntile * 128]),
                                     waits=[ttr] + wat)
                        trd[("tr", tslot)] = tke
                        prev[("atrL0", tslot)] = tke
                        ev_toks.append(tke)
                    prev[("attcL0", q2)] = ttr
                    yslot = s2d
                    wts = ev_toks + [v_toks[-1]] + ([ps_cons[("y", yslot)]]
                                                    if ("y", yslot) in ps_cons else [])
                    mm = None
                    for j in range(i + 1):
                        mm = ctx.op("tensor", lambda e, j=j, h01=h01, sl=sl, s2d=s2d,
                                    yslot=yslot:
                                    e.matmul(ps_y[yslot][:, 0:128],
                                             attT[s2d][:, j * 128:(j + 1) * 128],
                                             v_p[0][:, j * 256 + h01 * 128:
                                                     j * 256 + h01 * 128 + 128],
                                             start=(j == 0), stop=(j == i)),
                                    waits=wts if j == 0 else (),
                                    sem=None if j < i else "dflt")
                    prev[("avmm", s2d)] = mm
                    wgn = [prev[("gn", yslot)]] if ("gn", yslot) in prev else []
                    g0 = ctx.op("vector", lambda e, yslot=yslot, zb_=zb_:
                                e.reduce_sum(zp[:, zb_ + 13:zb_ + 14],
                                             ps_y[yslot][:, 0:128], axis=AX.X),
                                waits=[mm] + wgn)
                    g1 = ctx.op("vector", lambda e, zb_=zb_:
                                e.tensor_scalar_mul(zp[:, zb_ + 14:zb_ + 15],
                                                    zp[:, zb_ + 13:zb_ + 14],
                                                    -1.0 / 128), waits=[g0])
                    g2 = ctx.op("scalar", lambda e, yslot=yslot, zb_=zb_:
                                e.activation(gn[:], ps_y[yslot][:, 0:128], AF.Square,
                                             bias=zp[:, zb_ + 14:zb_ + 15],
                                             accum_out=zp[:, zb_ + 13:zb_ + 14]),
                                waits=[g1])
                    g3 = ctx.op("scalar", lambda e, zb_=zb_:
                                e.activation(zp[:, zb_ + 15:zb_ + 16],
                                             zp[:, zb_ + 13:zb_ + 14], AF.Ln,
                                             scale=1.0 / 128, bias=EPS), waits=[g2])
                    g4 = ctx.op("scalar", lambda e, zb_=zb_:
                                e.activation(zp[:, zb_ + 13:zb_ + 14],
                                             zp[:, zb_ + 15:zb_ + 16], AF.Exp,
                                             scale=-0.5, bias=LNL), waits=[g3])
                    g6 = ctx.op("vector", lambda e, zb_=zb_:
                                e.tensor_mul(zp[:, zb_ + 14:zb_ + 15],
                                             zp[:, zb_ + 14:zb_ + 15],
                                             zp[:, zb_ + 13:zb_ + 14]), waits=[g4])
                    wyb = [prev[("ylnbf", yslot)]] if ("ylnbf", yslot) in prev else []
                    g7 = ctx.op("scalar", lambda e, yslot=yslot, zb_=zb_:
                                e.activation(ylnbf[yslot][:], ps_y[yslot][:, 0:128],
                                             AF.Identity,
                                             scale=zp[:, zb_ + 13:zb_ + 14],
                                             bias=zp[:, zb_ + 14:zb_ + 15]),
                                waits=[g6] + wyb)
                    prev[("gn", yslot)] = g7
                    ps_cons[("y", yslot)] = g7
                    tslot2 = s2d
                    wts = [g7] + ([prev[("ytr", tslot2)]]
                                  if ("ytr", tslot2) in prev else [])
                    if ("tr", tslot2) in trd:
                        wts.append(trd[("tr", tslot2)])
                    ty = ctx.op("tensor", lambda e, yslot=yslot, tslot2=tslot2:
                                e.transpose(ps_tr[tslot2][:, 0:128], ylnbf[yslot][:],
                                            aux.idbf[:]), waits=wts)
                    tye = ctx.op("vector", lambda e, h=h, i=i, tslot2=tslot2:
                                 e.tensor_copy(ylnT[:, h * 2048 + i * 128:
                                                   h * 2048 + (i + 1) * 128],
                                               ps_tr[tslot2][:, 0:128]), waits=[ty])
                    trd[("tr", tslot2)] = tye
                    prev[("ytr", tslot2)] = tye
                    prev[("yln",)] = tye
                    pair_mm_last = mm
            cons[("w", sl)] = pair_mm_last
            cons[("vp", 0)] = pair_mm_last
            for j in range(4):
                cons[("qkT", j)] = pair_mm_last
        wc_tok = ctx.dma("gpsimd", wc_sb[:].rearrange("p (c m) -> p c m", c=16),
                         W["L0_attn_Wc"].rearrange("(c p) m -> p c m", p=128),
                         waits=[pair_mm_last, prev[("yln",)]], sem=wcld)
        out_mm_last = None
        for i in range(NT):
            oslot = i % 2
            wts = [prev[("yln",)], wc_tok] + ([ps_cons[("o", oslot)]]
                                              if ("o", oslot) in ps_cons else [])
            mm = None
            for db in range(2):
                for hch in range(16):
                    mm = ctx.op("tensor", lambda e, i=i, db=db, hch=hch,
                                o=ps_big[oslot]:
                                e.matmul(o[:, db * 512:(db + 1) * 512],
                                         ylnT[:, hch * 2048 + i * 128:
                                             hch * 2048 + (i + 1) * 128],
                                         wc_sb[:, hch * 1024 + db * 512:
                                             hch * 1024 + (db + 1) * 512],
                                         start=(hch == 0), stop=(hch == 15)),
                                waits=wts if (db == 0 and hch == 0) else (),
                                sem=None if not (db == 1 and hch == 15) else "dflt")
            out_mm_last = mm
            st, tk_ps = sandwich_finish(ctx, aux, swb, ps_big[oslot][:], mm,
                                        x_in, x_out, row, i, prev)
            ps_cons[("o", oslot)] = tk_ps
            ps_cons[("b", oslot)] = tk_ps
        cons["wc"] = out_mm_last
        cons["xnT"] = out_mm_last
        ctx.barrier()


# version 07 (final)
# speedup vs baseline: 1.1862x; 1.0115x over previous
import math
import numpy as np
import ml_dtypes
import concourse.bass as bass
import concourse.mybir as mybir
from concourse.bass_utils import run_bass_kernel_spmd

F32 = mybir.dt.float32
BF16 = mybir.dt.bfloat16
AF = mybir.ActivationFunctionType
ALU = mybir.AluOpType
AX = mybir.AxisListType

R = 2
T = 2048
D = 1024
NT = T // 128
EPS = 1e-5
LAMBDA_INIT = 0.8 - 0.6 * math.exp(-0.3)
NEG = -1.0e30
GA = 0.044715
SGA = math.sqrt(GA)
TC = math.sqrt(2.0 / math.pi)
LNL = math.log(1.0 - LAMBDA_INIT)


class Sem:
    __slots__ = ("h", "n")

    def __init__(self, h):
        self.h = h
        self.n = 0

    def tok(self):
        return (self.h, self.n)


class Ctx:
    ENGINES = ("tensor", "vector", "scalar", "gpsimd", "sync")

    def __init__(self, nc):
        self.nc = nc
        self.streams = {k: [] for k in self.ENGINES}
        self._nsem = 0
        self.all_sems = []
        self.esem = {k: self.new_sem() for k in self.ENGINES}

    def new_sem(self):
        self._nsem += 1
        s = Sem(self.nc.alloc_semaphore(name=f"s{self._nsem}"))
        self.all_sems.append(s)
        return s

    serial = False

    def op(self, eng, fn, waits=(), sem="dflt", amt=1):
        if Ctx.serial and fn is not None:
            waits = list(waits) + [s.tok() for s in self.all_sems if s.n > 0]
        if sem == "dflt":
            s, amt = self.esem[eng], 1
        else:
            s = sem
        tok = None
        if s is not None:
            s.n += amt
            tok = (s.h, s.n)
        self.streams[eng].append((tuple(waits), fn, s.h if s else None, amt))
        return tok

    def dma(self, eng, out, in_, waits=(), sem=None, **kw):
        assert sem is not None
        return self.op(eng, lambda e: e.dma_start(out, in_, **kw), waits=waits,
                       sem=sem, amt=16)

    def barrier(self):
        toks = [s.tok() for s in self.all_sems if s.n > 0]
        for e in self.ENGINES:
            self.op(e, None, waits=toks, sem=None)
        return toks

    def finalize(self, block):
        def runner(name):
            stream = self.streams[name]

            def body(eng):
                waited = {}
                for waits, fn, semh, amt in stream:
                    for (h, v) in waits:
                        if waited.get(h, -1) < v:
                            eng.wait_ge(h, v)
                            waited[h] = v
                    if fn is not None:
                        inst = fn(eng)
                        if semh is not None:
                            inst.then_inc(semh, amt)
                            waited[semh] = max(waited.get(semh, -1), 0)
            return body

        block.tensor(runner("tensor"))
        block.vector(runner("vector"))
        block.scalar(runner("scalar"))
        block.gpsimd(runner("gpsimd"))
        block.sync(runner("sync"))


def _host_aux():
    idbf = np.eye(128, dtype=ml_dtypes.bfloat16)
    idf32 = np.eye(128, dtype=np.float32)
    ones_bf = np.ones((128, 128), dtype=ml_dtypes.bfloat16)
    ones_f32 = np.ones((128, 128), dtype=np.float32)
    mask = np.where(np.arange(128)[None, :] <= np.arange(128)[:, None],
                    np.float32(0.0), np.float32(NEG)).astype(np.float32)
    return {"aux_idbf": idbf, "aux_idf32": idf32, "aux_onesbf": ones_bf,
            "aux_onesf32": ones_f32, "aux_mask": mask}


CONSTS = [1e-5, 1.0 / 1024, -1.0 / 1024, -0.5, SGA, TC, 0.5, 0.125, 0.0625,
          1.0 / 128, -1.0 / 128, 1.0 - LAMBDA_INIT, -1.0, 1.0, 2.0,
          1.0 / 512, -1.0 / 512, LNL]


class Aux:

    def __init__(self, nc, ctx, stack):
        self.nc = nc
        self.ctx = ctx
        ap = lambda n, s, d: nc.dram_tensor(n, s, d, kind="ExternalInput").ap()
        self.idbf_h = ap("aux_idbf", [128, 128], BF16)
        self.idf32_h = ap("aux_idf32", [128, 128], F32)
        self.onesbf_h = ap("aux_onesbf", [128, 128], BF16)
        self.onesf32_h = ap("aux_onesf32", [128, 128], F32)
        self.mask_h = ap("aux_mask", [128, 128], F32)
        sb = lambda n, s, d: stack.enter_context(nc.sbuf_tensor(n, s, d))
        self.idbf = sb("idbf", [128, 128], BF16)
        self.idf32 = sb("idf32", [128, 128], F32)
        self.onesbf = sb("onesbf", [128, 128], BF16)
        self.onesf32 = sb("onesf32", [128, 128], F32)
        self.mask = sb("mask", [128, 128], F32)
        self.ld = ctx.new_sem()
        for dst, src in ((self.idbf, self.idbf_h), (self.idf32, self.idf32_h),
                         (self.onesbf, self.onesbf_h), (self.onesf32, self.onesf32_h),
                         (self.mask, self.mask_h)):
            ctx.dma("sync", dst[:], src, sem=self.ld)
        self.ld_tok = self.ld.tok()
        self.cpool = sb("cpool", [128, len(CONSTS)], F32)
        for j, v in enumerate(CONSTS):
            ctx.op("vector", lambda e, j=j, v=v:
                   e.memset(self.cpool[:, j:j + 1], float(v)))
            nc.const_aps.register(F32, float(v), self.cpool[:, j:j + 1])


def ln_to_xnT(ctx, aux, bufs, x_hbm, row, tiles, xnT, tcols, evac_waits=()):
    (xt, stats, sq_scr, xn_bf, ps_tr, xld, prev) = bufs
    xv = xnT[:].rearrange("p (c j) -> p c j", c=8)
    t0 = tiles[0]
    last = None
    for t in tiles:
        s = t % 2
        w = [prev[("xt", s)]] if ("xt", s) in prev else []
        lx = ctx.dma("sync", xt[:, s * 1024:(s + 1) * 1024],
                     x_hbm[row * T + t * 128: row * T + (t + 1) * 128, :],
                     waits=w, sem=xld[s])
        c0, c1, c2, c3, c4, c5 = [stats[:, s * 8 + i:s * 8 + i + 1] for i in range(6)]
        w_st = [prev[("st", s)]] if ("st", s) in prev else []
        tk = ctx.op("vector", lambda e, a=xt[:, s * 1024:(s + 1) * 1024], o=c0:
                    e.reduce_sum(o, a, axis=AX.X), waits=[lx] + w_st)
        tk = ctx.op("vector", lambda e, i=c0, o=c1:
                    e.tensor_scalar_mul(o, i, -1.0 / D), waits=[tk])
        tk = ctx.op("scalar", lambda e, a=xt[:, s * 1024:(s + 1) * 1024], o=sq_scr,
                    b=c1, acc=c2:
                    e.activation(o[:], a, AF.Square, bias=b, accum_out=acc),
                    waits=[tk])
        tk = ctx.op("scalar", lambda e, i=c2, o=c3:
                    e.activation(o, i, AF.Ln, scale=1.0 / D, bias=EPS), waits=[tk])
        tk = ctx.op("scalar", lambda e, i=c3, o=c4:
                    e.activation(o, i, AF.Exp, scale=-0.5), waits=[tk])
        tk = ctx.op("vector", lambda e, a=c1, b=c4, o=c5:
                    e.tensor_mul(o, a, b), waits=[tk])
        tkc = ctx.op("scalar", lambda e, a=xt[:, s * 1024:(s + 1) * 1024],
                     o=xn_bf[:, s * 1024:(s + 1) * 1024], sc=c4, b=c5:
                     e.activation(o, a, AF.Identity, scale=sc, bias=b), waits=[tk])
        prev[("xt", s)] = tkc
        prev[("st", s)] = tkc
        for g in range(2):
            ttr = None
            for cc in range(4):
                c = g * 4 + cc
                w2 = [tkc, aux.ld_tok]
                if ("tr", g) in prev:
                    w2.append(prev[("tr", g)])
                ttr = ctx.op("tensor", lambda e,
                             i=xn_bf[:, s * 1024 + c * 128:s * 1024 + (c + 1) * 128],
                             o=ps_tr[g][:, cc * 128:(cc + 1) * 128]:
                             e.transpose(o, i, aux.idbf[:]),
                             waits=(w2 if cc == 0 else ()),
                             sem=None if cc < 3 else "dflt")
            dst = xv[:, g * 4:(g + 1) * 4, (t - t0) * 128:(t - t0 + 1) * 128]
            tke = ctx.op("vector", lambda e, i=ps_tr[g][:, 0:512],
                         o=dst: e.tensor_copy(o, i),
                         waits=[ttr] + list(evac_waits))
            prev[("tr", g)] = tke
            last = tke
    return last


def sandwich_finish(ctx, aux, bufs, ps_ap, ps_tok, x_hbm_in, x_hbm_out, row, t,
                    prev):
    (xt2, stats2, sq_scr2, lnout, xld2) = bufs
    s = t % 2
    w = [prev[("xt2", s)]] if ("xt2", s) in prev else []
    lx = ctx.dma("sync", xt2[:, s * 1024:(s + 1) * 1024],
                 x_hbm_in[row * T + t * 128: row * T + (t + 1) * 128, :],
                 waits=w, sem=xld2[s])
    c0, c1, c2, c3, c4, c5 = [stats2[:, s * 8 + i:s * 8 + i + 1] for i in range(6)]
    w_st = [prev[("st2", s)]] if ("st2", s) in prev else []
    tk = ctx.op("vector", lambda e, a=ps_ap, o=c0:
                e.reduce_sum(o, a, axis=AX.X), waits=[ps_tok] + w_st)
    tk = ctx.op("vector", lambda e, i=c0, o=c1:
                e.tensor_scalar_mul(o, i, -1.0 / D), waits=[tk])
    tk = ctx.op("scalar", lambda e, a=ps_ap, o=sq_scr2[:],
                b=c1, acc=c2:
                e.activation(o, a, AF.Square, bias=b, accum_out=acc), waits=[tk])
    tk = ctx.op("scalar", lambda e, i=c2, o=c3:
                e.activation(o, i, AF.Ln, scale=1.0 / D, bias=EPS), waits=[tk])
    tk = ctx.op("scalar", lambda e, i=c3, o=c4:
                e.activation(o, i, AF.Exp, scale=-0.5), waits=[tk])
    tk = ctx.op("vector", lambda e, a=c1, b=c4, o=c5:
                e.tensor_mul(o, a, b), waits=[tk])
    wlno = [prev[("lno",)]] if ("lno",) in prev else []
    tk_ps = ctx.op("scalar", lambda e, a=ps_ap, o=lnout[:],
                   sc=c4, b=c5:
                   e.activation(o, a, AF.Identity, scale=sc, bias=b),
                   waits=[tk] + wlno)
    tk2 = ctx.op("vector", lambda e, a=xt2[:, s * 1024:(s + 1) * 1024],
                 b=lnout[:],
                 o=xt2[:, s * 1024:(s + 1) * 1024]:
                 e.tensor_add(o, a, b), waits=[lx, tk_ps])
    prev[("lno",)] = tk2
    st = ctx.dma("sync", x_hbm_out[row * T + t * 128: row * T + (t + 1) * 128, :],
                 xt2[:, s * 1024:(s + 1) * 1024], waits=[tk2], sem=xld2[s])
    prev[("xt2", s)] = st
    prev[("st2", s)] = tk_ps
    return st, tk_ps


def mk_ln_bufs(nc, ctx, stack, tag, ps_tr):
    xt = stack.enter_context(nc.sbuf_tensor(f"xt_{tag}", [128, 2048], F32))
    stats = stack.enter_context(nc.sbuf_tensor(f"st_{tag}", [128, 16], F32))
    sq = stack.enter_context(nc.sbuf_tensor(f"sq_{tag}", [128, 1024], F32))
    xn = stack.enter_context(nc.sbuf_tensor(f"xn_{tag}", [128, 2048], BF16))
    xld = [ctx.new_sem(), ctx.new_sem()]
    return (xt, stats, sq, xn, ps_tr, xld, {})


def mk_sw_bufs(nc, ctx, stack, tag):
    xt2 = stack.enter_context(nc.sbuf_tensor(f"xt2_{tag}", [128, 2048], F32))
    stats2 = stack.enter_context(nc.sbuf_tensor(f"st2_{tag}", [128, 16], F32))
    sq2 = stack.enter_context(nc.sbuf_tensor(f"sq2_{tag}", [128, 1024], F32))
    lnout = stack.enter_context(nc.sbuf_tensor(f"lno_{tag}", [128, 1024], F32))
    xld2 = [ctx.new_sem(), ctx.new_sem()]
    return (xt2, stats2, sq2, lnout, xld2)


def build_l0ffn(nc, ctx, aux, stack, W, x_in, x_out, rows=range(R)):
    u1 = stack.enter_context(nc.sbuf_tensor("u1", [128, 8 * 256], BF16))
    v1 = stack.enter_context(nc.sbuf_tensor("v1", [128, 2 * 4096], BF16))
    u2 = stack.enter_context(nc.sbuf_tensor("u2", [128, 32 * 256], BF16))
    v2 = stack.enter_context(nc.sbuf_tensor("v2", [128, 2 * 1024], BF16))
    b1r = stack.enter_context(nc.sbuf_tensor("b1r", [32, 128], F32))
    b1c = stack.enter_context(nc.sbuf_tensor("b1c", [128, 32], F32))
    b2row = stack.enter_context(nc.sbuf_tensor("b2row", [1, 1024], BF16))
    wld = ctx.new_sem()
    ctx.dma("gpsimd", u1[:].rearrange("p (c m) -> p c m", c=8),
            W["L0_ffn_U1"].rearrange("(c p) m -> p c m", p=128), sem=wld)
    ctx.dma("gpsimd", v1[:].rearrange("p (c m) -> p c m", c=2),
            W["L0_ffn_V1"].rearrange("(c p) m -> p c m", p=128), sem=wld)
    ctx.dma("gpsimd", u2[:].rearrange("p (c m) -> p c m", c=32),
            W["L0_ffn_U2"].rearrange("(c p) m -> p c m", p=128), sem=wld)
    ctx.dma("gpsimd", v2[:].rearrange("p (c m) -> p c m", c=2),
            W["L0_ffn_V2"].rearrange("(c p) m -> p c m", p=128), sem=wld)
    ctx.dma("gpsimd", b1r[:], W["L0_ffn_b1"].rearrange("(c m) -> c m", m=128),
            sem=wld)
    ctx.dma("gpsimd", b2row[:], W["L0_ffn_b2"].rearrange("(c m) -> c m", c=1),
            sem=wld)
    wtok = wld.tok()

    ps_tr = [stack.enter_context(nc.psum_tensor(f"pstr_f0_{i}", [128, 512], BF16))
             for i in range(2)]
    ps = [stack.enter_context(nc.psum_tensor(f"psf0_{i}", [128, 512], F32))
          for i in range(2)]
    ps_o = [stack.enter_context(nc.psum_tensor(f"psof0_{i}", [128, 1024], F32))
            for i in range(2)]
    t1 = ctx.op("tensor", lambda e: e.transpose(ps[0][:, 0:32], b1r[:],
                                                aux.idf32[0:32, 0:32]),
                waits=[wtok, aux.ld_tok])
    t1 = ctx.op("vector", lambda e: e.tensor_copy(b1c[:], ps[0][:, 0:32]),
                waits=[t1])
    ps_cons = {0: t1}
    pso_cons = {}
    cons = {}
    prev = {}

    lnb = mk_ln_bufs(nc, ctx, stack, "f0", ps_tr)
    swb = mk_sw_bufs(nc, ctx, stack, "f0")
    xnT = stack.enter_context(nc.sbuf_tensor("xnT_f0", [128, 8 * 512], BF16))
    pT = stack.enter_context(nc.sbuf_tensor("pT_f0", [128, 2 * 512], BF16))
    hT = stack.enter_context(nc.sbuf_tensor("hT_f0", [128, 32 * 512], BF16))
    qT = stack.enter_context(nc.sbuf_tensor("qT_f0", [128, 2 * 512], BF16))
    zb = stack.enter_context(nc.sbuf_tensor("zb_f0", [128, 4 * 512], F32))
    sb_ = stack.enter_context(nc.sbuf_tensor("sb_f0", [128, 4 * 512], F32))
    ib = stack.enter_context(nc.sbuf_tensor("ib_f0", [128, 4 * 512], F32))
    tb = stack.enter_context(nc.sbuf_tensor("tb_f0", [128, 4 * 512], F32))

    for row in rows:
        for g in range(4):
            tiles = range(g * 4, (g + 1) * 4)
            ew = [cons["xnT"]] if "xnT" in cons else []
            xn_tok = ln_to_xnT(ctx, aux, lnb, x_in, row, tiles, xnT, 512,
                               evac_waits=ew)
            ptoks = []
            for rc in range(2):
                pslot = rc
                wts = [xn_tok, wtok] + ([ps_cons[pslot]] if pslot in ps_cons else [])
                if "pT" in cons:
                    wts.append(cons["pT"])
                mm = None
                for c in range(8):
                    mm = ctx.op("tensor", lambda e, c=c, rc=rc, o=ps[pslot]:
                                e.matmul(o[:], u1[:, c * 256 + rc * 128:
                                                 c * 256 + rc * 128 + 128],
                                         xnT[:, c * 512:(c + 1) * 512],
                                         start=(c == 0), stop=(c == 7)),
                                waits=wts if c == 0 else (),
                                sem=None if c < 7 else "dflt")
                ev = ctx.op("scalar", lambda e, rc=rc, i=ps[pslot]:
                            e.activation(pT[:, rc * 512:(rc + 1) * 512], i[:],
                                         AF.Copy), waits=[mm])
                ps_cons[pslot] = ev
                ptoks.append(ev)
            h_mm_last = None
            htoks = []
            for f in range(32):
                pslot = f % 2
                fs = f % 4
                wts = ptoks + ([ps_cons[pslot]] if pslot in ps_cons else [])
                if f < 2 and "hT" in cons:
                    wts.append(cons["hT"])
                mm = None
                for rc in range(2):
                    mm = ctx.op("tensor", lambda e, f=f, rc=rc, o=ps[pslot]:
                                e.matmul(o[:], v1[:, rc * 4096 + f * 128:
                                                 rc * 4096 + (f + 1) * 128],
                                         pT[:, rc * 512:(rc + 1) * 512],
                                         start=(rc == 0), stop=(rc == 1)),
                                waits=wts if rc == 0 else (),
                                sem=None if rc < 1 else "dflt")
                h_mm_last = mm
                wz = [mm] + ([prev[("z", fs)]] if ("z", fs) in prev else [])
                tz = ctx.op("scalar", lambda e, f=f, fs=fs, i=ps[pslot]:
                            e.activation(zb[:, fs * 512:(fs + 1) * 512], i[:],
                                         AF.Identity, bias=b1c[:, f:f + 1]), waits=wz)
                ps_cons[pslot] = tz
                ts = ctx.op("scalar", lambda e, fs=fs:
                            e.activation(sb_[:, fs * 512:(fs + 1) * 512],
                                         zb[:, fs * 512:(fs + 1) * 512],
                                         AF.Square, scale=SGA), waits=[tz])
                ti = ctx.op("vector", lambda e, fs=fs:
                            e.scalar_tensor_tensor(ib[:, fs * 512:(fs + 1) * 512],
                                                   sb_[:, fs * 512:(fs + 1) * 512],
                                                   1.0,
                                                   zb[:, fs * 512:(fs + 1) * 512],
                                                   ALU.add, ALU.mult), waits=[ts])
                tt = ctx.op("scalar", lambda e, fs=fs:
                            e.activation(tb[:, fs * 512:(fs + 1) * 512],
                                         ib[:, fs * 512:(fs + 1) * 512],
                                         AF.Tanh, scale=TC), waits=[ti])
                tp = ctx.op("vector", lambda e, fs=fs:
                            e.tensor_scalar(ib[:, fs * 512:(fs + 1) * 512],
                                            tb[:, fs * 512:(fs + 1) * 512],
                                            0.5, 0.5, ALU.mult, ALU.add), waits=[tt])
                th = ctx.op("vector", lambda e, f=f, fs=fs:
                            e.tensor_mul(hT[:, f * 512:(f + 1) * 512],
                                         ib[:, fs * 512:(fs + 1) * 512],
                                         zb[:, fs * 512:(fs + 1) * 512]), waits=[tp])
                prev[("z", fs)] = th
                htoks.append(th)
            cons["pT"] = h_mm_last
            qtoks = []
            q_mm_last = None
            for rc in range(2):
                pslot = rc
                wts = htoks + ([ps_cons[pslot]] if pslot in ps_cons else [])
                if "qT" in cons:
                    wts.append(cons["qT"])
                mm = None
                for f in range(32):
                    mm = ctx.op("tensor", lambda e, f=f, rc=rc, o=ps[pslot]:
                                e.matmul(o[:], u2[:, f * 256 + rc * 128:
                                                 f * 256 + rc * 128 + 128],
                                         hT[:, f * 512:(f + 1) * 512],
                                         start=(f == 0), stop=(f == 31)),
                                waits=wts if f == 0 else (),
                                sem=None if f < 31 else "dflt")
                q_mm_last = mm
                ev = ctx.op("vector", lambda e, rc=rc, i=ps[pslot]:
                            e.tensor_copy(qT[:, rc * 512:(rc + 1) * 512], i[:]),
                            waits=[mm])
                ps_cons[pslot] = ev
                qtoks.append(ev)
            cons["hT"] = q_mm_last
            out_mm_last = None
            for tt_ in range(4):
                t = g * 4 + tt_
                oslot = tt_ % 2
                wts = qtoks + [wtok] + ([pso_cons[oslot]] if oslot in pso_cons else [])
                mm = None
                for db in range(2):
                    for rc in range(2):
                        mm = ctx.op("tensor", lambda e, tt_=tt_, db=db, rc=rc,
                                    o=ps_o[oslot]:
                                    e.matmul(o[:, db * 512:(db + 1) * 512],
                                             qT[:, rc * 512 + tt_ * 128:
                                                rc * 512 + tt_ * 128 + 128],
                                             v2[:, rc * 1024 + db * 512:
                                                rc * 1024 + (db + 1) * 512],
                                             start=(rc == 0), stop=False),
                                    waits=wts if (db == 0 and rc == 0) else (),
                                    sem=None)
                    mm = ctx.op("tensor", lambda e, db=db, o=ps_o[oslot]:
                                e.matmul(o[:, db * 512:(db + 1) * 512],
                                         aux.onesbf[0:1, :],
                                         b2row[0:1, db * 512:(db + 1) * 512],
                                         start=False, stop=True),
                                waits=(), sem=None if db < 1 else "dflt")
                out_mm_last = mm
                st, tk_ps = sandwich_finish(ctx, aux, swb, ps_o[oslot][:],
                                            mm, x_in, x_out, row, t, prev)
                pso_cons[oslot] = tk_ps
            cons["qT"] = out_mm_last
            cons["xnT"] = h_mm_last


def _winputs(nc, names):
    shp = dict(WSHAPES)
    return {n: nc.dram_tensor(n, list(shp[n]), F32, kind="ExternalInput").ap()
            for n in names}


WSHAPES = {
    "L0_attn_Wq1": (1024, 1024), "L0_attn_Wq2": (1024, 1024),
    "L0_attn_Wk1": (1024, 1024), "L0_attn_Wk2": (1024, 1024),
    "L0_attn_Wv": (1024, 2048), "L0_attn_Wc": (2048, 1024),
    "L0_attn_lq1": (16, 64), "L0_attn_lk1": (16, 64),
    "L0_attn_lq2": (16, 64), "L0_attn_lk2": (16, 64),
    "L0_ffn_U1": (1024, 256), "L0_ffn_V1": (256, 4096), "L0_ffn_b1": (4096,),
    "L0_ffn_U2": (4096, 256), "L0_ffn_V2": (256, 1024), "L0_ffn_b2": (1024,),
    "L1_attn_Wq": (1024, 256), "L1_attn_Wk": (1024, 256),
    "L1_attn_Wvdown": (1024, 256), "L1_attn_Wvup": (256, 1024),
    "L1_ffn_w1_w": (4096, 1024), "L1_ffn_w1_s": (4096,),
    "L1_ffn_w2_w": (4096, 1024), "L1_ffn_w2_s": (4096,),
    "L1_ffn_h1_w": (4096, 4096), "L1_ffn_h1_s": (4096,),
    "L1_ffn_h2_w": (4096, 4096), "L1_ffn_h2_s": (4096,),
    "L1_ffn_w3_w": (1024, 4096), "L1_ffn_w3_s": (1024,),
}

L0FFN_W = ["L0_ffn_U1", "L0_ffn_V1", "L0_ffn_b1", "L0_ffn_U2", "L0_ffn_V2",
           "L0_ffn_b2"]


def build_prog(stages):
    import contextlib
    nc = bass.Bass("TRN2", target_bir_lowering=False)
    x_in = nc.dram_tensor("x_in", [R * T, D], F32, kind="ExternalInput").ap()
    x_out = nc.dram_tensor("x_out", [R * T, D], F32, kind="ExternalOutput").ap()
    x_mid = [nc.dram_tensor(f"x_mid{i}", [R * T, D], F32, kind="Internal").ap()
             for i in range(max(0, len(stages) - 1))]
    wnames = sorted(set(n for _, ws in stages for n in ws))
    W = _winputs(nc, wnames)
    with contextlib.ExitStack() as stack:
        ctx = Ctx(nc)
        aux = Aux(nc, ctx, stack)
        for i, (builder, _) in enumerate(stages):
            xi = x_in if i == 0 else x_mid[i - 1]
            xo = x_out if i == len(stages) - 1 else x_mid[i]
            with contextlib.ExitStack() as sstack:
                builder(nc, ctx, aux, sstack, W, xi, xo)
                ctx.barrier()
        with nc.Block() as block:
            ctx.finalize(block)
    return nc, wnames


def _run(progs, inputs, x):
    aux_np = _host_aux()
    cur = [np.ascontiguousarray(x[2 * c:2 * c + 2].reshape(R * T, D))
           for c in range(8)]
    for nc, wnames in progs:
        in_maps = []
        for c in range(8):
            m = dict(aux_np)
            m["x_in"] = cur[c]
            for n in wnames:
                m[n] = np.ascontiguousarray(np.asarray(inputs[n], dtype=np.float32))
            in_maps.append(m)
        res = run_bass_kernel_spmd(nc, in_maps, core_ids=list(range(8)))
        cur = [res.results[c]["x_out"] for c in range(8)]
    return np.stack([cur[c].reshape(R, T, D) for c in range(8)])


def kernel(**inputs):
    x = np.ascontiguousarray(np.asarray(inputs["L0_attn_x"], dtype=np.float32))
    B = x.shape[0]
    progs = [build_prog([(build_l0attn, L0ATTN_W), (build_l0ffn, L0FFN_W),
                         (build_l1attn, L1ATTN_W), (build_l1ffn, L1FFN_W)])]
    out = _run(progs, inputs, x)
    return out.reshape(B, T, D)


L1ATTN_W = ["L1_attn_Wq", "L1_attn_Wk", "L1_attn_Wvdown", "L1_attn_Wvup"]


def build_l1attn(nc, ctx, aux, stack, W, x_in, x_out, rows=range(R)):
    wq = stack.enter_context(nc.sbuf_tensor("wq_a1", [128, 8 * 256], BF16))
    wk = stack.enter_context(nc.sbuf_tensor("wk_a1", [128, 8 * 256], BF16))
    wvd = stack.enter_context(nc.sbuf_tensor("wvd_a1", [128, 8 * 256], BF16))
    wvdT = stack.enter_context(nc.sbuf_tensor("wvdT_a1", [128, 2 * 1024], BF16))
    wvu = stack.enter_context(nc.sbuf_tensor("wvu_a1", [128, 2 * 1024], BF16))
    wveff = stack.enter_context(nc.sbuf_tensor("wveff_a1", [128, 8 * 1024], BF16))
    wld = ctx.new_sem()
    ctx.dma("gpsimd", wq[:].rearrange("p (c m) -> p c m", c=8),
            W["L1_attn_Wq"].rearrange("(c p) m -> p c m", p=128), sem=wld)
    ctx.dma("gpsimd", wk[:].rearrange("p (c m) -> p c m", c=8),
            W["L1_attn_Wk"].rearrange("(c p) m -> p c m", p=128), sem=wld)
    ctx.dma("gpsimd", wvd[:].rearrange("p (c m) -> p c m", c=8),
            W["L1_attn_Wvdown"].rearrange("(c p) m -> p c m", p=128), sem=wld)
    ctx.dma("gpsimd", wvu[:].rearrange("p (c m) -> p c m", c=2),
            W["L1_attn_Wvup"].rearrange("(c p) m -> p c m", p=128), sem=wld)
    wtok = wld.tok()

    ps_tr = [stack.enter_context(nc.psum_tensor(f"pstr_a1_{i}", [128, 512], BF16))
             for i in range(2)]
    ps = [stack.enter_context(nc.psum_tensor(f"ps_a1_{i}", [128, 512], F32))
          for i in range(2)]
    ps_o = [stack.enter_context(nc.psum_tensor(f"pso_a1_{i}", [128, 1024], F32))
            for i in range(2)]

    tr_toks = []
    for c in range(8):
        for rr in range(2):
            tt = ctx.op("tensor", lambda e, c=c, rr=rr:
                        e.transpose(ps_tr[rr][:, (c % 4) * 128:(c % 4 + 1) * 128],
                                    wvd[:, c * 256 + rr * 128:c * 256 + (rr + 1) * 128],
                                    aux.idbf[:]),
                        waits=[wtok, aux.ld_tok] + tr_toks[-2:],
                        sem=None if False else "dflt")
        if c % 4 == 3:
            for rr in range(2):
                tk = ctx.op("vector", lambda e, c=c, rr=rr:
                            e.tensor_copy(wvdT[:, rr * 1024 + (c - 3) * 128:
                                               rr * 1024 + (c + 1) * 128],
                                          ps_tr[rr][:, 0:512]),
                            waits=[ctx.esem["tensor"].tok()])
                tr_toks.append(tk)
    wvdT_tok = tr_toks[-1]
    ve_toks = []
    for ec in range(8):
        for dvb in range(2):
            pslot = dvb
            wts = [wvdT_tok, wtok] + ve_toks[-2:]
            mm = None
            for rr in range(2):
                mm = ctx.op("tensor", lambda e, ec=ec, dvb=dvb, rr=rr, o=ps[pslot]:
                            e.matmul(o[:], wvdT[:, rr * 1024 + ec * 128:
                                             rr * 1024 + (ec + 1) * 128],
                                     wvu[:, rr * 1024 + dvb * 512:
                                         rr * 1024 + (dvb + 1) * 512],
                                     start=(rr == 0), stop=(rr == 1)),
                            waits=wts if rr == 0 else (),
                            sem=None if rr < 1 else "dflt")
            ev = ctx.op("scalar", lambda e, ec=ec, dvb=dvb, i=ps[pslot]:
                        e.activation(wveff[:, ec * 1024 + dvb * 512:
                                        ec * 1024 + (dvb + 1) * 512], i[:],
                                     AF.Copy), waits=[mm])
            ve_toks.append(ev)
    ps_cons = {0: ve_toks[-2], 1: ve_toks[-1]}
    pso_cons = {}
    cons = {}
    prev = {}

    lnb = mk_ln_bufs(nc, ctx, stack, "a1", ps_tr)
    swb = mk_sw_bufs(nc, ctx, stack, "a1")
    xnT = stack.enter_context(nc.sbuf_tensor("xnT_a1", [128, 8 * 2048], BF16))
    qT = stack.enter_context(nc.sbuf_tensor("qT_a1", [128, 2 * 2048], BF16))
    kT = stack.enter_context(nc.sbuf_tensor("kT_a1", [128, 2 * 2048], BF16))
    v_sb = stack.enter_context(nc.sbuf_tensor("v_a1", [128, 16 * 1024], BF16))
    e_bf = stack.enter_context(nc.sbuf_tensor("e_a1", [128, 2048], BF16))
    att_bf = stack.enter_context(nc.sbuf_tensor("att_a1", [128, 2048], BF16))
    attT = stack.enter_context(nc.sbuf_tensor("attT_a1", [128, 2048], BF16))
    s_sb = stack.enter_context(nc.sbuf_tensor("s_a1", [128, 512], F32))
    zp = stack.enter_context(nc.sbuf_tensor("zp_a1", [128, 8], F32))

    for row in rows:
        ew = [cons["xnT"]] if "xnT" in cons else []
        if row == 0:
            lnb[6][("tr", 0)] = tr_toks[-2]
            lnb[6][("tr", 1)] = tr_toks[-1]
        xn_tok = ln_to_xnT(ctx, aux, lnb, x_in, row, range(NT), xnT, 2048,
                           evac_waits=ew)
        qk_toks = []
        for w_sb, oT in ((wq, qT), (wk, kT)):
            for c2 in range(2):
                for nb in range(4):
                    pslot = nb % 2
                    wts = [xn_tok, wtok] + ([ps_cons[pslot]] if pslot in ps_cons else [])
                    if "qT" in cons:
                        wts.append(cons["qT"])
                    mm = None
                    for c in range(8):
                        mm = ctx.op("tensor", lambda e, c=c, c2=c2, nb=nb,
                                    w_sb=w_sb, o=ps[pslot]:
                                    e.matmul(o[:], w_sb[:, c * 256 + c2 * 128:
                                                     c * 256 + c2 * 128 + 128],
                                             xnT[:, c * 2048 + nb * 512:
                                                 c * 2048 + (nb + 1) * 512],
                                             start=(c == 0), stop=(c == 7)),
                                    waits=wts if c == 0 else (),
                                    sem=None if c < 7 else "dflt")
                    ev = ctx.op("scalar", lambda e, c2=c2, nb=nb, oT=oT, i=ps[pslot]:
                                e.activation(oT[:, c2 * 2048 + nb * 512:
                                              c2 * 2048 + (nb + 1) * 512], i[:],
                                             AF.Copy), waits=[mm])
                    ps_cons[pslot] = ev
                    qk_toks.append(ev)
        v_toks = []
        for t in range(NT):
            for dvb in range(2):
                pslot = dvb
                wts = [xn_tok, ve_toks[-1]] + ([ps_cons[pslot]] if pslot in ps_cons else [])
                if "v_sb" in cons:
                    wts.append(cons["v_sb"])
                mm = None
                for c in range(8):
                    mm = ctx.op("tensor", lambda e, c=c, t=t, dvb=dvb, o=ps[pslot]:
                                e.matmul(o[:], xnT[:, c * 2048 + t * 128:
                                                 c * 2048 + (t + 1) * 128],
                                         wveff[:, c * 1024 + dvb * 512:
                                             c * 1024 + (dvb + 1) * 512],
                                         start=(c == 0), stop=(c == 7)),
                                waits=wts if c == 0 else (),
                                sem=None if c < 7 else "dflt")
                ev = ctx.op("vector", lambda e, t=t, dvb=dvb, i=ps[pslot]:
                            e.tensor_copy(v_sb[:, t * 1024 + dvb * 512:
                                             t * 1024 + (dvb + 1) * 512], i[:]),
                            waits=[mm])
                ps_cons[pslot] = ev
                v_toks.append(ev)
        cons["xnT"] = v_toks[-1]
        for i in range(NT):
            L = (i + 1) * 128
            nkb = (L + 511) // 512
            ztoks = []
            etoks = []
            for kb in range(nkb):
                klen = min(512, L - kb * 512)
                pslot = kb % 2
                wts = qk_toks[-1:] + ([ps_cons[pslot]] if pslot in ps_cons else [])
                ewait = [prev[("e",)]] if (("e",) in prev and kb == 0) else []
                mm = None
                for c2 in range(2):
                    mm = ctx.op("tensor", lambda e, i=i, kb=kb, c2=c2, klen=klen,
                                o=ps[pslot]:
                                e.matmul(o[:, 0:klen],
                                         qT[:, c2 * 2048 + i * 128:
                                            c2 * 2048 + (i + 1) * 128],
                                         kT[:, c2 * 2048 + kb * 512:
                                            c2 * 2048 + kb * 512 + klen],
                                         start=(c2 == 0), stop=(c2 == 1)),
                                waits=wts if c2 == 0 else (),
                                sem=None if c2 < 1 else "dflt")
                if kb == nkb - 1:
                    t1 = ctx.op("scalar", lambda e, klen=klen, i_=ps[pslot]:
                                e.activation(s_sb[:, 0:klen], i_[:, 0:klen],
                                             AF.Identity, scale=0.0625),
                                waits=[mm, prev.get("s_sb", mm)] + ewait)
                    ps_cons[pslot] = t1
                    t2 = ctx.op("vector", lambda e, klen=klen:
                                e.tensor_add(s_sb[:, klen - 128:klen],
                                             s_sb[:, klen - 128:klen],
                                             aux.mask[:]), waits=[t1, aux.ld_tok])
                    t3 = ctx.op("scalar", lambda e, kb=kb, klen=klen:
                                e.activation(e_bf[:, kb * 512:kb * 512 + klen],
                                             s_sb[:, 0:klen], AF.Exp,
                                             accum_out=zp[:, kb:kb + 1]),
                                waits=[t2])
                    prev["s_sb"] = t3
                    ztoks.append(t3)
                    etoks.append(t3)
                else:
                    t3 = ctx.op("scalar", lambda e, kb=kb, i_=ps[pslot]:
                                e.activation(e_bf[:, kb * 512:(kb + 1) * 512],
                                             i_[:], AF.Exp, scale=0.0625,
                                             accum_out=zp[:, kb:kb + 1]),
                                waits=[mm] + ewait)
                    ps_cons[pslot] = t3
                    ztoks.append(t3)
                    etoks.append(t3)
            tz = ctx.op("vector", lambda e, nkb=nkb:
                        e.reduce_sum(zp[:, 6:7], zp[:, 0:nkb], axis=AX.X),
                        waits=ztoks)
            tz = ctx.op("vector", lambda e: e.reciprocal(zp[:, 7:8], zp[:, 6:7]),
                        waits=[tz])
            ta = ctx.op("vector", lambda e, L=L:
                        e.tensor_scalar(att_bf[:, 0:L], e_bf[:, 0:L],
                                        zp[:, 7:8], None, ALU.mult),
                        waits=[tz] + etoks + ([prev[("att", 0)]] if ("att", 0) in prev else []))
            prev[("e",)] = ta
            ev_toks = []
            for jg in range((i + 1 + 3) // 4):
                ntile = min(4, i + 1 - jg * 4)
                tslot = jg % 2
                wts = [ta] + ([prev[("atr", tslot)]] if ("atr", tslot) in prev else [])
                if ("tr", tslot) in lnb[6]:
                    wts.append(lnb[6][("tr", tslot)])
                ttr = None
                for jj in range(ntile):
                    j = jg * 4 + jj
                    ttr = ctx.op("tensor", lambda e, j=j, jj=jj, tslot=tslot:
                                 e.transpose(ps_tr[tslot][:, jj * 128:(jj + 1) * 128],
                                             att_bf[:, j * 128:(j + 1) * 128],
                                             aux.idbf[:]),
                                 waits=wts if jj == 0 else (),
                                 sem=None if jj < ntile - 1 else "dflt")
                tke = ctx.op("vector", lambda e, jg=jg, ntile=ntile, tslot=tslot:
                             e.tensor_copy(attT[:, jg * 512:jg * 512 + ntile * 128],
                                           ps_tr[tslot][:, 0:ntile * 128]),
                             waits=[ttr])
                lnb[6][("tr", tslot)] = tke
                prev[("atr", tslot)] = tke
                ev_toks.append(tke)
            oslot = i % 2
            wts = ev_toks + v_toks[-1:] + ([pso_cons[oslot]] if oslot in pso_cons else [])
            mm = None
            for dvb in range(2):
                for j in range(i + 1):
                    mm = ctx.op("tensor", lambda e, j=j, i=i, dvb=dvb, o=ps_o[oslot]:
                                e.matmul(o[:, dvb * 512:(dvb + 1) * 512],
                                         attT[:, j * 128:(j + 1) * 128],
                                         v_sb[:, j * 1024 + dvb * 512:
                                             j * 1024 + (dvb + 1) * 512],
                                         start=(j == 0), stop=(j == i)),
                                waits=wts if (dvb == 0 and j == 0) else (),
                                sem=None if not (dvb == 1 and j == i) else "dflt")
            prev[("att", 0)] = mm
            st, tk_ps = sandwich_finish(ctx, aux, swb, ps_o[oslot][:], mm,
                                        x_in, x_out, row, i, prev)
            pso_cons[oslot] = tk_ps
        cons["qT"] = mm
        cons["v_sb"] = mm


L1FFN_W = ["L1_ffn_w1_w", "L1_ffn_w1_s", "L1_ffn_w2_w", "L1_ffn_w2_s",
           "L1_ffn_h1_w", "L1_ffn_h1_s", "L1_ffn_h2_w", "L1_ffn_h2_s",
           "L1_ffn_w3_w", "L1_ffn_w3_s"]


def l1ffn_prep(nc, ctx, aux, stack, W, scr, wch, ssb, ps_tr):
    if True:
        wsem = [ctx.new_sem(), ctx.new_sem()]
        stsem = [ctx.new_sem(), ctx.new_sem()]
        prev = {}
        for mat, ns, nc_ in (("L1_ffn_w1_w", 32, 8), ("L1_ffn_w2_w", 32, 8),
                             ("L1_ffn_h1_w", 32, 32), ("L1_ffn_h2_w", 32, 32),
                             ("L1_ffn_w3_w", 8, 32)):
            ni = nc_ * 128
            for s in range(ns):
                sl = (len(prev) + s) % 2 if False else s % 2
                w = [prev[("wch", sl)]] if ("wch", sl) in prev else []
                ld = ctx.dma("gpsimd", wch[sl][:, 0:ni],
                             W[mat][s * 128:(s + 1) * 128, :], waits=w,
                             sem=wsem[sl])
                ev_last = None
                for cg in range((nc_ + 3) // 4):
                    ntile = min(4, nc_ - cg * 4)
                    tslot = cg % 2
                    wts = [ld, aux.ld_tok]
                    if ("ptr", tslot) in prev:
                        wts.append(prev[("ptr", tslot)])
                    ttr = None
                    for cc in range(ntile):
                        c = cg * 4 + cc
                        ttr = ctx.op("tensor", lambda e, c=c, cc=cc, sl=sl, tslot=tslot:
                                     e.transpose(ps_tr[tslot][:, cc * 128:(cc + 1) * 128],
                                                 wch[sl][:, c * 128:(c + 1) * 128],
                                                 aux.idbf[:]),
                                     waits=wts if cc == 0 else (),
                                     sem=None if cc < ntile - 1 else "dflt")
                    wse = [prev[("ssb", sl)]] if (("ssb", sl) in prev and cg == 0) else []
                    ev_last = ctx.op("vector", lambda e, cg=cg, ntile=ntile, sl=sl,
                                     tslot=tslot:
                                     e.tensor_copy(ssb[sl][:, cg * 512:cg * 512 + ntile * 128],
                                                   ps_tr[tslot][:, 0:ntile * 128]),
                                     waits=[ttr] + wse)
                    prev[("ptr", tslot)] = ev_last
                prev[("wch", sl)] = ev_last
                st = ctx.dma("sync", scr[mat][s * 128:(s + 1) * 128, :],
                             ssb[sl][:, 0:ni], waits=[ev_last], sem=stsem[sl])
                prev[("ssb", sl)] = st
        ctx.barrier()


def build_l1ffn(nc, ctx, aux, stack, W, x_in, x_out, rows=range(R), scr=None):
    ps = [stack.enter_context(nc.psum_tensor(f"ps_f1_{i}", [128, 512], F32))
          for i in range(2)]
    ps_tr = [stack.enter_context(nc.psum_tensor(f"pstr_f1_{i}", [128, 512], BF16))
             for i in range(2)]
    pso_bf = [stack.enter_context(nc.psum_tensor(f"psob_f1_{i}", [128, 1024], BF16))
              for i in range(2)]
    a_sb = stack.enter_context(nc.sbuf_tensor("a_f1", [128, 32 * 1024], BF16))
    h0T = stack.enter_context(nc.sbuf_tensor("h0_f1", [128, 32 * 1024], BF16))
    if scr is None:
        scr = {}
        for mat, ns, nc_ in (("L1_ffn_w1_w", 32, 8), ("L1_ffn_w2_w", 32, 8),
                             ("L1_ffn_h1_w", 32, 32), ("L1_ffn_h2_w", 32, 32),
                             ("L1_ffn_w3_w", 8, 32)):
            scr[mat] = nc.dram_tensor("scr_" + mat, [ns * 128, nc_ * 128], BF16,
                                      kind="Internal").ap()
        l1ffn_prep(nc, ctx, aux, stack, W, scr,
                   [a_sb[:, 0:4096], a_sb[:, 4096:8192]],
                   [h0T[:, 0:4096], h0T[:, 4096:8192]], ps_tr)
    scol = stack.enter_context(nc.sbuf_tensor("scol_f1", [128, 4 * 32 + 8], F32))
    sld = ctx.new_sem()
    ps_cons = {}
    stoks = []
    srow = stack.enter_context(nc.sbuf_tensor("srow_f1", [32, 128], F32))
    for j, (sn, ncol) in enumerate((("L1_ffn_w1_s", 32), ("L1_ffn_w2_s", 32),
                                    ("L1_ffn_h1_s", 32), ("L1_ffn_h2_s", 32),
                                    ("L1_ffn_w3_s", 8))):
        w = [stoks[-1]] if stoks else []
        ld = ctx.dma("sync", srow[0:ncol, :],
                     W[sn].rearrange("(c m) -> c m", m=128), waits=w, sem=sld)
        tt = ctx.op("tensor", lambda e, ncol=ncol:
                    e.transpose(ps[0][:, 0:ncol], srow[0:ncol, :],
                                aux.idf32[0:ncol, 0:ncol]),
                    waits=[ld, aux.ld_tok] + ([stoks[-1]] if stoks else []))
        tk = ctx.op("vector", lambda e, j=j, ncol=ncol:
                    e.tensor_copy(scol[:, j * 32:j * 32 + ncol], ps[0][:, 0:ncol]),
                    waits=[tt])
        stoks.append(tk)
    scol_tok = stoks[-1]
    ps_cons[0] = scol_tok
    w1s = scol[:, 0:32]; w2s = scol[:, 32:64]
    h1s = scol[:, 64:96]; h2s = scol[:, 96:128]; w3s = scol[:, 128:136]

    lnb = mk_ln_bufs(nc, ctx, stack, "f1", ps_tr)
    swb = mk_sw_bufs(nc, ctx, stack, "f1")
    xnT = stack.enter_context(nc.sbuf_tensor("xnT_f1", [128, 8 * 1024], BF16))
    outT = xnT
    wsb_s = [stack.enter_context(nc.sbuf_tensor(f"wss_f1_{i}", [128, 1024], BF16))
             for i in range(2)]
    wsb_b = [stack.enter_context(nc.sbuf_tensor(f"wsb_f1_{i}", [128, 4096], BF16))
             for i in range(2)]
    wsem_s = [ctx.new_sem(), ctx.new_sem()]
    wsem_b = [ctx.new_sem(), ctx.new_sem()]
    cons = {}
    prev = {}

    def tlin_pass(src_mat, nc_, rhs_buf, evac_fn, war_tok, xn_tok, ns=32,
                  wsb=None, wsem=None):
        toks = []
        mm = None
        for s in range(ns):
            sl = s % 2
            w = [prev[(src_mat, "w", sl)]] if (src_mat, "w", sl) in prev else []
            if ("wuse", wsb[sl].name) in prev:
                w.append(prev[("wuse", wsb[sl].name)])
            ld = ctx.dma("sync", wsb[sl][:, 0:nc_ * 128],
                         scr[src_mat][s * 128:(s + 1) * 128, :], waits=w,
                         sem=wsem[sl])
            for ng in range(2):
                pslot = ng
                wts = [ld, xn_tok] + ([ps_cons[pslot]] if pslot in ps_cons else [])
                if war_tok is not None and s == 0 and ng == 0:
                    wts.append(war_tok)
                for c in range(nc_):
                    mm = ctx.op("tensor", lambda e, s=s, sl=sl, c=c, ng=ng,
                                o=ps[pslot], rhs_buf=rhs_buf:
                                e.matmul(o[:], wsb[sl][:, c * 128:(c + 1) * 128],
                                         rhs_buf[:, c * 1024 + ng * 512:
                                                 c * 1024 + ng * 512 + 512],
                                         start=(c == 0), stop=(c == nc_ - 1)),
                                waits=wts if c == 0 else (),
                                sem=None if c < nc_ - 1 else "dflt")
                ev = evac_fn(s, ng, pslot, mm)
                ps_cons[pslot] = ev
                toks.append(ev)
            prev[("wuse", wsb[sl].name)] = mm
        return toks, mm

    for row in rows:
        for hh in range(2):
            tiles = range(hh * 8, (hh + 1) * 8)
            ew = [cons["xnT"]] if "xnT" in cons else []
            xn_tok = ln_to_xnT(ctx, aux, lnb, x_in, row, tiles, xnT, 1024,
                               evac_waits=ew)
            def ev_t1(s, ng, pslot, mm):
                w = [mm, scol_tok]
                if ("a", s, ng) in prev:
                    w.append(prev[("a", s, ng)])
                if cons.get("a_war") is not None and s == 0 and ng == 0:
                    w.append(cons["a_war"])
                return ctx.op("scalar", lambda e, s=s, ng=ng, i=ps[pslot]:
                              e.activation(a_sb[:, s * 1024 + ng * 512:
                                              s * 1024 + ng * 512 + 512],
                                           i[:], AF.Silu, scale=w1s[:, s:s + 1]),
                              waits=w)
            t1toks, mm1 = tlin_pass("L1_ffn_w1_w", 8, xnT, ev_t1,
                                    cons.get("xn_war"), xn_tok, wsb=wsb_s,
                                    wsem=wsem_s)
            def ev_t2(s, ng, pslot, mm):
                w = [mm, scol_tok, t1toks[2 * s + ng]]
                if ("h0", s, ng) in prev:
                    w.append(prev[("h0", s, ng)])
                tk = ctx.op("vector", lambda e, s=s, ng=ng, i=ps[pslot]:
                            e.scalar_tensor_tensor(
                                h0T[:, s * 1024 + ng * 512:s * 1024 + ng * 512 + 512],
                                i[:], w2s[:, s:s + 1],
                                a_sb[:, s * 1024 + ng * 512:s * 1024 + ng * 512 + 512],
                                ALU.mult, ALU.mult), waits=w)
                prev[("h0", s, ng)] = tk
                return tk
            t2toks, mm2 = tlin_pass("L1_ffn_w2_w", 8, xnT, ev_t2, None, xn_tok,
                                    wsb=wsb_s, wsem=wsem_s)
            cons["xnT_dead"] = mm2
            def ev_h1(s, ng, pslot, mm):
                w = [mm, scol_tok, t2toks[-1]]
                return ctx.op("scalar", lambda e, s=s, ng=ng, i=ps[pslot]:
                              e.activation(a_sb[:, s * 1024 + ng * 512:
                                              s * 1024 + ng * 512 + 512],
                                           i[:], AF.Gelu, scale=h1s[:, s:s + 1]),
                              waits=w)
            h1toks, mm3 = tlin_pass("L1_ffn_h1_w", 32, h0T, ev_h1, t2toks[-1],
                                    t2toks[-1], wsb=wsb_b, wsem=wsem_b)
            def ev_h2(s, ng, pslot, mm):
                w = [mm, scol_tok, h1toks[-1]]
                return ctx.op("scalar", lambda e, s=s, ng=ng, i=ps[pslot]:
                              e.activation(h0T[:, s * 1024 + ng * 512:
                                              s * 1024 + ng * 512 + 512],
                                           i[:], AF.Gelu, scale=h2s[:, s:s + 1]),
                              waits=w)
            h2toks, mm4 = tlin_pass("L1_ffn_h2_w", 32, a_sb, ev_h2, mm3,
                                    h1toks[-1], wsb=wsb_b, wsem=wsem_b)
            cons["a_war"] = mm4
            def ev_w3(s, ng, pslot, mm):
                w = [mm, scol_tok, h2toks[-1], cons["xnT_dead"]]
                if "oT_war" in cons:
                    w.append(cons["oT_war"])
                if ("oT", s, ng) in prev:
                    w.append(prev[("oT", s, ng)])
                tk = ctx.op("scalar", lambda e, s=s, ng=ng, i=ps[pslot]:
                            e.activation(outT[:, s * 1024 + ng * 512:
                                            s * 1024 + ng * 512 + 512],
                                         i[:], AF.Identity, scale=w3s[:, s:s + 1]),
                            waits=w)
                prev[("oT", s, ng)] = tk
                return tk
            w3toks, mm5 = tlin_pass("L1_ffn_w3_w", 32, h0T, ev_w3, mm4,
                                    h2toks[-1], ns=8, wsb=wsb_b, wsem=wsem_b)
            cons["xn_war"] = mm5
            for tt_ in range(8):
                t = hh * 8 + tt_
                oslot = tt_ % 2
                wts = [w3toks[-1]]
                if ("pso", oslot) in prev:
                    wts.append(prev[("pso", oslot)])
                ttr = None
                for oc in range(8):
                    ttr = ctx.op("tensor", lambda e, oc=oc, tt_=tt_, oslot=oslot:
                                 e.transpose(pso_bf[oslot][:, oc * 128:(oc + 1) * 128],
                                             outT[:, oc * 1024 + tt_ * 128:
                                                 oc * 1024 + (tt_ + 1) * 128],
                                             aux.idbf[:]),
                                 waits=wts if oc == 0 else (),
                                 sem=None if oc < 7 else "dflt")
                st, tk_ps = sandwich_finish(ctx, aux, swb, pso_bf[oslot][:], ttr,
                                            x_in, x_out, row, t, prev)
                prev[("pso", oslot)] = tk_ps
                cons["oT_war"] = ttr
                cons["xnT"] = ttr


L0ATTN_W = ["L0_attn_Wq1", "L0_attn_Wq2", "L0_attn_Wk1", "L0_attn_Wk2",
            "L0_attn_Wv", "L0_attn_Wc", "L0_attn_lq1", "L0_attn_lk1",
            "L0_attn_lq2", "L0_attn_lk2"]

def build_l0attn(nc, ctx, aux, stack, W, x_in, x_out, rows=range(R)):
    import contextlib
    ps_big = [stack.enter_context(nc.psum_tensor(f"psb_a0_{i}", [128, 1024], F32))
              for i in range(2)]
    ps_tr = [stack.enter_context(nc.psum_tensor(f"pstr_a0_{i}", [128, 512], BF16))
             for i in range(2)]
    ps_y = [stack.enter_context(nc.psum_tensor(f"psy_a0_{i}", [128, 512], F32))
            for i in range(2)]
    lt = stack.enter_context(nc.sbuf_tensor("lt_a0", [64, 4 * 16], F32))
    lm = stack.enter_context(nc.sbuf_tensor("lm_a0", [64, 2 * 16], F32))
    lam_bc = stack.enter_context(nc.sbuf_tensor("lam_a0", [128, 32], F32))
    lld = ctx.new_sem()
    for j, n in enumerate(("L0_attn_lq1", "L0_attn_lk1", "L0_attn_lq2",
                           "L0_attn_lk2")):
        ctx.dma("sync", lt[:, j * 16:(j + 1) * 16],
                W[n].rearrange("h d -> d h"), sem=lld,
                allow_slow_non_contiguous=True)
    ltok = lld.tok()
    tm1 = ctx.op("vector", lambda e: e.tensor_mul(lm[:, 0:16], lt[:, 0:16],
                                                  lt[:, 16:32]), waits=[ltok])
    tm2 = ctx.op("vector", lambda e: e.tensor_mul(lm[:, 16:32], lt[:, 32:48],
                                                  lt[:, 48:64]), waits=[tm1])
    tl1 = ctx.op("tensor", lambda e: e.matmul(ps_big[0][:, 0:16],
                                              aux.onesf32[0:64, :],
                                              lm[:, 0:16], start=True, stop=True),
                 waits=[tm1, aux.ld_tok])
    tl2 = ctx.op("tensor", lambda e: e.matmul(ps_big[0][:, 16:32],
                                              aux.onesf32[0:64, :],
                                              lm[:, 16:32], start=True, stop=True),
                 waits=[tm2])
    te1 = ctx.op("scalar", lambda e: e.activation(lam_bc[:, 16:32],
                                                  ps_big[0][:, 0:16],
                                                  AF.Exp), waits=[tl1, tl2])
    te2 = ctx.op("scalar", lambda e: e.activation(lam_bc[:, 0:16],
                                                  ps_big[0][:, 16:32],
                                                  AF.Exp), waits=[te1])
    tv1 = ctx.op("vector", lambda e:
                 e.scalar_tensor_tensor(lam_bc[:, 0:16], lam_bc[:, 0:16], -1.0,
                                        lam_bc[:, 16:32], ALU.mult, ALU.add),
                 waits=[te2])
    tv2 = ctx.op("vector", lambda e:
                 e.tensor_scalar_add(lam_bc[:, 0:16], lam_bc[:, 0:16],
                                     LAMBDA_INIT), waits=[tv1])
    lam_tok = tv2
    ps_cons = {("b", 0): te2}

    xnT = stack.enter_context(nc.sbuf_tensor("xnT_a0", [128, 8 * 2048], BF16))
    ylnT = stack.enter_context(nc.sbuf_tensor("ylnT_a0", [128, 16 * 2048], BF16))
    cons = {}
    prev = {}
    wsem = [ctx.new_sem(), ctx.new_sem()]
    wcld = ctx.new_sem()

    lnb = mk_ln_bufs(nc, ctx, stack, "a0", ps_tr)
    swb = mk_sw_bufs(nc, ctx, stack, "a0")
    trd = lnb[6]
    qkT = [stack.enter_context(nc.sbuf_tensor(f"qkT_a0_{i}", [128, 2048], BF16))
           for i in range(4)]
    v_p = [stack.enter_context(nc.sbuf_tensor("vp_a0_0", [128, 16 * 256], BF16))]
    wqk = [stack.enter_context(nc.sbuf_tensor("wqk_a0_0",
                                              [128, 4 * 1024 + 2048], BF16))]
    e1b = [stack.enter_context(nc.sbuf_tensor(f"e1_a0_{i}", [128, 2048], BF16))
           for i in range(2)]
    e2b = [stack.enter_context(nc.sbuf_tensor(f"e2_a0_{i}", [128, 2048], BF16))
           for i in range(2)]
    attb = stack.enter_context(nc.sbuf_tensor("att_a0", [128, 2048], BF16))
    attc = [stack.enter_context(nc.sbuf_tensor(f"attc_a0_{i}", [128, 2048], BF16))
            for i in range(2)]
    attT = [stack.enter_context(nc.sbuf_tensor(f"attT_a0_{i}", [128, 2048], BF16))
            for i in range(2)]
    ssb1 = [stack.enter_context(nc.sbuf_tensor("ss1_a0", [128, 512], F32))] * 2
    ssb2 = [stack.enter_context(nc.sbuf_tensor("ss2_a0", [128, 512], F32))] * 2
    gn = stack.enter_context(nc.sbuf_tensor("gn_a0", [128, 128], F32))
    ylnbf = [stack.enter_context(nc.sbuf_tensor(f"ylnbf_a0_{i}", [128, 128], BF16))
             for i in range(2)]
    zp = stack.enter_context(nc.sbuf_tensor("zp_a0", [128, 32], F32))
    wc_sb = xnT

    for row in rows:
        ew = [cons["xnT"]] if "xnT" in cons else []
        xn_tok = ln_to_xnT(ctx, aux, lnb, x_in, row, range(NT), xnT, 2048,
                           evac_waits=ew)
        pair_mm_last = None
        for p in range(8):
            sl = 0
            wv = [cons[("w", sl)]] if ("w", sl) in cons else []
            wts_ld = []
            for j, n in enumerate(("L0_attn_Wq1", "L0_attn_Wq2", "L0_attn_Wk1",
                                   "L0_attn_Wk2")):
                wts_ld.append(ctx.dma(
                    "gpsimd",
                    wqk[sl][:, j * 1024:(j + 1) * 1024].rearrange(
                        "p (c m) -> p c m", c=8),
                    W[n][:, p * 128:(p + 1) * 128].rearrange(
                        "(c p) m -> p c m", p=128), waits=wv, sem=wsem[sl]))
            wts_ld.append(ctx.dma(
                "gpsimd",
                wqk[sl][:, 4096:4096 + 2048].rearrange("p (c m) -> p c m", c=8),
                W["L0_attn_Wv"][:, p * 256:(p + 1) * 256].rearrange(
                    "(c p) m -> p c m", p=128), waits=wv, sem=wsem[sl]))
            wtok = wts_ld[-1]
            pj_toks = []
            for j in range(4):
                for nb in range(4):
                    pslot = nb % 2
                    wts = [xn_tok, wtok] + ([ps_cons[("b", pslot)]]
                                            if ("b", pslot) in ps_cons else [])
                    if ("qkT", j) in cons:
                        wts.append(cons[("qkT", j)])
                    mm = None
                    for c in range(8):
                        mm = ctx.op("tensor", lambda e, j=j, nb=nb, c=c, sl=sl,
                                    o=ps_big[pslot]:
                                    e.matmul(o[:, 0:512],
                                             wqk[sl][:, j * 1024 + c * 128:
                                                     j * 1024 + (c + 1) * 128],
                                             xnT[:, c * 2048 + nb * 512:
                                                 c * 2048 + (nb + 1) * 512],
                                             start=(c == 0), stop=(c == 7)),
                                    waits=wts if c == 0 else (),
                                    sem=None if c < 7 else "dflt")
                    ev = ctx.op("scalar", lambda e, j=j, nb=nb, i=ps_big[pslot]:
                                e.activation(qkT[j][:, nb * 512:(nb + 1) * 512],
                                             i[:, 0:512], AF.Copy), waits=[mm])
                    ps_cons[("b", pslot)] = ev
                    pj_toks.append(ev)
            v_toks = []
            for t in range(NT):
                pslot = t % 2
                wts = [xn_tok, wtok] + ([ps_cons[("b", pslot)]]
                                        if ("b", pslot) in ps_cons else [])
                if ("vp", 0) in cons:
                    wts.append(cons[("vp", 0)])
                mm = None
                for c in range(8):
                    mm = ctx.op("tensor", lambda e, t=t, c=c, sl=sl,
                                o=ps_big[pslot]:
                                e.matmul(o[:, 512:768],
                                         xnT[:, c * 2048 + t * 128:
                                             c * 2048 + (t + 1) * 128],
                                         wqk[sl][:, 4096 + c * 256:
                                                 4096 + (c + 1) * 256],
                                         start=(c == 0), stop=(c == 7)),
                                waits=wts if c == 0 else (),
                                sem=None if c < 7 else "dflt")
                ev = ctx.op("vector", lambda e, t=t, sl=sl, i=ps_big[pslot]:
                            e.tensor_copy(v_p[0][:, t * 256:(t + 1) * 256],
                                          i[:, 512:768]), waits=[mm])
                ps_cons[("b", pslot)] = ev
                v_toks.append(ev)
            for h01 in range(2):
                h = 2 * p + h01
                hb = 64 * h01
                for i in range(NT):
                    q2 = i % 2
                    zb_ = q2 * 16
                    s2d = i % 2
                    L = (i + 1) * 128
                    nkb = (L + 511) // 512
                    etoks = []
                    for kb in range(nkb):
                        klen = min(512, L - kb * 512)
                        pslot = kb % 2
                        wts = [pj_toks[-1]] + ([ps_cons[("b", pslot)]]
                                               if ("b", pslot) in ps_cons else [])
                        ewait = ([prev[("eL0", q2)]]
                                 if (("eL0", q2) in prev and kb == 0) else [])
                        mm1 = ctx.op("tensor", lambda e, i=i, kb=kb, hb=hb,
                                     klen=klen, o=ps_big[pslot]:
                                     e.matmul(o[:, 0:klen],
                                              qkT[0][hb:hb + 64, i * 128:(i + 1) * 128],
                                              qkT[2][hb:hb + 64,
                                                     kb * 512:kb * 512 + klen],
                                              start=True, stop=True),
                                     waits=wts, sem=None)
                        mm2 = ctx.op("tensor", lambda e, i=i, kb=kb, hb=hb,
                                     klen=klen, o=ps_big[pslot]:
                                     e.matmul(o[:, 512:512 + klen],
                                              qkT[1][hb:hb + 64, i * 128:(i + 1) * 128],
                                              qkT[3][hb:hb + 64,
                                                     kb * 512:kb * 512 + klen],
                                              start=True, stop=True),
                                     waits=(), sem="dflt")
                        if kb == nkb - 1:
                            tA = ctx.op("scalar", lambda e, klen=klen, s2d=s2d,
                                        i_=ps_big[pslot]:
                                        e.activation(ssb1[s2d][:, 0:klen],
                                                     i_[:, 0:klen],
                                                     AF.Identity, scale=0.125),
                                        waits=[mm2, prev.get(("ssb", s2d), mm2)]
                                        + ewait)
                            tB = ctx.op("scalar", lambda e, klen=klen, s2d=s2d,
                                        i_=ps_big[pslot]:
                                        e.activation(ssb2[s2d][:, 0:klen],
                                                     i_[:, 512:512 + klen],
                                                     AF.Identity, scale=0.125),
                                        waits=[tA])
                            ps_cons[("b", pslot)] = tB
                            tC = ctx.op("vector", lambda e, klen=klen, s2d=s2d:
                                        e.tensor_add(ssb1[s2d][:, klen - 128:klen],
                                                     ssb1[s2d][:, klen - 128:klen],
                                                     aux.mask[:]),
                                        waits=[tA, aux.ld_tok])
                            tD = ctx.op("vector", lambda e, klen=klen, s2d=s2d:
                                        e.tensor_add(ssb2[s2d][:, klen - 128:klen],
                                                     ssb2[s2d][:, klen - 128:klen],
                                                     aux.mask[:]), waits=[tB])
                            tE = ctx.op("scalar", lambda e, kb=kb, klen=klen, q2=q2,
                                        s2d=s2d, zb_=zb_:
                                        e.activation(e1b[q2][:, kb * 512:
                                                             kb * 512 + klen],
                                                     ssb1[s2d][:, 0:klen], AF.Exp,
                                                     accum_out=zp[:, zb_ + kb:
                                                                  zb_ + kb + 1]),
                                        waits=[tC, tD])
                            tF = ctx.op("scalar", lambda e, kb=kb, klen=klen, q2=q2,
                                        s2d=s2d, zb_=zb_:
                                        e.activation(e2b[q2][:, kb * 512:
                                                             kb * 512 + klen],
                                                     ssb2[s2d][:, 0:klen], AF.Exp,
                                                     accum_out=zp[:, zb_ + 4 + kb:
                                                                  zb_ + 5 + kb]),
                                        waits=[tD])
                            prev[("ssb", s2d)] = tF
                            etoks += [tE, tF]
                        else:
                            tE = ctx.op("scalar", lambda e, kb=kb, q2=q2, zb_=zb_,
                                        i_=ps_big[pslot]:
                                        e.activation(e1b[q2][:, kb * 512:
                                                             (kb + 1) * 512],
                                                     i_[:, 0:512], AF.Exp,
                                                     scale=0.125,
                                                     accum_out=zp[:, zb_ + kb:
                                                                  zb_ + kb + 1]),
                                        waits=[mm2] + ewait)
                            tF = ctx.op("scalar", lambda e, kb=kb, q2=q2, zb_=zb_,
                                        i_=ps_big[pslot]:
                                        e.activation(e2b[q2][:, kb * 512:
                                                             (kb + 1) * 512],
                                                     i_[:, 512:1024], AF.Exp,
                                                     scale=0.125,
                                                     accum_out=zp[:, zb_ + 4 + kb:
                                                                  zb_ + 5 + kb]),
                                        waits=[mm2])
                            ps_cons[("b", pslot)] = tF
                            etoks += [tE, tF]
                    tz = ctx.op("vector", lambda e, nkb=nkb, zb_=zb_:
                                e.reduce_sum(zp[:, zb_ + 8:zb_ + 9],
                                             zp[:, zb_:zb_ + nkb], axis=AX.X),
                                waits=etoks)
                    tz2 = ctx.op("vector", lambda e, nkb=nkb, zb_=zb_:
                                 e.reduce_sum(zp[:, zb_ + 9:zb_ + 10],
                                              zp[:, zb_ + 4:zb_ + 4 + nkb],
                                              axis=AX.X), waits=etoks)
                    tr1 = ctx.op("vector", lambda e, zb_=zb_:
                                 e.reciprocal(zp[:, zb_ + 10:zb_ + 11],
                                              zp[:, zb_ + 8:zb_ + 9]), waits=[tz2])
                    tr2 = ctx.op("vector", lambda e, zb_=zb_:
                                 e.reciprocal(zp[:, zb_ + 11:zb_ + 12],
                                              zp[:, zb_ + 9:zb_ + 10]), waits=[tz2])
                    tns = ctx.op("vector", lambda e, h=h, zb_=zb_:
                                 e.tensor_scalar(zp[:, zb_ + 12:zb_ + 13],
                                                 zp[:, zb_ + 11:zb_ + 12],
                                                 lam_bc[:, h:h + 1], -1.0,
                                                 ALU.mult, ALU.mult),
                                 waits=[tr2, lam_tok])
                    ta = ctx.op("vector", lambda e, L=L, q2=q2, zb_=zb_:
                                e.tensor_scalar(attb[:, 0:L], e1b[q2][:, 0:L],
                                                zp[:, zb_ + 10:zb_ + 11], None,
                                                ALU.mult),
                                waits=[tns])
                    tc = ctx.op("vector", lambda e, L=L, q2=q2, zb_=zb_:
                                e.scalar_tensor_tensor(attc[q2][:, 0:L],
                                                       e2b[q2][:, 0:L],
                                                       zp[:, zb_ + 12:zb_ + 13],
                                                       attb[:, 0:L],
                                                       ALU.mult, ALU.add),
                                waits=[ta] + ([prev[("attcL0", q2)]]
                                              if ("attcL0", q2) in prev else []))
                    prev[("eL0", q2)] = tc
                    ev_toks = []
                    for jg in range((i + 4) // 4):
                        ntile = min(4, i + 1 - jg * 4)
                        tslot = jg % 2
                        wts = [tc] + ([prev[("atrL0", tslot)]]
                                      if ("atrL0", tslot) in prev else [])
                        if ("tr", tslot) in trd:
                            wts.append(trd[("tr", tslot)])
                        ttr = None
                        for jj in range(ntile):
                            j = jg * 4 + jj
                            ttr = ctx.op("tensor", lambda e, j=j, jj=jj, q2=q2,
                                         tslot=tslot:
                                         e.transpose(ps_tr[tslot][:, jj * 128:
                                                                  (jj + 1) * 128],
                                                     attc[q2][:, j * 128:
                                                              (j + 1) * 128],
                                                     aux.idbf[:]),
                                         waits=wts if jj == 0 else (),
                                         sem=None if jj < ntile - 1 else "dflt")
                        wat = ([prev[("avmm", s2d)]]
                               if (("avmm", s2d) in prev and jg == 0) else [])
                        tke = ctx.op("vector", lambda e, jg=jg, ntile=ntile, s2d=s2d,
                                     tslot=tslot:
                                     e.tensor_copy(attT[s2d][:, jg * 512:
                                                            jg * 512 + ntile * 128],
                                                   ps_tr[tslot][:, 0:ntile * 128]),
                                     waits=[ttr] + wat)
                        trd[("tr", tslot)] = tke
                        prev[("atrL0", tslot)] = tke
                        ev_toks.append(tke)
                    prev[("attcL0", q2)] = ttr
                    yslot = s2d
                    wts = ev_toks + [v_toks[-1]] + ([ps_cons[("y", yslot)]]
                                                    if ("y", yslot) in ps_cons else [])
                    mm = None
                    for j in range(i + 1):
                        mm = ctx.op("tensor", lambda e, j=j, i=i, h01=h01, sl=sl,
                                    s2d=s2d, yslot=yslot:
                                    e.matmul(ps_y[yslot][:, 0:128],
                                             attT[s2d][:, j * 128:(j + 1) * 128],
                                             v_p[0][:, j * 256 + h01 * 128:
                                                     j * 256 + h01 * 128 + 128],
                                             start=(j == 0), stop=(j == i)),
                                    waits=wts if j == 0 else (),
                                    sem=None if j < i else "dflt")
                    prev[("avmm", s2d)] = mm
                    wgn = [prev[("gn", yslot)]] if ("gn", yslot) in prev else []
                    g0 = ctx.op("vector", lambda e, yslot=yslot, zb_=zb_:
                                e.reduce_sum(zp[:, zb_ + 13:zb_ + 14],
                                             ps_y[yslot][:, 0:128], axis=AX.X),
                                waits=[mm] + wgn)
                    g1 = ctx.op("vector", lambda e, zb_=zb_:
                                e.tensor_scalar_mul(zp[:, zb_ + 14:zb_ + 15],
                                                    zp[:, zb_ + 13:zb_ + 14],
                                                    -1.0 / 128), waits=[g0])
                    g2 = ctx.op("scalar", lambda e, yslot=yslot, zb_=zb_:
                                e.activation(gn[:], ps_y[yslot][:, 0:128], AF.Square,
                                             bias=zp[:, zb_ + 14:zb_ + 15],
                                             accum_out=zp[:, zb_ + 13:zb_ + 14]),
                                waits=[g1])
                    g3 = ctx.op("scalar", lambda e, zb_=zb_:
                                e.activation(zp[:, zb_ + 15:zb_ + 16],
                                             zp[:, zb_ + 13:zb_ + 14], AF.Ln,
                                             scale=1.0 / 128, bias=EPS), waits=[g2])
                    g4 = ctx.op("scalar", lambda e, zb_=zb_:
                                e.activation(zp[:, zb_ + 13:zb_ + 14],
                                             zp[:, zb_ + 15:zb_ + 16], AF.Exp,
                                             scale=-0.5, bias=LNL), waits=[g3])
                    g6 = ctx.op("vector", lambda e, zb_=zb_:
                                e.tensor_mul(zp[:, zb_ + 14:zb_ + 15],
                                             zp[:, zb_ + 14:zb_ + 15],
                                             zp[:, zb_ + 13:zb_ + 14]), waits=[g4])
                    wyb = [prev[("ylnbf", yslot)]] if ("ylnbf", yslot) in prev else []
                    g7 = ctx.op("scalar", lambda e, yslot=yslot, zb_=zb_:
                                e.activation(ylnbf[yslot][:], ps_y[yslot][:, 0:128],
                                             AF.Identity,
                                             scale=zp[:, zb_ + 13:zb_ + 14],
                                             bias=zp[:, zb_ + 14:zb_ + 15]),
                                waits=[g6] + wyb)
                    prev[("gn", yslot)] = g7
                    ps_cons[("y", yslot)] = g7
                    tslot2 = s2d
                    wts = [g7] + ([prev[("ytr", tslot2)]]
                                  if ("ytr", tslot2) in prev else [])
                    if ("tr", tslot2) in trd:
                        wts.append(trd[("tr", tslot2)])
                    ty = ctx.op("tensor", lambda e, yslot=yslot, tslot2=tslot2:
                                e.transpose(ps_tr[tslot2][:, 0:128], ylnbf[yslot][:],
                                            aux.idbf[:]), waits=wts)
                    tye = ctx.op("vector", lambda e, h=h, i=i, tslot2=tslot2:
                                 e.tensor_copy(ylnT[:, h * 2048 + i * 128:
                                                   h * 2048 + (i + 1) * 128],
                                               ps_tr[tslot2][:, 0:128]), waits=[ty])
                    trd[("tr", tslot2)] = tye
                    prev[("ytr", tslot2)] = tye
                    prev[("yln",)] = tye
                    pair_mm_last = mm
            cons[("w", sl)] = pair_mm_last
            cons[("vp", 0)] = pair_mm_last
            for j in range(4):
                cons[("qkT", j)] = pair_mm_last
        wc_tok = ctx.dma("gpsimd", wc_sb[:].rearrange("p (c m) -> p c m", c=16),
                         W["L0_attn_Wc"].rearrange("(c p) m -> p c m", p=128),
                         waits=[pair_mm_last, prev[("yln",)]], sem=wcld)
        out_mm_last = None
        for i in range(NT):
            oslot = i % 2
            wts = [prev[("yln",)], wc_tok] + ([ps_cons[("o", oslot)]]
                                              if ("o", oslot) in ps_cons else [])
            mm = None
            for db in range(2):
                for hch in range(16):
                    mm = ctx.op("tensor", lambda e, i=i, db=db, hch=hch,
                                o=ps_big[oslot]:
                                e.matmul(o[:, db * 512:(db + 1) * 512],
                                         ylnT[:, hch * 2048 + i * 128:
                                             hch * 2048 + (i + 1) * 128],
                                         wc_sb[:, hch * 1024 + db * 512:
                                             hch * 1024 + (db + 1) * 512],
                                         start=(hch == 0), stop=(hch == 15)),
                                waits=wts if (db == 0 and hch == 0) else (),
                                sem=None if not (db == 1 and hch == 15) else "dflt")
            out_mm_last = mm
            st, tk_ps = sandwich_finish(ctx, aux, swb, ps_big[oslot][:], mm,
                                        x_in, x_out, row, i, prev)
            ps_cons[("o", oslot)] = tk_ps
            ps_cons[("b", oslot)] = tk_ps
        cons["wc"] = out_mm_last
        cons["xnT"] = out_mm_last
        ctx.barrier()
